# version 01
# baseline (speedup 1.0000x reference)
import numpy as np
import concourse.bass as bass
import concourse.mybir as mybir
from concourse.bass_utils import run_bass_kernel_spmd
from contextlib import ExitStack

F32 = mybir.dt.float32
BF16 = mybir.dt.bfloat16
AF = mybir.ActivationFunctionType
ALU = mybir.AluOpType
AX = mybir.AxisListType

T_FULL = 8192
D = 1024
NFT = 8
CH = 512
NCH = T_FULL // CH
HEADS = 8
DH = 64
WIN = 512
HIDP = 2816
NW1T = 44
NW2K = 22
GD = 2048
NGT = 16
GCH = 256
NGCH = T_FULL // GCH
GIC = 128
NEG = -30000.0

_UPTO = 10
_DBG_CHUNK = -1
_GLA_LIMIT = None
_GLA_PH = 3
_GLA_OP = 8
_LN_LV = 9
_GLA_NOUNION = False
_T_RELAX = True
_F1A_OOP = False


def _host_consts():
    n = T_FULL
    half = np.arange(0, DH, 2, dtype=np.float64)
    inv_freq = 1.0 / 10000.0 ** (half / DH)
    t = np.arange(n, dtype=np.float64)
    freqs32 = t[:, None] * inv_freq[None, :]
    base32 = (half + 0.4 * DH) / (1.4 * DH)
    power = (t - n // 2) / float(WIN // 2)
    scale32 = base32[None, :] ** power[:, None]
    freqs = np.concatenate([freqs32, freqs32], 1)
    scale = np.concatenate([scale32, scale32], 1)
    cos = np.cos(freqs); sin = np.sin(freqs)
    perm = np.empty(64, np.int64)
    perm[0::2] = np.arange(32); perm[1::2] = np.arange(32, 64)
    cos = cos[:, perm]; sin = sin[:, perm]; scale = scale[:, perm]
    sgn = np.empty(64); sgn[0::2] = -1.0; sgn[1::2] = 1.0
    cq = (cos * scale).T
    sq = (sin * scale).T * sgn[:, None]
    ck = (cos / scale).T
    sk = (sin / scale).T * sgn[:, None]
    dup = lambda a: np.concatenate([a, a], 0)
    rtab = np.ascontiguousarray(
        np.stack([dup(cq), dup(sq), dup(ck), dup(sk)], 0).astype(np.float32))
    i = np.arange(128)[:, None]; j = np.arange(128)[None, :]
    mprev = np.where(j >= i, 0.0, NEG).astype(np.float32)
    mdiag = np.where(j <= i, 0.0, NEG).astype(np.float32)
    jj = np.arange(GIC)[:, None]; ii = np.arange(GIC)[None, :]
    gtri = np.where(jj <= ii, 1.0, 0.0).astype(np.float32)
    ident = np.eye(128, dtype=np.float32)
    hs = np.zeros((128, 2), np.float32)
    hs[0:64, 0] = 1.0
    hs[64:128, 1] = 1.0
    return rtab, mprev, mdiag, gtri, ident, perm, hs


def _prep(inputs):
    g = {k: np.asarray(v) for k, v in inputs.items()}
    ac = np.ascontiguousarray
    w = {}
    w["aWKT"] = ac(g["L0_attn_a_W_K"].T)
    w["aW1T"] = ac(g["L0_attn_a_W1"].T)
    w["aW2T"] = ac(g["L0_attn_a_W2"].T)
    rtab, mprev, mdiag, gtri, ident, perm, hs = _host_consts()
    w["rtab"], w["mprev"], w["mdiag"], w["gtri"], w["ident"] = \
        rtab, mprev, mdiag, gtri, ident
    w["halfsel"] = hs
    for l in (0, 1):
        wq = g["L0_ffn_wqkv"][l]
        qc = wq[:, 0:512].reshape(D, HEADS, DH)[:, :, perm].reshape(D, 512)
        kc = wq[:, 512:1024].reshape(D, HEADS, DH)[:, :, perm].reshape(D, 512)
        vc = wq[:, 1024:1536]
        w[f"fWQKV{l}"] = ac(np.concatenate([qc, kc, vc], 1))
        qs = g["L0_ffn_q_scale"][l][perm]; ks = g["L0_ffn_k_scale"][l][perm]
        z64 = np.zeros(64, np.float32)
        w[f"fQSa{l}"] = ac(np.concatenate([qs, z64])[None, :].astype(np.float32))
        w[f"fQSb{l}"] = ac(np.concatenate([z64, qs])[None, :].astype(np.float32))
        w[f"fKSa{l}"] = ac(np.concatenate([ks, z64])[None, :].astype(np.float32))
        w[f"fKSb{l}"] = ac(np.concatenate([z64, ks])[None, :].astype(np.float32))
        w[f"fWO{l}"] = ac(g["L0_ffn_wo"][l])
        w[f"fLG{l}"] = ac(g["L0_ffn_ln_g"][l].reshape(NFT, 128).T)
        w[f"fLB{l}"] = ac(g["L0_ffn_ln_b"][l].reshape(NFT, 128).T)
        w1 = g["L0_ffn_w1"][l]; b1 = g["L0_ffn_b1"][l]
        w1p = np.zeros((D, 2 * HIDP), np.float32)
        b1p = np.zeros(2 * HIDP, np.float32)
        for q2 in range(NW1T // 2):
            a0, a1 = 128 * q2, min(128 * q2 + 128, 2730)
            wd = a1 - a0
            if wd <= 0:
                continue
            w1p[:, 256 * q2:256 * q2 + wd] = w1[:, a0:a1]
            w1p[:, 256 * q2 + 128:256 * q2 + 128 + wd] = w1[:, 2730 + a0:2730 + a1]
            b1p[256 * q2:256 * q2 + wd] = b1[a0:a1]
            b1p[256 * q2 + 128:256 * q2 + 128 + wd] = b1[2730 + a0:2730 + a1]
        w[f"fW1{l}"] = ac(w1p)
        w[f"fB1{l}"] = ac(b1p.reshape(NW1T, 128).T)
        w2p = np.zeros((HIDP, D), np.float32)
        w2p[0:2730] = g["L0_ffn_w2"][l]
        w[f"fW2{l}"] = ac(w2p)
        w[f"fB2{l}"] = ac(g["L0_ffn_b2"][l].reshape(NFT, 128).T)
    w["gUPT"] = ac(g["L0_attn_b_up_w"].T)
    w["gGATET"] = ac(g["L0_attn_b_gate_w"].T)
    w["gDOWNT"] = ac(g["L0_attn_b_down_w"].T)
    w["gQKT"] = ac(np.concatenate([g["L0_attn_b_q_w"].T, g["L0_attn_b_k_w"].T], 1))
    w["gGTT"] = ac(np.concatenate([g["L0_attn_b_g_w"].T, g["L0_attn_b_tau_w"].T], 1))
    w["gIG1T"] = ac(g["L0_attn_b_ig_w1"].T)
    w["gIG2T"] = ac(g["L0_attn_b_ig_w2"].T)
    cw = g["L0_attn_b_conv_w"]
    cd = np.zeros((4, NGT, 128, 128), np.float32)
    for jt in range(4):
        for it in range(NGT):
            np.fill_diagonal(cd[jt, it], cw[128 * it:128 * it + 128, jt])
    w["gCONVD"] = ac(cd.transpose(2, 0, 1, 3))
    w["gCONVB"] = ac(g["L0_attn_b_conv_b"].reshape(NGT, 128).T)
    w["gGB"] = ac(g["L0_attn_b_g_b"].reshape(64, 1))
    w["gTAUB"] = ac(g["L0_attn_b_tau_b"].reshape(64, 1))
    w["gIGB2"] = ac(g["L0_attn_b_ig_b2"].reshape(NGT, 128).T)
    w["gRW"] = ac(g["L0_attn_b_residual_weight"].reshape(NGT, 128).T)
    w["gNW"] = ac(g["L0_attn_b_norm_weight"].reshape(NGT, 128).T)
    return w



class Buf:
    __slots__ = ("name", "t", "prod", "cons")

    def __init__(self, name, t):
        self.name, self.t = name, t
        self.prod = {}
        self.cons = {}


ENGFN = {"t": "tensor", "v": "vector", "s": "scalar", "g": "gpsimd", "y": "sync"}
OWN = {"t": "t", "v": "v", "s": "s", "g": "g", "y": None}
DMAS = {"g": "gd", "y": "yd", "s": "sd"}


class Pr:
    strict = True

    def __init__(self, nc, block, stack):
        self.nc, self.block, self.stack = nc, block, stack
        self.sem = {}
        self.cnt = {}
        for s in ("t", "v", "s", "g", "gd", "yd", "sd"):
            self.sem[s] = stack.enter_context(nc.semaphore(f"S{s}"))
            self.cnt[s] = 0
        self.seen = {e: dict.fromkeys(self.sem, 0) for e in ENGFN}

    skip = False

    def sec(self, e, f):
        if self.skip:
            return
        getattr(self.block, ENGFN[e])(lambda raw: f(Eng(self, e, raw)))

    def buf(self, handle, name=None):
        return Buf(name or getattr(handle, "name", "b"), handle)

    @staticmethod
    def alias(primary, handle_or_ap, name="alias"):
        b = Buf(name, handle_or_ap)
        b.prod = primary.prod
        b.cons = primary.cons
        return b

    def barrier(self):
        snap = dict(self.cnt)
        for e in ENGFN:
            def f(eng, snap=snap, e=e):
                for sn, c in snap.items():
                    if c and sn != OWN[e] and self.seen[e][sn] < c:
                        eng.raw.wait_ge(self.sem[sn], c)
                        self.seen[e][sn] = c
            self.sec(e, f)


class Eng:
    __slots__ = ("pr", "e", "raw")

    def __init__(self, pr, e, raw):
        self.pr, self.e, self.raw = pr, e, raw

    def _wait(self, need):
        for sn, c in need.items():
            if sn == OWN[self.e] and (not self.pr.strict or (self.e == "t" and _T_RELAX)):
                continue
            if sn in ("gd", "yd", "sd"):
                c = self.pr.cnt[sn]
            if self.pr.seen[self.e][sn] < c:
                self.raw.wait_ge(self.pr.sem[sn], c)
                self.pr.seen[self.e][sn] = c

    def do(self, build, reads=(), writes=(), dma=False, rw=()):
        need = {}
        for b in tuple(reads) + tuple(rw):
            for s, c in b.prod.items():
                need[s] = max(need.get(s, 0), c)
        for b in tuple(writes) + tuple(rw):
            for s, c in b.cons.items():
                need[s] = max(need.get(s, 0), c)
            for s, c in b.prod.items():
                need[s] = max(need.get(s, 0), c)
        if dma:
            sn = DMAS[self.e]
            need[sn] = max(need.get(sn, 0), self.pr.cnt[sn])
        self._wait(need)
        ins = build(self.raw)
        if dma:
            sn, inc = DMAS[self.e], 16
        else:
            sn, inc = OWN[self.e], 1
        self.pr.cnt[sn] += inc
        ins.then_inc(self.pr.sem[sn], inc)
        if not dma and (not self.pr.strict or (self.e == "t" and _T_RELAX)):
            self.pr.seen[self.e][sn] = self.pr.cnt[sn]
        cend = self.pr.cnt[sn]
        for b in tuple(writes) + tuple(rw):
            b.prod.clear()
            b.prod[sn] = cend
        for b in tuple(reads) + tuple(rw):
            b.cons[sn] = cend
        return ins


def load_w(p, dst, src_ap):
    def f(eng):
        eng.do(lambda g: g.dma_start(out=dst.t[:], in_=src_ap), writes=[dst], dma=True)
    p.sec("g", f)



def emit_ln(p, C, xap, xbufs, sq, rows, statx, statq, mb, rb, N, sq_bf16=False,
            outap=None, outbufs=None, xmap=None, xmbufs=None):
    _lv = _LN_LV
    """Emit LN for one chunk of N tokens over NFT feature tiles.

    xap(i): f32 AP [128, N] of input tile i; xbufs: dep Bufs covering them.
    sq:     scratch Buf [128, NFT, N] (f32 or bf16, matching sq_bf16).
    rows:   scratch Buf f32 [1, 3N].
    statx/statq: psum Bufs (row 0, N cols used); mb/rb: psum Bufs [128, N].
    Writes xm (x - mean, f32) via xmap(i) and (x-mean)*rstd via outap(i).
    """
    def f_sq(eng):
        for i in range(NFT):
            eng.do(lambda s, i=i: s.activation(sq.t[:, i, :], xap(i), AF.Square),
                   reads=xbufs, writes=[sq])
    p.sec("s", f_sq)

    onesq = C["onescolb"] if sq_bf16 else C["onescol"]
    if _lv < 2:
        return

    def f_stat(eng):
        for i in range(NFT):
            eng.do(lambda t, i=i: t.matmul(statx.t[0:1, 0:N], C["onescol"].t[:, 0:1],
                                           xap(i), start=(i == 0), stop=(i == NFT - 1)),
                   reads=list(xbufs) + [C["onescol"]], writes=[statx])
        for i in range(NFT):
            eng.do(lambda t, i=i: t.matmul(statq.t[0:1, 0:N], onesq.t[:, 0:1],
                                           sq.t[:, i, :], start=(i == 0), stop=(i == NFT - 1)),
                   reads=[sq, onesq], writes=[statq])
    p.sec("t", f_stat)

    if _lv < 3:
        return

    def f_rows1(eng):
        eng.do(lambda v: v.tensor_scalar_mul(rows.t[0:1, 0:N], statx.t[0:1, 0:N],
                                             1.0 / D), reads=[statx], writes=[rows])
        eng.do(lambda v: v.tensor_scalar(rows.t[0:1, N:2 * N], statq.t[0:1, 0:N],
                                         1.0 / D, 1e-5, ALU.mult, ALU.add),
               reads=[statq], rw=[rows])
    p.sec("v", f_rows1)

    def f_rows2(eng):
        eng.do(lambda s: s.activation(statx.t[0:1, 0:N], rows.t[0:1, 0:N],
                                      AF.Square), reads=[rows], writes=[statx])
    p.sec("s", f_rows2)

    def f_rows3(eng):
        eng.do(lambda v: v.tensor_sub(rows.t[0:1, N:2 * N], rows.t[0:1, N:2 * N],
                                      statx.t[0:1, 0:N]), reads=[statx], rw=[rows])
    p.sec("v", f_rows3)

    def f_rows4(eng):
        eng.do(lambda s: s.activation(statx.t[0:1, 0:N], rows.t[0:1, N:2 * N],
                                      AF.Ln), reads=[rows], writes=[statx])
        eng.do(lambda s: s.activation(rows.t[0:1, N:2 * N], statx.t[0:1, 0:N],
                                      AF.Exp, scale=-0.5), reads=[statx], rw=[rows])
    p.sec("s", f_rows4)

    if _lv < 4:
        return

    def f_bcast(eng):
        eng.do(lambda t: t.matmul(mb.t[:], C["onesrow"].t[0:1, :], rows.t[0:1, 0:N],
                                  start=True, stop=True),
               reads=[rows, C["onesrow"]], writes=[mb])
        eng.do(lambda t: t.matmul(rb.t[:], C["onesrow"].t[0:1, :], rows.t[0:1, N:2 * N],
                                  start=True, stop=True),
               reads=[rows, C["onesrow"]], writes=[rb])
    p.sec("t", f_bcast)

    if outap is not None:
        def f_apply(eng):
            for i in range(NFT):
                eng.do(lambda v, i=i: v.tensor_sub(xmap(i), xap(i), mb.t[:]),
                       reads=list(xbufs) + [mb], writes=list(xmbufs))
            for i in range(NFT):
                eng.do(lambda v, i=i: v.tensor_mul(outap(i), xmap(i), rb.t[:]),
                       reads=list(xmbufs) + [rb], writes=list(outbufs))
        p.sec("v", f_apply)



def st_attn_a(p, nc, C, src3, dst3, aWKT, aW1T, aW2T):
    es = ExitStack()
    sb = lambda nm, shp, dt: p.buf(es.enter_context(nc.sbuf_tensor(nm, shp, dt)), nm)
    ps = lambda nm, shp: p.buf(es.enter_context(nc.psum_tensor(nm, shp, F32)), nm)

    wkt = sb("a_wkt", [128, NFT, D], BF16)
    w1t = sb("a_w1t", [128, NFT, 64], BF16)
    w2t = sb("a_w2t", [64, D], BF16)
    load_w(p, wkt, aWKT.rearrange("(a p) m -> p a m", p=128))
    load_w(p, w1t, aW1T.rearrange("(a p) m -> p a m", p=128))
    load_w(p, w2t, aW2T)

    x = [sb(f"a_x{i}", [128, NFT, CH], F32) for i in range(2)]
    sq = sb("a_sq", [128, NFT, CH], F32)
    xm = sb("a_xm", [128, NFT, CH], F32)
    xn = sb("a_xn", [128, NFT, CH], BF16)
    kb = sb("a_kb", [128, NFT, CH], BF16)
    gg = sb("a_gg", [64, CH], BF16)
    rows = sb("a_rows", [1, 2 * CH], F32)
    xo = [sb(f"a_xo{i}", [128, NFT, CH], F32) for i in range(2)]

    statx = ps("a_statx", [1, CH])
    statq = ps("a_statq", [1, CH])
    mb = ps("a_mb", [128, CH])
    rb = ps("a_rb", [128, CH])
    kp = [ps(f"a_kp{i}", [128, CH]) for i in range(2)]
    hp = ps("a_hp", [64, CH])
    rp = [ps("a_rp0", [128, CH])] * 2

    for c in range(NCH):
        c0 = c * CH
        xc, xoc = x[c % 2], xo[c % 2]

        def f_load(eng, c0=c0, xc=xc):
            eng.do(lambda y: y.dma_start(out=xc.t[:], in_=src3[:, :, c0:c0 + CH]),
                   writes=[xc], dma=True)
        p.sec("y", f_load)

        emit_ln(p, C, lambda i, xc=xc: xc.t[:, i, :], [xc], sq, rows, statx, statq,
                mb, rb, CH, sq_bf16=False,
                outap=lambda i: xn.t[:, i, :], outbufs=[xn],
                xmap=lambda i: xm.t[:, i, :], xmbufs=[xm])

        for m in range(NFT):
            kpp = kp[m % 2]

            def f_mm(eng, m=m, kpp=kpp):
                for k in range(NFT):
                    eng.do(lambda t, k=k, m=m: t.matmul(
                        kpp.t[:], wkt.t[:, k, 128 * m:128 * m + 128], xn.t[:, k, :],
                        start=(k == 0), stop=(k == NFT - 1)),
                        reads=[wkt, xn], writes=[kpp])
            p.sec("t", f_mm)

            def f_ev(eng, m=m, kpp=kpp):
                eng.do(lambda s: s.activation(kb.t[:, m, :], kpp.t[:], AF.Copy),
                       reads=[kpp], writes=[kb])
            p.sec("s", f_ev)

        def f_h(eng):
            for k in range(NFT):
                eng.do(lambda t, k=k: t.matmul(hp.t[:], w1t.t[:, k, :], kb.t[:, k, :],
                                               start=(k == 0), stop=(k == NFT - 1)),
                       reads=[w1t, kb], writes=[hp])
        p.sec("t", f_h)

        def f_g(eng):
            eng.do(lambda s: s.activation(gg.t[:], hp.t[:], AF.Gelu),
                   reads=[hp], writes=[gg])
        p.sec("s", f_g)

        for m in range(NFT):
            rpp = rp[m % 2]

            def f_mm2(eng, m=m, rpp=rpp):
                eng.do(lambda t: t.matmul(rpp.t[:], w2t.t[:, 128 * m:128 * m + 128],
                                          gg.t[:], start=True, stop=True),
                       reads=[w2t, gg], writes=[rpp])
            p.sec("t", f_mm2)

            def f_res(eng, m=m, rpp=rpp, xc=xc, xoc=xoc):
                eng.do(lambda v: v.scalar_tensor_tensor(
                    xoc.t[:, m, :], rpp.t[:], 1.0, xc.t[:, m, :], ALU.bypass, ALU.add),
                    reads=[rpp, xc], writes=[xoc])
            p.sec("v", f_res)

        def f_st(eng, c0=c0, xoc=xoc):
            eng.do(lambda y: y.dma_start(out=dst3[:, :, c0:c0 + CH], in_=xoc.t[:]),
                   reads=[xoc], dma=True)
        p.sec("y", f_st)

    es.close()


def st_ffn_attn(p, nc, C, l, first_ln, src3, dst3, fwl, rtab):
    es = ExitStack()
    sb = lambda nm, shp, dt: p.buf(es.enter_context(nc.sbuf_tensor(nm, shp, dt)), nm)
    ps = lambda nm, shp, dt=F32: p.buf(es.enter_context(nc.psum_tensor(nm, shp, dt)), nm)
    sfx = f"_{l}_{1 if first_ln else 0}_{p.cnt['t']}"

    wq = sb("f_wq" + sfx, [128, NFT, 1536], BF16)
    wo = sb("f_wo" + sfx, [128, 4, D], BF16)
    qsa = sb("f_qsa" + sfx, [1, 128], F32); qsb = sb("f_qsb" + sfx, [1, 128], F32)
    ksa = sb("f_ksa" + sfx, [1, 128], F32); ksb = sb("f_ksb" + sfx, [1, 128], F32)
    load_w(p, wq, fwl["WQKV"].rearrange("(a p) m -> p a m", p=128))
    load_w(p, wo, fwl["WO"].rearrange("(a p) m -> p a m", p=128))
    load_w(p, qsa, fwl["QSa"]); load_w(p, qsb, fwl["QSb"])
    load_w(p, ksa, fwl["KSa"]); load_w(p, ksb, fwl["KSb"])

    yc = [sb(f"f_yc{i}" + sfx, [128, NFT, CH], F32) for i in range(2)]
    xn = sb("f_xn" + sfx, [128, NFT, CH + 1], BF16)
    if first_ln:
        sq = sb("f_sq" + sfx, [128, NFT, CH], BF16)
        yn = sb("f_yn" + sfx, [128, NFT, CH], F32)
        lrow = sb("f_lrow" + sfx, [1, 2 * CH], F32)
    qf = sb("f_qf" + sfx, [128, 4, CH], F32)
    kraw = sb("f_kraw" + sfx, [128, 4, CH], F32)
    vb = sb("f_vb" + sfx, [128, 4, CH], BF16)
    sqq = sb("f_sqq" + sfx, [128, CH], BF16)
    qb = sb("f_qb" + sfx, [128, 4, CH], BF16)
    kf = [sb(f"f_kf{i}" + sfx, [128, 4, CH], BF16) for i in range(2)]
    vt = [sb(f"f_vt{i}" + sfx, [128, 4, 4, 128], BF16) for i in range(2)]
    rt2 = [sb(f"f_rt{i}" + sfx, [128, 4, CH], F32) for i in range(2)]
    rows = sb("f_rows" + sfx, [1, 2 * CH], F32)
    snm = sb("f_snm" + sfx, [128, 16], F32)
    Eb = sb("f_eb" + sfx, [128, 640], BF16)
    Et = sb("f_et" + sfx, [128, 640], BF16)
    Ob = sb("f_ob" + sfx, [128, 4, CH], BF16)
    yo = sb("f_yo" + sfx, [128, NFT, CH], F32)
    qn = sb("f_qn" + sfx, [128, CH], F32)
    sh = sb("f_sh" + sfx, [128, CH], F32)
    t1 = sb("f_t1" + sfx, [128, CH], F32)

    ps_a = ps("f_pa" + sfx, [128, CH])
    ps_b = ps("f_pb" + sfx, [128, CH])
    ps_c = ps("f_pc" + sfx, [128, CH])
    ps_d = ps("f_pd" + sfx, [128, CH])
    ps_e = ps("f_pe" + sfx, [128, CH])
    ps_f = ps("f_pf" + sfx, [128, CH])
    ps_g = ps("f_pg" + sfx, [128, CH])
    ps_h = ps("f_ph" + sfx, [128, 1024], BF16)

    swapmask = [i + 1 if i % 2 == 0 else i - 1 for i in range(32)]
    LN8 = float(np.log(8.0))

    def f_z(eng):
        eng.do(lambda v: v.memset(xn.t[:, :, 0:1], 0.0), writes=[xn])
    p.sec("v", f_z)

    for c in range(NCH):
        c0 = c * CH
        ycc = yc[c % 2]
        kf_c, kf_p = kf[c % 2], kf[(c + 1) % 2]
        vt_c, vt_p = vt[c % 2], vt[(c + 1) % 2]
        rtc = rt2[c % 2]

        def f_load(eng, c0=c0, ycc=ycc, rtc=rtc):
            eng.do(lambda y: y.dma_start(out=ycc.t[:], in_=src3[:, :, c0:c0 + CH]),
                   writes=[ycc], dma=True)
            eng.do(lambda y: y.dma_start(
                out=rtc.t[:], in_=rtab.rearrange("a p n -> p a n")[:, :, c0:c0 + CH]),
                writes=[rtc], dma=True)
        p.sec("y", f_load)

        if first_ln:
            emit_ln(p, C, lambda i, ycc=ycc: ycc.t[:, i, :], [ycc], sq, lrow,
                    ps_e, ps_f, ps_c, ps_d, CH, sq_bf16=True,
                    outap=lambda i: yn.t[:, i, :], outbufs=[yn],
                    xmap=lambda i, ycc=ycc: ycc.t[:, i, :], xmbufs=[ycc])

            def f_xn(eng):
                for i in range(NFT):
                    eng.do(lambda s, i=i: s.activation(xn.t[:, i, 1:CH + 1],
                                                       yn.t[:, i, :], AF.Copy),
                           reads=[yn], writes=[xn])
            p.sec("s", f_xn)
            base = yn
        else:
            def f_xn(eng, ycc=ycc):
                for i in range(NFT):
                    eng.do(lambda s, i=i: s.activation(xn.t[:, i, 1:CH + 1],
                                                       ycc.t[:, i, :], AF.Copy),
                           reads=[ycc], writes=[xn])
            p.sec("s", f_xn)
            base = ycc

        for m in range(12):
            pp = ps_a if m % 2 == 0 else ps_b

            def f_mm(eng, m=m, pp=pp):
                for k in range(NFT):
                    off = 1 if k < 4 else 0
                    eng.do(lambda t, k=k, m=m, off=off: t.matmul(
                        pp.t[:], wq.t[:, k, 128 * m:128 * m + 128],
                        xn.t[:, k, off:off + CH],
                        start=(k == 0), stop=(k == NFT - 1)),
                        reads=[wq, xn], writes=[pp])
            p.sec("t", f_mm)

            def f_ev(eng, m=m, pp=pp):
                if m < 4:
                    eng.do(lambda s, m=m: s.activation(qf.t[:, m, :], pp.t[:], AF.Copy),
                           reads=[pp], writes=[qf])
                elif m < 8:
                    eng.do(lambda s, m=m: s.activation(kraw.t[:, m - 4, :], pp.t[:],
                                                       AF.Copy),
                           reads=[pp], writes=[kraw])
                else:
                    eng.do(lambda s, m=m: s.activation(vb.t[:, m - 8, :], pp.t[:],
                                                       AF.Copy),
                           reads=[pp], writes=[vb])
            p.sec("s", f_ev)

        if c == _DBG_CHUNK:
            def f_dbg2(eng):
                dbg = p.dbg
                eng.do(lambda g: g.dma_start(out=dbg[1, :, 0:CH],
                                             in_=kraw.t[:, 0, :]),
                       reads=[kraw], dma=True)
            p.sec("g", f_dbg2)

        def f_halo(eng):
            eng.do(lambda v: v.tensor_copy(xn.t[:, :, 0:1], xn.t[:, :, CH:CH + 1]),
                   rw=[xn])
        p.sec("v", f_halo)

        for pr in range(4):
            for which in range(2):
                srcb = qf if which == 0 else kraw
                dstb = qb if which == 0 else kf_c
                sa_, sb_ = (qsa, qsb) if which == 0 else (ksa, ksb)
                bias = LN8 if which == 0 else 0.0
                ct, st_ = (0, 1) if which == 0 else (2, 3)

                def f_sq2(eng, pr=pr, srcb=srcb):
                    eng.do(lambda s: s.activation(sqq.t[:], srcb.t[:, pr, :], AF.Square),
                           reads=[srcb], writes=[sqq])
                p.sec("s", f_sq2)

                def f_ls(eng):
                    eng.do(lambda t: t.matmul(ps_e.t[0:1, :], C["halfselb"].t[:, 0:1],
                                              sqq.t[:], start=True, stop=True),
                           reads=[sqq, C["halfselb"]], writes=[ps_e])
                    eng.do(lambda t: t.matmul(ps_f.t[0:1, :], C["halfselb"].t[:, 1:2],
                                              sqq.t[:], start=True, stop=True),
                           reads=[sqq, C["halfselb"]], writes=[ps_f])
                p.sec("t", f_ls)

                def f_rw(eng):
                    eng.do(lambda s: s.activation(rows.t[0:1, 0:CH], ps_e.t[0:1, :],
                                                  AF.Ln), reads=[ps_e], writes=[rows])
                    eng.do(lambda s: s.activation(rows.t[0:1, CH:2 * CH], ps_f.t[0:1, :],
                                                  AF.Ln), reads=[ps_f], rw=[rows])
                    eng.do(lambda s: s.activation(rows.t[0:1, :], rows.t[0:1, :],
                                                  AF.Exp, scale=-0.5), rw=[rows])
                p.sec("s", f_rw)

                def f_cap(eng, which=which):
                    if which == 0:
                        eng.do(lambda v: v.tensor_scalar(rows.t[0:1, :], rows.t[0:1, :],
                                                         8.0, 8e12, ALU.mult, ALU.min),
                               rw=[rows])
                    else:
                        eng.do(lambda v: v.tensor_scalar_min(rows.t[0:1, :],
                                                             rows.t[0:1, :], 1e12),
                               rw=[rows])
                p.sec("v", f_cap)

                def f_qm(eng, sa_=sa_, sb_=sb_):
                    eng.do(lambda t: t.matmul(ps_g.t[:], sa_.t[0:1, :],
                                              rows.t[0:1, 0:CH], start=True, stop=False),
                           reads=[rows, sa_], writes=[ps_g])
                    eng.do(lambda t: t.matmul(ps_g.t[:], sb_.t[0:1, :],
                                              rows.t[0:1, CH:2 * CH], start=False,
                                              stop=True),
                           reads=[rows, sb_], writes=[ps_g])
                p.sec("t", f_qm)

                def f_rot(eng, pr=pr, srcb=srcb, dstb=dstb, ct=ct, st_=st_, rtc=rtc):
                    eng.do(lambda v: v.tensor_mul(qn.t[:], srcb.t[:, pr, :], ps_g.t[:]),
                           reads=[srcb, ps_g], writes=[qn])
                    eng.do(lambda v: v.stream_shuffle(sh.t[:], qn.t[:], swapmask),
                           reads=[qn], writes=[sh])
                    eng.do(lambda v, ct=ct: v.tensor_mul(t1.t[:], qn.t[:],
                                                         rtc.t[:, ct, :]),
                           reads=[qn, rtc], writes=[t1])
                    eng.do(lambda v, st_=st_: v.tensor_mul(sh.t[:], sh.t[:],
                                                           rtc.t[:, st_, :]),
                           reads=[rtc], rw=[sh])
                    eng.do(lambda v, pr=pr: v.tensor_add(dstb.t[:, pr, :], t1.t[:],
                                                         sh.t[:]),
                           reads=[t1, sh], writes=[dstb])
                p.sec("v", f_rot)

        for tt in range(4):
            def f_tr(eng, tt=tt):
                for pr in range(4):
                    eng.do(lambda t, pr=pr, tt=tt: t.transpose(
                        ps_h.t[:, 128 * pr:128 * pr + 128],
                        vb.t[:, pr, 128 * tt:128 * tt + 128], C["ident"].t[:]),
                        reads=[vb, C["ident"]], writes=[ps_h])
            p.sec("t", f_tr)

            def f_tre(eng, tt=tt, vt_c=vt_c):
                eng.do(lambda s, tt=tt: s.activation(vt_c.t[:, tt, :, :],
                                                     ps_h.t[:, 0:512], AF.Copy),
                       reads=[ps_h], writes=[vt_c])
            p.sec("s", f_tre)

        for pr in range(4):
            for r in range(4):
                P = (4 - r) * 128 if c > 0 else 0
                W = (r + 1) * 128
                for h in range(2):
                    hb = 64 * h
                    s0 = ps_a if h == 0 else ps_c
                    s1 = ps_b if h == 0 else ps_d
                    col = 8 * h

                    def f_s(eng, pr=pr, r=r, h=h, hb=hb, s0=s0, s1=s1, P=P, W=W,
                            kf_c=kf_c, kf_p=kf_p):
                        if P:
                            eng.do(lambda t: t.matmul(
                                s0.t[:, 0:P], qb.t[hb:hb + 64, pr, 128 * r:128 * r + 128],
                                kf_p.t[hb:hb + 64, pr, 128 * r:CH],
                                start=True, stop=True, tile_position=(hb, 0)),
                                reads=[qb, kf_p], writes=[s0])
                        eng.do(lambda t: t.matmul(
                            s1.t[:, 0:W], qb.t[hb:hb + 64, pr, 128 * r:128 * r + 128],
                            kf_c.t[hb:hb + 64, pr, 0:W],
                            start=True, stop=True, tile_position=(hb, 0)),
                            reads=[qb, kf_c], writes=[s1])
                    p.sec("t", f_s)

                    def f_m(eng, s0=s0, s1=s1, P=P, W=W, col=col):
                        if P:
                            eng.do(lambda v: v.tensor_add(s0.t[:, 0:128], s0.t[:, 0:128],
                                                          C["mprev"].t[:]),
                                   reads=[C["mprev"]], rw=[s0])
                        eng.do(lambda v, W=W: v.tensor_add(s1.t[:, W - 128:W],
                                                           s1.t[:, W - 128:W],
                                                           C["mdiag"].t[:]),
                               reads=[C["mdiag"]], rw=[s1])
                        if P:
                            eng.do(lambda v: v.reduce_max(snm.t[:, col:col + 1],
                                                          s0.t[:, 0:P], axis=AX.X,
                                                          negate=True),
                                   reads=[s0], writes=[snm])
                        eng.do(lambda v, W=W: v.reduce_max(snm.t[:, col + 1:col + 2],
                                                           s1.t[:, 0:W], axis=AX.X,
                                                           negate=True),
                               reads=[s1], rw=[snm])
                        if P:
                            eng.do(lambda v: v.tensor_tensor(
                                snm.t[:, col + 2:col + 3], snm.t[:, col:col + 1],
                                snm.t[:, col + 1:col + 2], ALU.min), rw=[snm])
                        else:
                            eng.do(lambda v: v.tensor_copy(snm.t[:, col + 2:col + 3],
                                                           snm.t[:, col + 1:col + 2]),
                                   rw=[snm])
                    p.sec("v", f_m)

                    def f_e(eng, s0=s0, s1=s1, P=P, W=W, col=col):
                        if P:
                            eng.do(lambda s: s.activation(
                                Eb.t[:, 0:P], s0.t[:, 0:P], AF.Exp,
                                bias=snm.t[:, col + 2:col + 3],
                                accum_out=snm.t[:, col + 3:col + 4]),
                                reads=[s0, snm], writes=[Eb], rw=[snm])
                        eng.do(lambda s, P=P, W=W: s.activation(
                            Eb.t[:, P:P + W], s1.t[:, 0:W], AF.Exp,
                            bias=snm.t[:, col + 2:col + 3],
                            accum_out=snm.t[:, col + 4:col + 5]),
                            reads=[s1, snm], writes=[Eb], rw=[snm])
                    p.sec("s", f_e)

                    def f_n(eng, P=P, W=W, col=col):
                        if P:
                            eng.do(lambda v: v.tensor_add(snm.t[:, col + 5:col + 6],
                                                          snm.t[:, col + 3:col + 4],
                                                          snm.t[:, col + 4:col + 5]),
                                   rw=[snm])
                        else:
                            eng.do(lambda v: v.tensor_copy(snm.t[:, col + 5:col + 6],
                                                           snm.t[:, col + 4:col + 5]),
                                   rw=[snm])
                        eng.do(lambda v: v.reciprocal(snm.t[:, col + 6:col + 7],
                                                      snm.t[:, col + 5:col + 6]),
                               rw=[snm])
                        eng.do(lambda v, P=P, W=W: v.tensor_scalar_mul(
                            Eb.t[:, 0:P + W], Eb.t[:, 0:P + W],
                            snm.t[:, col + 6:col + 7]),
                            reads=[snm], rw=[Eb])
                    p.sec("v", f_n)

                    ns = (P + W) // 128

                    def f_t2(eng, ns=ns):
                        for s2 in range(ns):
                            eng.do(lambda t, s2=s2: t.transpose(
                                ps_h.t[:, 128 * s2:128 * s2 + 128],
                                Eb.t[:, 128 * s2:128 * s2 + 128], C["ident"].t[:]),
                                reads=[Eb, C["ident"]], writes=[ps_h])
                    p.sec("t", f_t2)

                    def f_te(eng, ns=ns):
                        eng.do(lambda s, ns=ns: s.activation(Et.t[:, 0:128 * ns],
                                                             ps_h.t[:, 0:128 * ns],
                                                             AF.Copy),
                               reads=[ps_h], writes=[Et])
                    p.sec("s", f_te)

                    def f_pv(eng, pr=pr, r=r, h=h, hb=hb, ns=ns, P=P,
                             vt_c=vt_c, vt_p=vt_p):
                        np_ = P // 128
                        for s2 in range(ns):
                            if s2 < np_:
                                vsl = vt_p
                                tt = r + s2
                            else:
                                vsl = vt_c
                                tt = s2 - np_
                            oslc = (ps_g.t[0:64, 0:128] if h == 0
                                    else ps_g.t[64:128, 0:128])
                            eng.do(lambda t, s2=s2, vsl=vsl, tt=tt, oslc=oslc: t.matmul(
                                oslc, vsl.t[:, tt, pr, 64 * h:64 * h + 64],
                                Et.t[:, 128 * s2:128 * s2 + 128],
                                start=(s2 == 0), stop=(s2 == ns - 1),
                                tile_position=(0, 64 * h)),
                                reads=[vsl, Et], writes=[ps_g])
                    p.sec("t", f_pv)

                if c == _DBG_CHUNK and False and pr == 0 and r == 2:
                    def f_dba(eng):
                        dbg = p.dbg
                        eng.do(lambda g: g.dma_start(out=dbg[4, :, 0:640], in_=Eb.t[:]),
                               reads=[Eb], dma=True)
                        eng.do(lambda g: g.dma_start(out=dbg[5, :, 0:640], in_=Et.t[:]),
                               reads=[Et], dma=True)
                        eng.do(lambda g: g.dma_start(out=dbg[3, :, 520:536],
                                                     in_=snm.t[:]),
                               reads=[snm], dma=True)
                    p.sec("g", f_dba)

                def f_oe(eng, pr=pr, r=r):
                    eng.do(lambda s: s.activation(Ob.t[:, pr, 128 * r:128 * r + 128],
                                                  ps_g.t[:, 0:128], AF.Copy),
                           reads=[ps_g], writes=[Ob])
                p.sec("s", f_oe)

        if c == _DBG_CHUNK:
            def f_dbg(eng, c0=c0, ycc=ycc, kf_c=kf_c):
                dbg = p.dbg
                eng.do(lambda g: g.dma_start(out=dbg[0, :, 0:CH], in_=qb.t[:, 0, :]),
                       reads=[qb], dma=True)
                eng.do(lambda g: g.dma_start(out=dbg[1, :, 0:CH], in_=kf_c.t[:, 0, :]),
                       reads=[kf_c], dma=True)
                eng.do(lambda g: g.dma_start(out=dbg[2, :, 0:CH], in_=vb.t[:, 0, :]),
                       reads=[vb], dma=True)
                eng.do(lambda g: g.dma_start(out=dbg[3, :, 0:CH + 1],
                                             in_=xn.t[:, 0, :]),
                       reads=[xn], dma=True)
                eng.do(lambda g: g.dma_start(out=dbg[6, :, 0:CH], in_=Ob.t[:, 0, :]),
                       reads=[Ob], dma=True)
                eng.do(lambda g: g.dma_start(out=dbg[7, :, 0:CH],
                                             in_=(yn if first_ln else ycc).t[:, 0, :]),
                       reads=[(yn if first_ln else ycc)], dma=True)
                eng.do(lambda g: g.dma_start(out=dbg[4, :, 0:CH], in_=qf.t[:, 0, :]),
                       reads=[qf], dma=True)
                eng.do(lambda g: g.dma_start(out=dbg[5, :, 0:CH], in_=kraw.t[:, 0, :]),
                       reads=[kraw], dma=True)
            p.sec("g", f_dbg)

        for m in range(NFT):
            pp = ps_e if m % 2 == 0 else ps_f

            def f_wo(eng, m=m, pp=pp):
                for k in range(4):
                    eng.do(lambda t, k=k, m=m: t.matmul(
                        pp.t[:], wo.t[:, k, 128 * m:128 * m + 128], Ob.t[:, k, :],
                        start=(k == 0), stop=(k == 3)),
                        reads=[wo, Ob], writes=[pp])
            p.sec("t", f_wo)

            def f_res(eng, m=m, pp=pp, base=base):
                eng.do(lambda v: v.scalar_tensor_tensor(
                    yo.t[:, m, :], pp.t[:], 1.0, base.t[:, m, :], ALU.bypass, ALU.add),
                    reads=[pp, base], writes=[yo])
            p.sec("v", f_res)

        def f_st(eng, c0=c0):
            eng.do(lambda y: y.dma_start(out=dst3[:, :, c0:c0 + CH], in_=yo.t[:]),
                   reads=[yo], dma=True)
        p.sec("y", f_st)

    es.close()




def st_ffn_mlp(p, nc, C, l, src3, dst3, merge_src3, fwl):
    es = ExitStack()
    sb = lambda nm, shp, dt: p.buf(es.enter_context(nc.sbuf_tensor(nm, shp, dt)), nm)
    ps = lambda nm, shp: p.buf(es.enter_context(nc.psum_tensor(nm, shp, F32)), nm)
    sfx = f"_m{l}_{p.cnt['t']}"

    w1 = sb("m_w1" + sfx, [128, NFT, 2 * HIDP], BF16)
    w2 = sb("m_w2" + sfx, [128, NW2K, D], BF16)
    b1 = sb("m_b1" + sfx, [128, NW1T], F32)
    b2 = sb("m_b2" + sfx, [128, NFT], F32)
    lg = sb("m_lg" + sfx, [128, NFT], F32)
    lb = sb("m_lb" + sfx, [128, NFT], F32)
    load_w(p, w1, fwl["W1"].rearrange("(a p) m -> p a m", p=128))
    load_w(p, w2, fwl["W2"].rearrange("(a p) m -> p a m", p=128))
    load_w(p, b1, fwl["B1"]); load_w(p, b2, fwl["B2"])
    load_w(p, lg, fwl["LG"]); load_w(p, lb, fwl["LB"])

    hb = sb("m_hb" + sfx, [128, NFT, CH + 1], F32)
    sq = sb("m_sq" + sfx, [128, NFT, CH], BF16)
    hnb = sq
    yo = sb("m_yo" + sfx, [128, NFT, CH], F32)
    prod = sb("m_prod" + sfx, [128, NW2K, CH], BF16)
    gel = sb("m_gel" + sfx, [128, CH], BF16)
    rows = sb("m_rows" + sfx, [1, 2 * CH], F32)

    P = [ps(f"m_p{i}" + sfx, [128, CH]) for i in range(8)]
    statx, statq, mb, rb = P[0], P[1], P[2], P[3]
    if merge_src3 is not None:
        mrgt = [sb(f"m_mrg{i}" + sfx, [128, CH], F32) for i in range(2)]

    def f_z(eng):
        eng.do(lambda v: v.memset(hb.t[:, :, 0:1], 0.0), writes=[hb])
    p.sec("v", f_z)

    for c in range(NCH):
        c0 = c * CH

        def f_load(eng, c0=c0):
            eng.do(lambda y: y.dma_start(out=hb.t[:, :, 1:CH + 1],
                                         in_=src3[:, :, c0:c0 + CH]),
                   writes=[hb], dma=True)
        p.sec("y", f_load)

        def shap(i):
            return hb.t[:, i, 1:CH + 1] if i < 4 else hb.t[:, i, 0:CH]

        emit_ln(p, C, shap, [hb], sq, rows, statx, statq, mb, rb, CH, sq_bf16=True,
                outap=None)
        p.skip = _GLA_OP < 2

        def f_ap(eng):
            for i in range(NFT):
                eng.do(lambda v, i=i: v.tensor_sub(yo.t[:, i, :], shap(i), mb.t[:]),
                       reads=[hb, mb], writes=[yo])
            for i in range(NFT):
                eng.do(lambda v, i=i: v.tensor_mul(yo.t[:, i, :], yo.t[:, i, :],
                                                   rb.t[:]), reads=[rb], rw=[yo])
            for i in range(NFT):
                eng.do(lambda v, i=i: v.tensor_scalar(
                    hnb.t[:, i, :], yo.t[:, i, :], lg.t[:, i:i + 1], lb.t[:, i:i + 1],
                    ALU.mult, ALU.add), reads=[yo, lg, lb], writes=[hnb])
        p.sec("v", f_ap)

        def f_halo(eng):
            eng.do(lambda v: v.tensor_copy(hb.t[:, :, 0:1], hb.t[:, :, CH:CH + 1]),
                   rw=[hb])
        p.sec("v", f_halo)

        for j in range(NW2K):
            pa = P[4 + (2 * j) % 4]
            pg = P[4 + (2 * j + 1) % 4]

            def f_w1(eng, j=j, pa=pa, pg=pg):
                for k in range(NFT):
                    eng.do(lambda t, k=k, j=j: t.matmul(
                        pa.t[:], w1.t[:, k, 256 * j:256 * j + 128], hnb.t[:, k, :],
                        start=(k == 0), stop=(k == NFT - 1)),
                        reads=[w1, hnb], writes=[pa])
                for k in range(NFT):
                    eng.do(lambda t, k=k, j=j: t.matmul(
                        pg.t[:], w1.t[:, k, 256 * j + 128:256 * j + 256], hnb.t[:, k, :],
                        start=(k == 0), stop=(k == NFT - 1)),
                        reads=[w1, hnb], writes=[pg])
            p.sec("t", f_w1)

            def f_gel(eng, j=j, pg=pg):
                eng.do(lambda s, j=j: s.activation(gel.t[:], pg.t[:], AF.Gelu,
                                                   bias=b1.t[:, 2 * j + 1:2 * j + 2]),
                       reads=[pg, b1], writes=[gel])
            p.sec("s", f_gel)

            def f_prod(eng, j=j, pa=pa):
                eng.do(lambda v, j=j: v.scalar_tensor_tensor(
                    prod.t[:, j, :], pa.t[:], b1.t[:, 2 * j:2 * j + 1], gel.t[:],
                    ALU.add, ALU.mult), reads=[pa, gel, b1], writes=[prod])
            p.sec("v", f_prod)

        for m in range(NFT):
            pp = P[m % 2]

            def f_w2(eng, m=m, pp=pp):
                for k in range(NW2K):
                    eng.do(lambda t, k=k, m=m: t.matmul(
                        pp.t[:], w2.t[:, k, 128 * m:128 * m + 128], prod.t[:, k, :],
                        start=(k == 0), stop=(k == NW2K - 1)),
                        reads=[w2, prod], writes=[pp])
            p.sec("t", f_w2)

            if merge_src3 is not None:
                def f_ml(eng, m=m, c0=c0):
                    eng.do(lambda y, m=m: y.dma_start(
                        out=mrgt[m % 2].t[:], in_=merge_src3[:, m, c0:c0 + CH]),
                        writes=[mrgt[m % 2]], dma=True)
                p.sec("y", f_ml)

            def f_res(eng, m=m, pp=pp):
                eng.do(lambda v, m=m: v.scalar_tensor_tensor(
                    yo.t[:, m, :], pp.t[:], b2.t[:, m:m + 1], hb.t[:, m, 1:CH + 1],
                    ALU.add, ALU.add), reads=[pp, b2, hb], writes=[yo])
                if merge_src3 is not None:
                    eng.do(lambda v, m=m: v.tensor_add(yo.t[:, m, :], yo.t[:, m, :],
                                                       mrgt[m % 2].t[:]),
                           reads=[mrgt[m % 2]], rw=[yo])
            p.sec("v", f_res)

        def f_st(eng, c0=c0):
            eng.do(lambda y: y.dma_start(out=dst3[:, :, c0:c0 + CH], in_=yo.t[:]),
                   reads=[yo], dma=True)
        p.sec("y", f_st)

    es.close()


def st_gla(p, nc, C, src3, dst3, gw):
    es = ExitStack()
    sb = lambda nm, shp, dt: p.buf(es.enter_context(nc.sbuf_tensor(nm, shp, dt)), nm)

    upt = sb("g_upt", [128, NFT, GD], BF16)
    gatet = sb("g_gatet", [128, NFT, GD], BF16)
    downt = sb("g_downt", [128, NGT, D], BF16)
    qkt = sb("g_qkt", [128, NGT, 128], BF16)
    gtt = sb("g_gtt", [128, NGT, 128], BF16)
    ig1t = sb("g_ig1t", [128, NGT, 16], BF16)
    ig2t = sb("g_ig2t", [16, GD], BF16)
    convd = sb("g_convd", [128, 4, NGT, 128], BF16)
    convb = sb("g_convb", [128, NGT], F32)
    gbv = sb("g_gb", [64, 1], F32)
    taubv = sb("g_taub", [64, 1], F32)
    igb2 = sb("g_igb2", [128, NGT], F32)
    rwv = sb("g_rw", [128, NGT], F32)
    nwv = sb("g_nw", [128, NGT], F32)
    load_w(p, upt, gw["UPT"].rearrange("(a p) m -> p a m", p=128))
    load_w(p, gatet, gw["GATET"].rearrange("(a p) m -> p a m", p=128))
    load_w(p, downt, gw["DOWNT"].rearrange("(a p) m -> p a m", p=128))
    load_w(p, qkt, gw["QKT"].rearrange("(a p) m -> p a m", p=128))
    load_w(p, gtt, gw["GTT"].rearrange("(a p) m -> p a m", p=128))
    load_w(p, ig1t, gw["IG1T"].rearrange("(a p) m -> p a m", p=128))
    load_w(p, ig2t, gw["IG2T"])
    load_w(p, convd, gw["CONVD"])
    load_w(p, convb, gw["CONVB"])
    load_w(p, gbv, gw["GB"])
    load_w(p, taubv, gw["TAUB"])
    load_w(p, igb2, gw["IGB2"])
    load_w(p, rwv, gw["RW"])
    load_w(p, nwv, gw["NW"])

    if _GLA_NOUNION:
        x2c = sb("g_x2c", [128, 8, GCH], F32)
        xn = sb("g_xnr", [128, 8, GCH], BF16)
        xgn = sb("g_xgnr", [128, 16, GCH], BF16)
        xo = sb("g_xor", [128, 8, GCH], F32)
        sq2 = sb("g_sq2r", [128, 16, GIC], BF16)
        o2 = sb("g_o2r", [128, 16, GIC], F32)
        usb = sb("g_usbr", [64, GCH], F32)
        t2g = sb("g_t2gr", [64, GCH], F32)
        ggs = sb("g_ggsr", [64, GCH], F32)
        taus = sb("g_tausr", [64, GCH], F32)
        its = sb("g_itsr", [64, GCH], F32)
        kg = sb("g_kgr", [64, GCH], F32)
    else:
        UNA = sb("g_una", [128, 6144], BF16)
        x2c = Pr.alias(UNA, UNA.t[:][:, 0:4096].bitcast(F32)
                       .rearrange("p (a n) -> p a n", a=8), "x2c")
        xn = Pr.alias(UNA, UNA.t[:][:, 4096:6144]
                      .rearrange("p (a n) -> p a n", a=8), "xn")
        xgn = Pr.alias(UNA, UNA.t[:][:, 0:4096]
                       .rearrange("p (a n) -> p a n", a=16), "xgn")
        UNB = sb("g_unb", [128, 4096], BF16)
        xo = Pr.alias(UNB, UNB.t[:].bitcast(F32).rearrange("p (a n) -> p a n", a=8),
                      "xo")
        sq2 = Pr.alias(UNB, UNB.t[:][:, 0:2048].rearrange("p (a n) -> p a n", a=16),
                       "sq2")
        UNC = sb("g_unc", [128, 4096], BF16)
        o2 = Pr.alias(UNC, UNC.t[:].bitcast(F32).rearrange("p (a n) -> p a n", a=16),
                      "o2")
        _sm = UNC.t[:][:, 0:3072].bitcast(F32)
        usb = Pr.alias(UNC, _sm[0:64, 0:256], "usb")
        t2g = Pr.alias(UNC, _sm[0:64, 256:512], "t2g")
        ggs = Pr.alias(UNC, _sm[0:64, 512:768], "ggs")
        taus = Pr.alias(UNC, _sm[0:64, 768:1024], "taus")
        its = Pr.alias(UNC, _sm[0:64, 1024:1280], "its")
        kg = Pr.alias(UNC, _sm[0:64, 1280:1536], "kg")

    grow = sb("g_grow", [1, 2 * GCH], F32)
    xup = sb("g_xup", [128, NGT, GCH + 3], BF16)
    fgb = sb("g_fgb", [128, NGT, GCH], BF16)
    shiftb = sb("g_shiftb", [128, NGT, GCH], BF16)
    vbg = sb("g_vbg", [128, NGT, GCH], BF16)
    sgt = sb("g_sgt", [128, GCH], BF16)
    qf32 = sb("g_qf32", [64, GCH], F32)
    knf = sb("g_knf", [64, GCH], BF16)
    sqk = Pr.alias(knf, knf.t[:], "sqk")
    rts = sb("g_rts", [64, GCH], F32)
    ig1b = sb("g_ig1b", [16, GCH], BF16)
    S_sb = sb("g_S", [64, GD], F32)
    S_bf = sb("g_Sbf", [64, GD], BF16)
    if _GLA_NOUNION:
        vt_sb = sb("g_vtr", [128, GD], BF16)
    else:
        vt_sb = Pr.alias(UNB, UNB.t[:][:, 2048:4096], "vt")
    bsc = sb("g_bsc", [64, GIC], F32)
    tm1 = sb("g_tm1", [64, GIC], F32)
    ebx = sb("g_ebx", [64, GIC], F32)
    ebt = sb("g_ebt", [64, 1], F32)
    qd_b = sb("g_qd", [64, GIC], BF16)
    kd_b = sb("g_kd", [64, GIC], BF16)
    kdec_b = sb("g_kdec", [64, GIC], BF16)
    qdec_b = sb("g_qdec", [64, GIC], BF16)
    am_sb = sb("g_am", [GIC, GIC], BF16)
    kdt_sb = sb("g_kdt", [GIC, 64], BF16)
    gr2 = Pr.alias(grow, grow.t[:][0:1, GCH:2 * GCH], "gr2")

    def f_init(eng):
        eng.do(lambda v: v.memset(S_sb.t[:], 0.0), writes=[S_sb])
        eng.do(lambda v: v.memset(S_bf.t[:], 0.0), writes=[S_bf])
        eng.do(lambda v: v.memset(xup.t[:, :, 0:3], 0.0), writes=[xup])
    p.sec("v", f_init)

    if _GLA_NOUNION:
        resv = o2.t[:].rearrange("p a n -> p (a n)").bitcast(F32)
    else:
        resv = UNC.t[:].bitcast(F32)
    resv = resv.rearrange("p (a n) -> p a n", a=NFT)
    resb = Pr.alias(o2, resv, "resv")

    _n = NGCH if _GLA_LIMIT is None else _GLA_LIMIT
    for c in range(_n):
        c0 = c * GCH
        po = ExitStack()
        pso = lambda nm: p.buf(po.enter_context(
            nc.psum_tensor(nm + f"_{c}", [128, GCH], F32)), nm)
        statx, statq = pso("go_sx"), pso("go_sq")
        mbp, rbp = pso("go_mb"), pso("go_rb")
        pA, pB = pso("go_pA"), pso("go_pB")
        pQ, pG = pso("go_pQ"), pso("go_pG")

        def f_load(eng, c0=c0):
            eng.do(lambda y: y.dma_start(out=x2c.t[:], in_=src3[:, :, c0:c0 + GCH]),
                   writes=[x2c], dma=True)
        p.sec("y", f_load)

        p.skip = _GLA_OP < 1
        emit_ln(p, C, lambda i: x2c.t[:, i, :], [x2c], xn, grow, statx, statq,
                mbp, rbp, GCH, sq_bf16=True, outap=None)

        p.skip = _GLA_OP < 2

        def f_ap(eng):
            for i in range(NFT):
                eng.do(lambda v, i=i: v.tensor_sub(xo.t[:, i, :], x2c.t[:, i, :],
                                                   mbp.t[:]),
                       reads=[x2c, mbp], writes=[xo])
            for i in range(NFT):
                eng.do(lambda v, i=i: v.tensor_mul(xn.t[:, i, :], xo.t[:, i, :],
                                                   rbp.t[:]),
                       reads=[xo, rbp], writes=[xn])
        p.sec("v", f_ap)

        p.skip = _GLA_OP < 3
        for m in range(NGT):
            pp = pA if m % 2 == 0 else pB

            def f_up(eng, m=m, pp=pp):
                for k in range(NFT):
                    eng.do(lambda t, k=k, m=m: t.matmul(
                        pp.t[:], upt.t[:, k, 128 * m:128 * m + 128], xn.t[:, k, :],
                        start=(k == 0), stop=(k == NFT - 1)),
                        reads=[upt, xn], writes=[pp])
            p.sec("t", f_up)

            def f_upe(eng, m=m, pp=pp):
                eng.do(lambda s, m=m: s.activation(xup.t[:, m, 3:3 + GCH], pp.t[:],
                                                   AF.Copy),
                       reads=[pp], writes=[xup])
            p.sec("s", f_upe)
        for m in range(NGT):
            pp = pA if m % 2 == 0 else pB

            def f_ga(eng, m=m, pp=pp):
                for k in range(NFT):
                    eng.do(lambda t, k=k, m=m: t.matmul(
                        pp.t[:], gatet.t[:, k, 128 * m:128 * m + 128], xn.t[:, k, :],
                        start=(k == 0), stop=(k == NFT - 1)),
                        reads=[gatet, xn], writes=[pp])
            p.sec("t", f_ga)

            def f_gae(eng, m=m, pp=pp):
                eng.do(lambda s, m=m: s.activation(fgb.t[:, m, :], pp.t[:], AF.Silu),
                       reads=[pp], writes=[fgb])
            p.sec("s", f_gae)

        p.skip = _GLA_OP < 4
        for m in range(NGT):
            pp = pQ if m % 2 == 0 else pG

            def f_cv(eng, m=m, pp=pp):
                for j in range(4):
                    eng.do(lambda t, j=j, m=m: t.matmul(
                        pp.t[:], convd.t[:, j, m, :], xup.t[:, m, j:j + GCH],
                        start=(j == 0), stop=(j == 3)),
                        reads=[convd, xup], writes=[pp])
            p.sec("t", f_cv)

            def f_cve(eng, m=m, pp=pp):
                eng.do(lambda s, m=m: s.activation(shiftb.t[:, m, :], pp.t[:],
                                                   AF.Silu, bias=convb.t[:, m:m + 1]),
                       reads=[pp, convb], writes=[shiftb])
            p.sec("s", f_cve)

        p.skip = _GLA_OP < 5
        def f_qk(eng):
            for k in range(NGT):
                eng.do(lambda t, k=k: t.matmul(pQ.t[:], qkt.t[:, k, :],
                                               shiftb.t[:, k, :],
                                               start=(k == 0), stop=(k == NGT - 1)),
                       reads=[qkt, shiftb], writes=[pQ])
            for k in range(NGT):
                eng.do(lambda t, k=k: t.matmul(pG.t[:], gtt.t[:, k, :],
                                               shiftb.t[:, k, :],
                                               start=(k == 0), stop=(k == NGT - 1)),
                       reads=[gtt, shiftb], writes=[pG])
        p.sec("t", f_qk)

        def f_qke(eng):
            eng.do(lambda s: s.activation(qf32.t[:], pQ.t[0:64, :], AF.Copy,
                                          scale=0.125), reads=[pQ], writes=[qf32])
            eng.do(lambda s: s.activation(kg.t, pQ.t[64:128, :], AF.Copy),
                   reads=[pQ], writes=[kg])
            eng.do(lambda s: s.activation(taus.t, pG.t[64:128, :], AF.Sigmoid,
                                          bias=taubv.t[:, 0:1]),
                   reads=[pG, taubv], writes=[taus])
        p.sec("s", f_qke)

        def f_qke2(eng):
            eng.do(lambda v: v.tensor_scalar_add(usb.t, pG.t[0:64, :],
                                                 gbv.t[:, 0:1]),
                   reads=[pG, gbv], writes=[usb])
        p.sec("v", f_qke2)

        p.skip = _GLA_OP < 6
        def f_ig1(eng):
            for k in range(NGT):
                eng.do(lambda t, k=k: t.matmul(statq.t[0:16, :], ig1t.t[:, k, :],
                                               xup.t[:, k, 3:3 + GCH],
                                               start=(k == 0), stop=(k == NGT - 1)),
                       reads=[ig1t, xup], writes=[statq])
        p.sec("t", f_ig1)

        def f_ig1e(eng):
            eng.do(lambda s: s.activation(ig1b.t[:], statq.t[0:16, :], AF.Copy),
                   reads=[statq], writes=[ig1b])
        p.sec("s", f_ig1e)

        for m in range(NGT):
            pp = pA if m % 2 == 0 else pB

            def f_ig2(eng, m=m, pp=pp):
                eng.do(lambda t, m=m: t.matmul(pp.t[:],
                                               ig2t.t[:, 128 * m:128 * m + 128],
                                               ig1b.t[:], start=True, stop=True),
                       reads=[ig2t, ig1b], writes=[pp])
            p.sec("t", f_ig2)

            def f_ig2e(eng, m=m, pp=pp):
                eng.do(lambda s, m=m: s.activation(sgt.t[:], pp.t[:], AF.Sigmoid,
                                                   bias=igb2.t[:, m:m + 1]),
                       reads=[pp, igb2], writes=[sgt])
            p.sec("s", f_ig2e)

            def f_vm(eng, m=m):
                eng.do(lambda v, m=m: v.tensor_mul(vbg.t[:, m, :], sgt.t[:],
                                                   xup.t[:, m, 3:3 + GCH]),
                       reads=[sgt, xup], writes=[vbg])
            p.sec("v", f_vm)

        p.skip = _GLA_OP < 7
        def f_sp2(eng):
            eng.do(lambda s: s.activation(t2g.t, usb.t, AF.Abs),
                   reads=[usb], writes=[t2g])
            eng.do(lambda s: s.activation(t2g.t, t2g.t, AF.Exp, scale=-1.0),
                   rw=[t2g])
            eng.do(lambda s: s.activation(t2g.t, t2g.t, AF.Ln, bias=1.0),
                   rw=[t2g])
        p.sec("s", f_sp2)

        def f_sp3(eng):
            eng.do(lambda v: v.tensor_scalar_max(ggs.t, usb.t, 0.0),
                   reads=[usb], writes=[ggs])
            eng.do(lambda v: v.tensor_add(ggs.t, ggs.t, t2g.t),
                   reads=[t2g], rw=[ggs])
            eng.do(lambda v: v.scalar_tensor_tensor(rts.t[:], ggs.t, -1.0,
                                                    taus.t, ALU.mult, ALU.mult),
                   reads=[ggs, taus], writes=[rts])
        p.sec("v", f_sp3)

        def f_it1(eng):
            eng.do(lambda s: s.activation(t2g.t, ggs.t, AF.Ln),
                   reads=[ggs], writes=[t2g])
        p.sec("s", f_it1)

        def f_it2(eng):
            eng.do(lambda v: v.tensor_mul(t2g.t, t2g.t, taus.t),
                   reads=[taus], rw=[t2g])
        p.sec("v", f_it2)

        def f_it3(eng):
            eng.do(lambda s: s.activation(its.t, t2g.t, AF.Exp),
                   reads=[t2g], writes=[its])
        p.sec("s", f_it3)

        def f_kn1(eng):
            eng.do(lambda s: s.activation(sqk.t[:], kg.t, AF.Square),
                   reads=[kg], writes=[sqk])
        p.sec("s", f_kn1)

        def f_kn2(eng):
            eng.do(lambda t: t.matmul(statx.t[0:1, :], C["onescolb"].t[0:64, 0:1],
                                      sqk.t[:], start=True, stop=True),
                   reads=[sqk, C["onescolb"]], writes=[statx])
        p.sec("t", f_kn2)

        def f_kn3(eng):
            eng.do(lambda s: s.activation(grow.t[0:1, 0:GCH], statx.t[0:1, :], AF.Ln),
                   reads=[statx], writes=[grow])
            eng.do(lambda s: s.activation(grow.t[0:1, 0:GCH], grow.t[0:1, 0:GCH],
                                          AF.Exp, scale=-0.5), rw=[grow])
        p.sec("s", f_kn3)

        def f_kn4(eng):
            eng.do(lambda v: v.tensor_scalar_min(grow.t[0:1, 0:GCH],
                                                 grow.t[0:1, 0:GCH], 1e12), rw=[grow])
        p.sec("v", f_kn4)

        def f_kn5(eng):
            eng.do(lambda t: t.matmul(mbp.t[0:64, :], C["onesrow"].t[0:1, 0:64],
                                      grow.t[0:1, 0:GCH], start=True, stop=True),
                   reads=[grow, C["onesrow"]], writes=[mbp])
        p.sec("t", f_kn5)

        def f_kn6(eng):
            eng.do(lambda v: v.tensor_mul(t2g.t, kg.t, mbp.t[0:64, :]),
                   reads=[kg, mbp], writes=[t2g])
            eng.do(lambda v: v.tensor_mul(knf.t[:], t2g.t, its.t),
                   reads=[t2g, its], writes=[knf])
        p.sec("v", f_kn6)

        p.skip = _GLA_OP < 8
        def f_shrw(eng):
            for m in range(NGT):
                eng.do(lambda v, m=m: v.tensor_scalar_mul(
                    shiftb.t[:, m, :], shiftb.t[:, m, :], rwv.t[:, m:m + 1]),
                    reads=[rwv], rw=[shiftb])
        p.sec("v", f_shrw)

        def f_halo(eng):
            eng.do(lambda v: v.tensor_copy(xup.t[:, :, 0:3],
                                           xup.t[:, :, GCH:GCH + 3]), rw=[xup])
        p.sec("v", f_halo)

        p.skip = False
        po.close()
        p.barrier()

        pss = ExitStack()
        _scan_on = _GLA_PH >= 2
        of_ps = p.buf(pss.enter_context(
            nc.psum_tensor(f"gs_of_{c}", [128, 8, GIC], F32)), "of")
        vt_ps = p.buf(pss.enter_context(
            nc.psum_tensor(f"gs_vt_{c}", [128, GD], BF16)), "vtp")
        at_ps = p.buf(pss.enter_context(
            nc.psum_tensor(f"gs_at_{c}", [128, GIC], F32)), "atp")
        kdt_ps = p.buf(pss.enter_context(
            nc.psum_tensor(f"gs_kdt_{c}", [GIC, 64], BF16)), "kdtp")
        U_ps = p.buf(pss.enter_context(
            nc.psum_tensor(f"gs_u_{c}", [64, GD // 2], F32)), "ups")

        for ic in range(GCH // GIC if _scan_on else 0):
            i0 = ic * GIC

            def f_b(eng, i0=i0):
                eng.do(lambda v: v.tensor_tensor_scan(
                    bsc.t[:], rts.t[:, i0:i0 + GIC], rts.t[:, i0:i0 + GIC], 0.0,
                    ALU.add, ALU.bypass), reads=[rts], writes=[bsc])
                eng.do(lambda v: v.tensor_scalar_sub(tm1.t[:], bsc.t[:],
                                                     bsc.t[:, GIC // 2:GIC // 2 + 1]),
                       reads=[bsc], writes=[tm1])
            p.sec("v", f_b)

            def f_eb1(eng):
                eng.do(lambda s: s.activation(ebx.t[:], tm1.t[:], AF.Exp),
                       reads=[tm1], writes=[ebx])
            p.sec("s", f_eb1)

            def f_qd(eng, i0=i0):
                eng.do(lambda v: v.tensor_mul(qd_b.t[:], qf32.t[:, i0:i0 + GIC],
                                              ebx.t[:]), reads=[qf32, ebx],
                       writes=[qd_b])
                eng.do(lambda v: v.tensor_scalar(tm1.t[:], bsc.t[:], -1.0,
                                                 bsc.t[:, GIC // 2:GIC // 2 + 1],
                                                 ALU.mult, ALU.add),
                       reads=[bsc], writes=[tm1])
            p.sec("v", f_qd)

            def f_eb2(eng):
                eng.do(lambda s: s.activation(ebx.t[:], tm1.t[:], AF.Exp),
                       reads=[tm1], writes=[ebx])
            p.sec("s", f_eb2)

            def f_kd(eng, i0=i0):
                eng.do(lambda v: v.tensor_mul(kd_b.t[:], knf.t[:, i0:i0 + GIC],
                                              ebx.t[:]), reads=[knf, ebx],
                       writes=[kd_b])
                eng.do(lambda v: v.tensor_scalar(tm1.t[:], bsc.t[:], -1.0,
                                                 bsc.t[:, GIC - 1:GIC], ALU.mult,
                                                 ALU.add),
                       reads=[bsc], writes=[tm1])
            p.sec("v", f_kd)

            def f_eb3(eng):
                eng.do(lambda s: s.activation(ebx.t[:], tm1.t[:], AF.Exp),
                       reads=[tm1], writes=[ebx])
                eng.do(lambda s: s.activation(ebt.t[:], bsc.t[0:64, GIC - 1:GIC],
                                              AF.Exp), reads=[bsc], writes=[ebt])
            p.sec("s", f_eb3)

            def f_kdec(eng, i0=i0):
                eng.do(lambda v: v.tensor_mul(kdec_b.t[:], knf.t[:, i0:i0 + GIC],
                                              ebx.t[:]), reads=[knf, ebx],
                       writes=[kdec_b])
            p.sec("v", f_kdec)

            def f_eb4(eng):
                eng.do(lambda s: s.activation(ebx.t[:], bsc.t[:], AF.Exp),
                       reads=[bsc], writes=[ebx])
            p.sec("s", f_eb4)

            def f_qdec(eng, i0=i0):
                eng.do(lambda v: v.tensor_mul(qdec_b.t[:], qf32.t[:, i0:i0 + GIC],
                                              ebx.t[:]), reads=[qf32, ebx],
                       writes=[qdec_b])
            p.sec("v", f_qdec)

            def f_at(eng):
                eng.do(lambda t: t.matmul(at_ps.t[:], kd_b.t[:], qd_b.t[:],
                                          start=True, stop=True),
                       reads=[kd_b, qd_b], writes=[at_ps])
                eng.do(lambda t: t.transpose(kdt_ps.t[:], kdec_b.t[:],
                                             C["ident"].t[0:64, 0:64]),
                       reads=[kdec_b, C["ident"]], writes=[kdt_ps])
            p.sec("t", f_at)

            def f_ate(eng):
                eng.do(lambda v: v.tensor_mul(am_sb.t[:], at_ps.t[:],
                                              C["gtri"].t[:]),
                       reads=[at_ps, C["gtri"]], writes=[am_sb])
                eng.do(lambda v: v.tensor_copy(kdt_sb.t[:], kdt_ps.t[:]),
                       reads=[kdt_ps], writes=[kdt_sb])
            p.sec("v", f_ate)

            def f_vt(eng, i0=i0):
                for m in range(NGT):
                    eng.do(lambda t, m=m: t.transpose(
                        vt_ps.t[:, 128 * m:128 * m + 128],
                        vbg.t[:, m, i0:i0 + GIC], C["ident"].t[:]),
                        reads=[vbg, C["ident"]], writes=[vt_ps])
            p.sec("t", f_vt)

            def f_vte(eng):
                eng.do(lambda s: s.activation(vt_sb.t[:], vt_ps.t[:], AF.Copy),
                       reads=[vt_ps], writes=[vt_sb])
            p.sec("s", f_vte)

            for hf in range(2):
                def f_o(eng, hf=hf):
                    for m2 in range(8):
                        m = 8 * hf + m2
                        eng.do(lambda t, m=m, m2=m2: t.matmul(
                            of_ps.t[:, m2, :],
                            S_bf.t[:, 128 * m:128 * m + 128], qdec_b.t[:],
                            start=True, stop=False),
                            reads=[S_bf, qdec_b], writes=[of_ps])
                        eng.do(lambda t, m=m, m2=m2: t.matmul(
                            of_ps.t[:, m2, :],
                            vt_sb.t[:, 128 * m:128 * m + 128], am_sb.t[:],
                            start=False, stop=True),
                            reads=[vt_sb, am_sb], writes=[of_ps])
                p.sec("t", f_o)

                def f_g1(eng, hf=hf, i0=i0):
                    eng.do(lambda v: v.tensor_add(
                        o2.t[:, 8 * hf:8 * hf + 8, :],
                        shiftb.t[:, 8 * hf:8 * hf + 8, i0:i0 + GIC], of_ps.t[:]),
                        reads=[shiftb, of_ps], writes=[o2])
                p.sec("v", f_g1)

            for hf in range(2):
                def f_u(eng, hf=hf):
                    for q2 in range(2):
                        eng.do(lambda t, hf=hf, q2=q2: t.matmul(
                            U_ps.t[:, 512 * q2:512 * q2 + 512], kdt_sb.t[:],
                            vt_sb.t[:, 1024 * hf + 512 * q2:
                                    1024 * hf + 512 * q2 + 512],
                            start=True, stop=True),
                            reads=[kdt_sb, vt_sb], writes=[U_ps])
                p.sec("t", f_u)

                def f_su(eng, hf=hf):
                    eng.do(lambda v, hf=hf: v.tensor_scalar_mul(
                        S_sb.t[:, 1024 * hf:1024 * hf + 1024],
                        S_sb.t[:, 1024 * hf:1024 * hf + 1024], ebt.t[:, 0:1]),
                        reads=[ebt], rw=[S_sb])
                    eng.do(lambda v, hf=hf: v.tensor_add(
                        S_sb.t[:, 1024 * hf:1024 * hf + 1024],
                        S_sb.t[:, 1024 * hf:1024 * hf + 1024], U_ps.t[:]),
                        reads=[U_ps], rw=[S_sb])
                p.sec("v", f_su)

            def f_sbf(eng):
                eng.do(lambda v: v.tensor_copy(S_bf.t[:], S_sb.t[:]),
                       reads=[S_sb], writes=[S_bf])
            p.sec("v", f_sbf)

            def f_g2(eng, i0=i0):
                eng.do(lambda v, i0=i0: v.tensor_mul(
                    o2.t[:], o2.t[:], fgb.t[:, :, i0:i0 + GIC]),
                    reads=[fgb], rw=[o2])
            p.sec("v", f_g2)

            def f_g3(eng):
                eng.do(lambda s: s.activation(sq2.t[:], o2.t[:], AF.Square),
                       reads=[o2], writes=[sq2])
            p.sec("s", f_g3)

            def f_g4(eng):
                for m in range(NGT):
                    eng.do(lambda t, m=m: t.matmul(
                        at_ps.t[0:1, :], C["onescolb"].t[:, 0:1],
                        sq2.t[:, m, :],
                        start=(m == 0), stop=(m == NGT - 1)),
                        reads=[sq2, C["onescolb"]], writes=[at_ps])
            p.sec("t", f_g4)

            def f_g5(eng):
                eng.do(lambda v: v.tensor_scalar(gr2.t[0:1, 0:GIC],
                                                 at_ps.t[0:1, :], 1.0 / GD, 1e-5,
                                                 ALU.mult, ALU.add),
                       reads=[at_ps], writes=[gr2])
            p.sec("v", f_g5)

            def f_g6(eng):
                eng.do(lambda s: s.activation(gr2.t[0:1, GIC:2 * GIC],
                                              gr2.t[0:1, 0:GIC], AF.Ln), rw=[gr2])
                eng.do(lambda s: s.activation(gr2.t[0:1, 0:GIC],
                                              gr2.t[0:1, GIC:2 * GIC], AF.Exp,
                                              scale=-0.5), rw=[gr2])
            p.sec("s", f_g6)

            def f_g7(eng):
                eng.do(lambda t: t.matmul(at_ps.t[:], C["onesrow"].t[0:1, :],
                                          gr2.t[0:1, 0:GIC], start=True, stop=True),
                       reads=[gr2, C["onesrow"]], writes=[at_ps])
            p.sec("t", f_g7)

            def f_g8(eng, i0=i0):
                for m in range(NGT):
                    eng.do(lambda v, m=m: v.scalar_tensor_tensor(
                        xgn.t[:, m, i0:i0 + GIC], o2.t[:, m, :],
                        nwv.t[:, m:m + 1], at_ps.t[:], ALU.mult, ALU.mult),
                        reads=[o2, nwv, at_ps], writes=[xgn])
            p.sec("v", f_g8)

        pss.close()
        p.barrier()

        pd = ExitStack()
        if _GLA_PH < 3:
            def f_st0(eng, c0=c0):
                eng.do(lambda y: y.dma_start(out=dst3[:, :, c0:c0 + GCH],
                                             in_=xo.t[:]),
                       reads=[xo], dma=True)
            p.sec("y", f_st0)
            pd.close()
            p.barrier()
            continue

        def f_rld(eng, c0=c0):
            eng.do(lambda y: y.dma_start(out=resv, in_=src3[:, :, c0:c0 + GCH]),
                   writes=[resb], dma=True)
        p.sec("y", f_rld)
        pD = [p.buf(pd.enter_context(
            nc.psum_tensor(f"gd_p{i}_{c}", [128, GCH], F32)), f"pD{i}")
            for i in range(2)]
        for m in range(NFT):
            pp = pD[m % 2]

            def f_dn(eng, m=m, pp=pp):
                for k in range(NGT):
                    eng.do(lambda t, k=k, m=m: t.matmul(
                        pp.t[:], downt.t[:, k, 128 * m:128 * m + 128],
                        xgn.t[:, k, :],
                        start=(k == 0), stop=(k == NGT - 1)),
                        reads=[downt, xgn], writes=[pp])
            p.sec("t", f_dn)

            def f_dne(eng, m=m, pp=pp):
                eng.do(lambda v, m=m: v.scalar_tensor_tensor(
                    xo.t[:, m, :], pp.t[:], 1.0, resv[:, m, :], ALU.bypass, ALU.add),
                    reads=[pp, resb], writes=[xo])
            p.sec("v", f_dne)

        def f_st(eng, c0=c0):
            eng.do(lambda y: y.dma_start(out=dst3[:, :, c0:c0 + GCH], in_=xo.t[:]),
                   reads=[xo], dma=True)
        p.sec("y", f_st)
        pd.close()
        p.barrier()

    es.close()



def build_program(upto):
    nc = bass.Bass("TRN2", target_bir_lowering=False,
                   dynamic_dma_scratch_size=16384)
    stk = ExitStack()

    def din(name, shape, dt=F32):
        return nc.dram_tensor(name, list(shape), dt, kind="ExternalInput").ap()

    xt = din("xt", [D, T_FULL])
    aWKT = din("aWKT", [D, D]); aW1T = din("aW1T", [D, 64]); aW2T = din("aW2T", [64, D])
    rtab = din("rtab", [4, 128, T_FULL])
    mprev = din("mprev", [128, 128]); mdiag = din("mdiag", [128, 128])
    gtri = din("gtri", [GIC, GIC]); ident = din("ident", [128, 128])
    halfsel = din("halfsel", [128, 2])
    fw = {}
    for l in (0, 1):
        fw[l] = dict(
            WQKV=din(f"fWQKV{l}", [D, 1536]),
            QSa=din(f"fQSa{l}", [1, 128]), QSb=din(f"fQSb{l}", [1, 128]),
            KSa=din(f"fKSa{l}", [1, 128]), KSb=din(f"fKSb{l}", [1, 128]),
            WO=din(f"fWO{l}", [512, D]),
            LG=din(f"fLG{l}", [128, NFT]), LB=din(f"fLB{l}", [128, NFT]),
            W1=din(f"fW1{l}", [D, 2 * HIDP]), B1=din(f"fB1{l}", [128, NW1T]),
            W2=din(f"fW2{l}", [HIDP, D]), B2=din(f"fB2{l}", [128, NFT]),
        )
    gw = dict(
        UPT=din("gUPT", [D, GD]), GATET=din("gGATET", [D, GD]),
        DOWNT=din("gDOWNT", [GD, D]), QKT=din("gQKT", [GD, 128]),
        GTT=din("gGTT", [GD, 128]), IG1T=din("gIG1T", [GD, 16]),
        IG2T=din("gIG2T", [16, GD]), CONVD=din("gCONVD", [128, 4, NGT, 128]),
        CONVB=din("gCONVB", [128, NGT]), GB=din("gGB", [64, 1]),
        TAUB=din("gTAUB", [64, 1]), IGB2=din("gIGB2", [128, NGT]),
        RW=din("gRW", [128, NGT]), NW=din("gNW", [128, NGT]),
    )
    res = nc.dram_tensor("res", [D, T_FULL], F32, kind="Internal").ap()
    ffs = nc.dram_tensor("ffs", [D, T_FULL], F32, kind="Internal").ap()
    out = nc.dram_tensor("out", [D, T_FULL], F32, kind="ExternalOutput").ap()
    dbg = (nc.dram_tensor("dbg", [8, 128, 656], F32, kind="ExternalOutput").ap()
           if _DBG_CHUNK >= 0 else None)
    r3 = lambda ap: ap.rearrange("(a p) n -> p a n", p=128)
    xt3, res3, ffs3, out3 = r3(xt), r3(res), r3(ffs), r3(out)

    with stk:
        block = stk.enter_context(nc.Block())
        p = Pr(nc, block, stk)
        p.dbg = dbg

        cst = ExitStack()
        sb = lambda nm, shp, dt: p.buf(cst.enter_context(nc.sbuf_tensor(nm, shp, dt)), nm)
        C = dict(
            ident=sb("c_ident", [128, 128], BF16),
            mprev=sb("c_mprev", [128, 128], F32),
            mdiag=sb("c_mdiag", [128, 128], F32),
            gtri=sb("c_gtri", [GIC, GIC], BF16),
            onesrow=sb("c_onesrow", [1, 128], F32),
            onescol=sb("c_onescol", [128, 2], F32),
            onescolb=sb("c_onescolb", [128, 2], BF16),
            halfselb=sb("c_halfselb", [128, 2], BF16),
        )
        load_w(p, C["ident"], ident)
        load_w(p, C["mprev"], mprev)
        load_w(p, C["mdiag"], mdiag)
        load_w(p, C["gtri"], gtri)
        load_w(p, C["halfselb"], halfsel)

        def f_c(eng):
            eng.do(lambda g: g.memset(C["onesrow"].t[:], 1.0), writes=[C["onesrow"]])
            eng.do(lambda g: g.memset(C["onescol"].t[:], 1.0), writes=[C["onescol"]])
            eng.do(lambda g: g.memset(C["onescolb"].t[:], 1.0), writes=[C["onescolb"]])
        p.sec("g", f_c)
        p.barrier()

        st_attn_a(p, nc, C, xt3, res3 if upto > 1 else res3, aWKT, aW1T, aW2T)
        p.barrier()
        if upto >= 2:
            st_ffn_attn(p, nc, C, 0, True, res3, ffs3, fw[0], rtab)
            p.barrier()
        if upto >= 3:
            st_ffn_mlp(p, nc, C, 0, ffs3, ffs3, None, fw[0])
            p.barrier()
        if upto >= 4:
            st_ffn_attn(p, nc, C, 1, False, ffs3,
                        (res3 if _F1A_OOP and upto == 4 else ffs3), fw[1], rtab)
            p.barrier()
        if upto >= 5:
            st_ffn_mlp(p, nc, C, 1, ffs3, (res3 if upto > 5 else out3), res3, fw[1])
            p.barrier()
        if upto >= 6:
            st_gla(p, nc, C, res3, (res3 if upto > 6 else out3), gw)
            p.barrier()
        if upto >= 7:
            st_ffn_attn(p, nc, C, 0, True, res3, ffs3, fw[0], rtab)
            p.barrier()
        if upto >= 8:
            st_ffn_mlp(p, nc, C, 0, ffs3, ffs3, None, fw[0])
            p.barrier()
        if upto >= 9:
            st_ffn_attn(p, nc, C, 1, False, ffs3, ffs3, fw[1], rtab)
            p.barrier()
        if upto >= 10:
            st_ffn_mlp(p, nc, C, 1, ffs3, out3, res3, fw[1])
            p.barrier()

        if upto in (1, 2, 3, 4, 7, 8, 9):
            src3 = res3 if (upto == 1 or (_F1A_OOP and upto == 4)) else ffs3
            cp = ExitStack()
            tb = p.buf(cp.enter_context(nc.sbuf_tensor("cpb", [128, NFT, CH], F32)))
            for c in range(NCH):
                c0 = c * CH

                def fl(eng, c0=c0):
                    eng.do(lambda y: y.dma_start(out=tb.t[:], in_=src3[:, :, c0:c0 + CH]),
                           writes=[tb], dma=True)
                    eng.do(lambda y: y.dma_start(out=out3[:, :, c0:c0 + CH], in_=tb.t[:]),
                           reads=[tb], dma=True)
                p.sec("y", fl)
            cp.close()

        snap = dict(p.cnt)

        def fin(eng):
            for sn, c in snap.items():
                if c and p.seen["y"][sn] < c:
                    eng.raw.wait_ge(p.sem[sn], c)
                    p.seen["y"][sn] = c
        p.sec("y", fin)
        cst.close()
    return nc


def _kernel_impl(inputs, upto=_UPTO, cores=None):
    w = _prep(inputs)
    xin = np.asarray(inputs["L0_attn_a_x"])
    B = xin.shape[0]
    if cores is None:
        cores = list(range(B))
    nc = build_program(upto)
    in_maps = []
    for b in cores:
        m = {"xt": np.ascontiguousarray(xin[b].T)}
        m.update(w)
        in_maps.append(m)
    resl = run_bass_kernel_spmd(nc, in_maps, core_ids=list(range(len(cores))))
    global _LAST
    _LAST = resl.results
    outs = [np.ascontiguousarray(r["out"].T) for r in resl.results]
    return np.stack(outs, 0)


_LAST = None


def kernel(**inputs):
    return _kernel_impl(inputs)


# version 02
# speedup vs baseline: 1.0047x; 1.0047x over previous
import numpy as np
import concourse.bass as bass
import concourse.mybir as mybir
from concourse.bass_utils import run_bass_kernel_spmd
from contextlib import ExitStack

F32 = mybir.dt.float32
BF16 = mybir.dt.bfloat16
AF = mybir.ActivationFunctionType
ALU = mybir.AluOpType
AX = mybir.AxisListType

T_FULL = 8192
D = 1024
NFT = 8
CH = 512
NCH = T_FULL // CH
HEADS = 8
DH = 64
WIN = 512
HIDP = 2816
NW1T = 44
NW2K = 22
GD = 2048
NGT = 16
GCH = 256
NGCH = T_FULL // GCH
GIC = 128
NEG = -30000.0

_UPTO = 10
_DBG_CHUNK = -1
_GLA_LIMIT = None
_GLA_PH = 3
_GLA_OP = 8
_LN_LV = 9
_GLA_NOUNION = False
_T_RELAX = True
_F1A_OOP = False


def _host_consts():
    n = T_FULL
    half = np.arange(0, DH, 2, dtype=np.float64)
    inv_freq = 1.0 / 10000.0 ** (half / DH)
    t = np.arange(n, dtype=np.float64)
    freqs32 = t[:, None] * inv_freq[None, :]
    base32 = (half + 0.4 * DH) / (1.4 * DH)
    power = (t - n // 2) / float(WIN // 2)
    scale32 = base32[None, :] ** power[:, None]
    freqs = np.concatenate([freqs32, freqs32], 1)
    scale = np.concatenate([scale32, scale32], 1)
    cos = np.cos(freqs); sin = np.sin(freqs)
    perm = np.empty(64, np.int64)
    perm[0::2] = np.arange(32); perm[1::2] = np.arange(32, 64)
    cos = cos[:, perm]; sin = sin[:, perm]; scale = scale[:, perm]
    sgn = np.empty(64); sgn[0::2] = -1.0; sgn[1::2] = 1.0
    cq = (cos * scale).T
    sq = (sin * scale).T * sgn[:, None]
    ck = (cos / scale).T
    sk = (sin / scale).T * sgn[:, None]
    dup = lambda a: np.concatenate([a, a], 0)
    rtab = np.ascontiguousarray(
        np.stack([dup(cq), dup(sq), dup(ck), dup(sk)], 0).astype(np.float32))
    i = np.arange(128)[:, None]; j = np.arange(128)[None, :]
    mprev = np.where(j >= i, 0.0, NEG).astype(np.float32)
    mdiag = np.where(j <= i, 0.0, NEG).astype(np.float32)
    jj = np.arange(GIC)[:, None]; ii = np.arange(GIC)[None, :]
    gtri = np.where(jj <= ii, 1.0, 0.0).astype(np.float32)
    ident = np.eye(128, dtype=np.float32)
    hs = np.zeros((128, 2), np.float32)
    hs[0:64, 0] = 1.0
    hs[64:128, 1] = 1.0
    return rtab, mprev, mdiag, gtri, ident, perm, hs


def _prep(inputs):
    g = {k: np.asarray(v) for k, v in inputs.items()}
    ac = np.ascontiguousarray
    w = {}
    w["aWKT"] = ac(g["L0_attn_a_W_K"].T)
    w["aW1T"] = ac(g["L0_attn_a_W1"].T)
    w["aW2T"] = ac(g["L0_attn_a_W2"].T)
    rtab, mprev, mdiag, gtri, ident, perm, hs = _host_consts()
    w["rtab"], w["mprev"], w["mdiag"], w["gtri"], w["ident"] = \
        rtab, mprev, mdiag, gtri, ident
    w["halfsel"] = hs
    for l in (0, 1):
        wq = g["L0_ffn_wqkv"][l]
        qc = wq[:, 0:512].reshape(D, HEADS, DH)[:, :, perm].reshape(D, 512)
        kc = wq[:, 512:1024].reshape(D, HEADS, DH)[:, :, perm].reshape(D, 512)
        vc = wq[:, 1024:1536]
        w[f"fWQKV{l}"] = ac(np.concatenate([qc, kc, vc], 1))
        qs = g["L0_ffn_q_scale"][l][perm]; ks = g["L0_ffn_k_scale"][l][perm]
        z64 = np.zeros(64, np.float32)
        w[f"fQSa{l}"] = ac(np.concatenate([qs, z64])[None, :].astype(np.float32))
        w[f"fQSb{l}"] = ac(np.concatenate([z64, qs])[None, :].astype(np.float32))
        w[f"fKSa{l}"] = ac(np.concatenate([ks, z64])[None, :].astype(np.float32))
        w[f"fKSb{l}"] = ac(np.concatenate([z64, ks])[None, :].astype(np.float32))
        w[f"fWO{l}"] = ac(g["L0_ffn_wo"][l])
        w[f"fLG{l}"] = ac(g["L0_ffn_ln_g"][l].reshape(NFT, 128).T)
        w[f"fLB{l}"] = ac(g["L0_ffn_ln_b"][l].reshape(NFT, 128).T)
        w1 = g["L0_ffn_w1"][l]; b1 = g["L0_ffn_b1"][l]
        w1p = np.zeros((D, 2 * HIDP), np.float32)
        b1p = np.zeros(2 * HIDP, np.float32)
        for q2 in range(NW1T // 2):
            a0, a1 = 128 * q2, min(128 * q2 + 128, 2730)
            wd = a1 - a0
            if wd <= 0:
                continue
            w1p[:, 256 * q2:256 * q2 + wd] = w1[:, a0:a1]
            w1p[:, 256 * q2 + 128:256 * q2 + 128 + wd] = w1[:, 2730 + a0:2730 + a1]
            b1p[256 * q2:256 * q2 + wd] = b1[a0:a1]
            b1p[256 * q2 + 128:256 * q2 + 128 + wd] = b1[2730 + a0:2730 + a1]
        w[f"fW1{l}"] = ac(w1p)
        w[f"fB1{l}"] = ac(b1p.reshape(NW1T, 128).T)
        w2p = np.zeros((HIDP, D), np.float32)
        w2p[0:2730] = g["L0_ffn_w2"][l]
        w[f"fW2{l}"] = ac(w2p)
        w[f"fB2{l}"] = ac(g["L0_ffn_b2"][l].reshape(NFT, 128).T)
    w["gUPT"] = ac(g["L0_attn_b_up_w"].T)
    w["gGATET"] = ac(g["L0_attn_b_gate_w"].T)
    w["gDOWNT"] = ac(g["L0_attn_b_down_w"].T)
    w["gQKT"] = ac(np.concatenate([g["L0_attn_b_q_w"].T, g["L0_attn_b_k_w"].T], 1))
    w["gGTT"] = ac(np.concatenate([g["L0_attn_b_g_w"].T, g["L0_attn_b_tau_w"].T], 1))
    w["gIG1T"] = ac(g["L0_attn_b_ig_w1"].T)
    w["gIG2T"] = ac(g["L0_attn_b_ig_w2"].T)
    cw = g["L0_attn_b_conv_w"]
    cd = np.zeros((4, NGT, 128, 128), np.float32)
    for jt in range(4):
        for it in range(NGT):
            np.fill_diagonal(cd[jt, it], cw[128 * it:128 * it + 128, jt])
    w["gCONVD"] = ac(cd.transpose(2, 0, 1, 3))
    w["gCONVB"] = ac(g["L0_attn_b_conv_b"].reshape(NGT, 128).T)
    w["gGB"] = ac(g["L0_attn_b_g_b"].reshape(64, 1))
    w["gTAUB"] = ac(g["L0_attn_b_tau_b"].reshape(64, 1))
    w["gIGB2"] = ac(g["L0_attn_b_ig_b2"].reshape(NGT, 128).T)
    w["gRW"] = ac(g["L0_attn_b_residual_weight"].reshape(NGT, 128).T)
    w["gNW"] = ac(g["L0_attn_b_norm_weight"].reshape(NGT, 128).T)
    return w



class Buf:
    __slots__ = ("name", "t", "prod", "cons")

    def __init__(self, name, t):
        self.name, self.t = name, t
        self.prod = {}
        self.cons = {}


ENGFN = {"t": "tensor", "v": "vector", "s": "scalar", "g": "gpsimd", "y": "sync"}
OWN = {"t": "t", "v": "v", "s": "s", "g": "g", "y": None}
DMAS = {"g": "gd", "y": "yd", "s": "sd"}


class Pr:
    strict = True

    def __init__(self, nc, block, stack):
        self.nc, self.block, self.stack = nc, block, stack
        self.sem = {}
        self.cnt = {}
        for s in ("t", "v", "s", "g", "gd", "yd", "sd"):
            self.sem[s] = stack.enter_context(nc.semaphore(f"S{s}"))
            self.cnt[s] = 0
        self.seen = {e: dict.fromkeys(self.sem, 0) for e in ENGFN}

    skip = False

    def sec(self, e, f):
        if self.skip:
            return
        getattr(self.block, ENGFN[e])(lambda raw: f(Eng(self, e, raw)))

    def buf(self, handle, name=None):
        return Buf(name or getattr(handle, "name", "b"), handle)

    @staticmethod
    def alias(primary, handle_or_ap, name="alias"):
        b = Buf(name, handle_or_ap)
        b.prod = primary.prod
        b.cons = primary.cons
        return b

    def barrier(self):
        snap = dict(self.cnt)
        for e in ENGFN:
            def f(eng, snap=snap, e=e):
                for sn, c in snap.items():
                    if c and sn != OWN[e] and self.seen[e][sn] < c:
                        eng.raw.wait_ge(self.sem[sn], c)
                        self.seen[e][sn] = c
            self.sec(e, f)


class Eng:
    __slots__ = ("pr", "e", "raw")

    def __init__(self, pr, e, raw):
        self.pr, self.e, self.raw = pr, e, raw

    def _wait(self, need):
        for sn, c in need.items():
            if sn == OWN[self.e] and (not self.pr.strict or (self.e == "t" and _T_RELAX)):
                continue
            if sn in ("gd", "yd", "sd"):
                c = self.pr.cnt[sn]
            if self.pr.seen[self.e][sn] < c:
                self.raw.wait_ge(self.pr.sem[sn], c)
                self.pr.seen[self.e][sn] = c

    def do(self, build, reads=(), writes=(), dma=False, rw=()):
        need = {}
        for b in tuple(reads) + tuple(rw):
            for s, c in b.prod.items():
                need[s] = max(need.get(s, 0), c)
        for b in tuple(writes) + tuple(rw):
            for s, c in b.cons.items():
                need[s] = max(need.get(s, 0), c)
            for s, c in b.prod.items():
                need[s] = max(need.get(s, 0), c)
        if dma:
            sn = DMAS[self.e]
            need[sn] = max(need.get(sn, 0), self.pr.cnt[sn])
        self._wait(need)
        ins = build(self.raw)
        if dma:
            sn, inc = DMAS[self.e], 16
        else:
            sn, inc = OWN[self.e], 1
        self.pr.cnt[sn] += inc
        ins.then_inc(self.pr.sem[sn], inc)
        if not dma and (not self.pr.strict or (self.e == "t" and _T_RELAX)):
            self.pr.seen[self.e][sn] = self.pr.cnt[sn]
        cend = self.pr.cnt[sn]
        for b in tuple(writes) + tuple(rw):
            b.prod.clear()
            b.prod[sn] = cend
        for b in tuple(reads) + tuple(rw):
            b.cons[sn] = cend
        return ins


def load_w(p, dst, src_ap):
    def f(eng):
        eng.do(lambda g: g.dma_start(out=dst.t[:], in_=src_ap), writes=[dst], dma=True)
    p.sec("g", f)



def emit_ln(p, C, xap, xbufs, sq, rows, statx, statq, mb, rb, N, sq_bf16=False,
            outap=None, outbufs=None, xmap=None, xmbufs=None, xap_whole=None):
    _lv = _LN_LV
    """Emit LN for one chunk of N tokens over NFT feature tiles.

    xap(i): f32 AP [128, N] of input tile i; xbufs: dep Bufs covering them.
    sq:     scratch Buf [128, NFT, N] (f32 or bf16, matching sq_bf16).
    rows:   scratch Buf f32 [1, 3N].
    statx/statq: psum Bufs (row 0, N cols used); mb/rb: psum Bufs [128, N].
    Writes xm (x - mean, f32) via xmap(i) and (x-mean)*rstd via outap(i).
    """
    def f_sq(eng):
        if xap_whole is not None:
            eng.do(lambda s: s.activation(sq.t[:], xap_whole, AF.Square),
                   reads=xbufs, writes=[sq])
        else:
            for i in range(NFT):
                eng.do(lambda s, i=i: s.activation(sq.t[:, i, :], xap(i), AF.Square),
                       reads=xbufs, writes=[sq])
    p.sec("s", f_sq)

    onesq = C["onescolb"] if sq_bf16 else C["onescol"]
    if _lv < 2:
        return

    def f_stat(eng):
        for i in range(NFT):
            eng.do(lambda t, i=i: t.matmul(statx.t[0:1, 0:N], C["onescol"].t[:, 0:1],
                                           xap(i), start=(i == 0), stop=(i == NFT - 1)),
                   reads=list(xbufs) + [C["onescol"]], writes=[statx])
        for i in range(NFT):
            eng.do(lambda t, i=i: t.matmul(statq.t[0:1, 0:N], onesq.t[:, 0:1],
                                           sq.t[:, i, :], start=(i == 0), stop=(i == NFT - 1)),
                   reads=[sq, onesq], writes=[statq])
    p.sec("t", f_stat)

    if _lv < 3:
        return

    def f_rows1(eng):
        eng.do(lambda v: v.tensor_scalar_mul(rows.t[0:1, 0:N], statx.t[0:1, 0:N],
                                             1.0 / D), reads=[statx], writes=[rows])
        eng.do(lambda v: v.tensor_scalar(rows.t[0:1, N:2 * N], statq.t[0:1, 0:N],
                                         1.0 / D, 1e-5, ALU.mult, ALU.add),
               reads=[statq], rw=[rows])
    p.sec("v", f_rows1)

    def f_rows2(eng):
        eng.do(lambda s: s.activation(statx.t[0:1, 0:N], rows.t[0:1, 0:N],
                                      AF.Square), reads=[rows], writes=[statx])
    p.sec("s", f_rows2)

    def f_rows3(eng):
        eng.do(lambda v: v.tensor_sub(rows.t[0:1, N:2 * N], rows.t[0:1, N:2 * N],
                                      statx.t[0:1, 0:N]), reads=[statx], rw=[rows])
    p.sec("v", f_rows3)

    def f_rows4(eng):
        eng.do(lambda s: s.activation(statx.t[0:1, 0:N], rows.t[0:1, N:2 * N],
                                      AF.Ln), reads=[rows], writes=[statx])
        eng.do(lambda s: s.activation(rows.t[0:1, N:2 * N], statx.t[0:1, 0:N],
                                      AF.Exp, scale=-0.5), reads=[statx], rw=[rows])
    p.sec("s", f_rows4)

    if _lv < 4:
        return

    def f_bcast(eng):
        eng.do(lambda t: t.matmul(mb.t[:], C["onesrow"].t[0:1, :], rows.t[0:1, 0:N],
                                  start=True, stop=True),
               reads=[rows, C["onesrow"]], writes=[mb])
        eng.do(lambda t: t.matmul(rb.t[:], C["onesrow"].t[0:1, :], rows.t[0:1, N:2 * N],
                                  start=True, stop=True),
               reads=[rows, C["onesrow"]], writes=[rb])
    p.sec("t", f_bcast)

    if outap is not None:
        def f_apply(eng):
            for i in range(NFT):
                eng.do(lambda v, i=i: v.tensor_sub(xmap(i), xap(i), mb.t[:]),
                       reads=list(xbufs) + [mb], writes=list(xmbufs))
            for i in range(NFT):
                eng.do(lambda v, i=i: v.tensor_mul(outap(i), xmap(i), rb.t[:]),
                       reads=list(xmbufs) + [rb], writes=list(outbufs))
        p.sec("v", f_apply)



def st_attn_a(p, nc, C, src3, dst3, aWKT, aW1T, aW2T):
    es = ExitStack()
    sb = lambda nm, shp, dt: p.buf(es.enter_context(nc.sbuf_tensor(nm, shp, dt)), nm)
    ps = lambda nm, shp: p.buf(es.enter_context(nc.psum_tensor(nm, shp, F32)), nm)

    wkt = sb("a_wkt", [128, NFT, D], BF16)
    w1t = sb("a_w1t", [128, NFT, 64], BF16)
    w2t = sb("a_w2t", [64, D], BF16)
    load_w(p, wkt, aWKT.rearrange("(a p) m -> p a m", p=128))
    load_w(p, w1t, aW1T.rearrange("(a p) m -> p a m", p=128))
    load_w(p, w2t, aW2T)

    x = [sb(f"a_x{i}", [128, NFT, CH], F32) for i in range(2)]
    sq = sb("a_sq", [128, NFT, CH], F32)
    xm = sb("a_xm", [128, NFT, CH], F32)
    xn = sb("a_xn", [128, NFT, CH], BF16)
    kb = sb("a_kb", [128, NFT, CH], BF16)
    gg = sb("a_gg", [64, CH], BF16)
    rows = sb("a_rows", [1, 2 * CH], F32)
    xo = [sb(f"a_xo{i}", [128, NFT, CH], F32) for i in range(2)]

    statx = ps("a_statx", [1, CH])
    statq = ps("a_statq", [1, CH])
    mb = ps("a_mb", [128, CH])
    rb = ps("a_rb", [128, CH])
    kp = [ps(f"a_kp{i}", [128, CH]) for i in range(2)]
    hp = ps("a_hp", [64, CH])
    rp = [ps("a_rp0", [128, CH])] * 2

    for c in range(NCH):
        c0 = c * CH
        xc, xoc = x[c % 2], xo[c % 2]

        def f_load(eng, c0=c0, xc=xc):
            eng.do(lambda y: y.dma_start(out=xc.t[:], in_=src3[:, :, c0:c0 + CH]),
                   writes=[xc], dma=True)
        p.sec("y", f_load)

        emit_ln(p, C, lambda i, xc=xc: xc.t[:, i, :], [xc], sq, rows, statx, statq,
                mb, rb, CH, sq_bf16=False, xap_whole=xc.t[:],
                outap=lambda i: xn.t[:, i, :], outbufs=[xn],
                xmap=lambda i: xm.t[:, i, :], xmbufs=[xm])

        for m in range(NFT):
            kpp = kp[m % 2]

            def f_mm(eng, m=m, kpp=kpp):
                for k in range(NFT):
                    eng.do(lambda t, k=k, m=m: t.matmul(
                        kpp.t[:], wkt.t[:, k, 128 * m:128 * m + 128], xn.t[:, k, :],
                        start=(k == 0), stop=(k == NFT - 1)),
                        reads=[wkt, xn], writes=[kpp])
            p.sec("t", f_mm)

            def f_ev(eng, m=m, kpp=kpp):
                eng.do(lambda s: s.activation(kb.t[:, m, :], kpp.t[:], AF.Copy),
                       reads=[kpp], writes=[kb])
            p.sec("s", f_ev)

        def f_h(eng):
            for k in range(NFT):
                eng.do(lambda t, k=k: t.matmul(hp.t[:], w1t.t[:, k, :], kb.t[:, k, :],
                                               start=(k == 0), stop=(k == NFT - 1)),
                       reads=[w1t, kb], writes=[hp])
        p.sec("t", f_h)

        def f_g(eng):
            eng.do(lambda s: s.activation(gg.t[:], hp.t[:], AF.Gelu),
                   reads=[hp], writes=[gg])
        p.sec("s", f_g)

        for m in range(NFT):
            rpp = rp[m % 2]

            def f_mm2(eng, m=m, rpp=rpp):
                eng.do(lambda t: t.matmul(rpp.t[:], w2t.t[:, 128 * m:128 * m + 128],
                                          gg.t[:], start=True, stop=True),
                       reads=[w2t, gg], writes=[rpp])
            p.sec("t", f_mm2)

            def f_res(eng, m=m, rpp=rpp, xc=xc, xoc=xoc):
                eng.do(lambda v: v.scalar_tensor_tensor(
                    xoc.t[:, m, :], rpp.t[:], 1.0, xc.t[:, m, :], ALU.bypass, ALU.add),
                    reads=[rpp, xc], writes=[xoc])
            p.sec("v", f_res)

        def f_st(eng, c0=c0, xoc=xoc):
            eng.do(lambda y: y.dma_start(out=dst3[:, :, c0:c0 + CH], in_=xoc.t[:]),
                   reads=[xoc], dma=True)
        p.sec("y", f_st)

    es.close()


def st_ffn_attn(p, nc, C, l, first_ln, src3, dst3, fwl, rtab):
    es = ExitStack()
    sb = lambda nm, shp, dt: p.buf(es.enter_context(nc.sbuf_tensor(nm, shp, dt)), nm)
    ps = lambda nm, shp, dt=F32: p.buf(es.enter_context(nc.psum_tensor(nm, shp, dt)), nm)
    sfx = f"_{l}_{1 if first_ln else 0}_{p.cnt['t']}"

    wq = sb("f_wq" + sfx, [128, NFT, 1536], BF16)
    wo = sb("f_wo" + sfx, [128, 4, D], BF16)
    qsa = sb("f_qsa" + sfx, [1, 128], F32); qsb = sb("f_qsb" + sfx, [1, 128], F32)
    ksa = sb("f_ksa" + sfx, [1, 128], F32); ksb = sb("f_ksb" + sfx, [1, 128], F32)
    load_w(p, wq, fwl["WQKV"].rearrange("(a p) m -> p a m", p=128))
    load_w(p, wo, fwl["WO"].rearrange("(a p) m -> p a m", p=128))
    load_w(p, qsa, fwl["QSa"]); load_w(p, qsb, fwl["QSb"])
    load_w(p, ksa, fwl["KSa"]); load_w(p, ksb, fwl["KSb"])

    yc = [sb(f"f_yc{i}" + sfx, [128, NFT, CH], F32) for i in range(2)]
    xn = sb("f_xn" + sfx, [128, NFT, CH + 1], BF16)
    if first_ln:
        sq = sb("f_sq" + sfx, [128, NFT, CH], BF16)
        yn = sb("f_yn" + sfx, [128, NFT, CH], F32)
        lrow = sb("f_lrow" + sfx, [1, 2 * CH], F32)
    qf = sb("f_qf" + sfx, [128, 4, CH], F32)
    kraw = sb("f_kraw" + sfx, [128, 4, CH], F32)
    vb = sb("f_vb" + sfx, [128, 4, CH], BF16)
    sqq = sb("f_sqq" + sfx, [128, CH], BF16)
    qb = sb("f_qb" + sfx, [128, 4, CH], BF16)
    kf = [sb(f"f_kf{i}" + sfx, [128, 4, CH], BF16) for i in range(2)]
    vt = [sb(f"f_vt{i}" + sfx, [128, 4, 4, 128], BF16) for i in range(2)]
    rt2 = [sb(f"f_rt{i}" + sfx, [128, 4, CH], F32) for i in range(2)]
    rows = sb("f_rows" + sfx, [1, 2 * CH], F32)
    snm = sb("f_snm" + sfx, [128, 16], F32)
    Eb = sb("f_eb" + sfx, [128, 1024], BF16)
    Et = sb("f_et" + sfx, [128, 640], BF16)
    Ob = sb("f_ob" + sfx, [128, 4, CH], BF16)
    yo = sb("f_yo" + sfx, [128, NFT, CH], F32)
    qn = sb("f_qn" + sfx, [128, CH], F32)
    sh = sb("f_sh" + sfx, [128, CH], F32)
    t1 = sb("f_t1" + sfx, [128, CH], F32)

    ps_A = ps("f_pA" + sfx, [128, 2 * CH])
    ps_B = ps("f_pB" + sfx, [128, 2 * CH])
    ps_c = ps("f_pc" + sfx, [128, CH])
    ps_d = ps("f_pd" + sfx, [128, CH])
    ps_g = ps("f_pg" + sfx, [128, CH])
    ps_h = ps("f_ph" + sfx, [128, 1024], BF16)

    swapmask = [i + 1 if i % 2 == 0 else i - 1 for i in range(32)]
    LN8 = float(np.log(8.0))

    def f_z(eng):
        eng.do(lambda v: v.memset(xn.t[:, :, 0:1], 0.0), writes=[xn])
    p.sec("v", f_z)

    for c in range(NCH):
        c0 = c * CH
        ycc = yc[c % 2]
        kf_c, kf_p = kf[c % 2], kf[(c + 1) % 2]
        vt_c, vt_p = vt[c % 2], vt[(c + 1) % 2]
        rtc = rt2[c % 2]

        def f_load(eng, c0=c0, ycc=ycc, rtc=rtc):
            eng.do(lambda y: y.dma_start(out=ycc.t[:], in_=src3[:, :, c0:c0 + CH]),
                   writes=[ycc], dma=True)
            eng.do(lambda y: y.dma_start(
                out=rtc.t[:], in_=rtab.rearrange("a p n -> p a n")[:, :, c0:c0 + CH]),
                writes=[rtc], dma=True)
        p.sec("y", f_load)

        if first_ln:
            emit_ln(p, C, lambda i, ycc=ycc: ycc.t[:, i, :], [ycc], sq, lrow,
                    ps_c, ps_d, ps_c, ps_d, CH, sq_bf16=True, xap_whole=ycc.t[:],
                    outap=lambda i: yn.t[:, i, :], outbufs=[yn],
                    xmap=lambda i, ycc=ycc: ycc.t[:, i, :], xmbufs=[ycc])

            def f_xn(eng):
                for i in range(NFT):
                    eng.do(lambda s, i=i: s.activation(xn.t[:, i, 1:CH + 1],
                                                       yn.t[:, i, :], AF.Copy),
                           reads=[yn], writes=[xn])
            p.sec("s", f_xn)
            base = yn
        else:
            def f_xn(eng, ycc=ycc):
                for i in range(NFT):
                    eng.do(lambda s, i=i: s.activation(xn.t[:, i, 1:CH + 1],
                                                       ycc.t[:, i, :], AF.Copy),
                           reads=[ycc], writes=[xn])
            p.sec("s", f_xn)
            base = ycc

        for m in range(12):
            pp = ps_A if m % 2 == 0 else ps_B

            def f_mm(eng, m=m, pp=pp):
                for k in range(NFT):
                    off = 1 if k < 4 else 0
                    eng.do(lambda t, k=k, m=m, off=off: t.matmul(
                        pp.t[:, 0:CH], wq.t[:, k, 128 * m:128 * m + 128],
                        xn.t[:, k, off:off + CH],
                        start=(k == 0), stop=(k == NFT - 1)),
                        reads=[wq, xn], writes=[pp])
            p.sec("t", f_mm)

            if m < 4:
                def f_ev(eng, m=m, pp=pp):
                    eng.do(lambda s, m=m: s.activation(qf.t[:, m, :], pp.t[:, 0:CH],
                                                       AF.Copy),
                           reads=[pp], writes=[qf])
                p.sec("s", f_ev)
            elif m < 8:
                def f_ev(eng, m=m, pp=pp):
                    eng.do(lambda v, m=m: v.tensor_copy(kraw.t[:, m - 4, :],
                                                        pp.t[:, 0:CH]),
                           reads=[pp], writes=[kraw])
                p.sec("v", f_ev)
            else:
                def f_ev(eng, m=m, pp=pp):
                    eng.do(lambda s, m=m: s.activation(vb.t[:, m - 8, :],
                                                       pp.t[:, 0:CH], AF.Copy),
                           reads=[pp], writes=[vb])
                p.sec("s", f_ev)

        if c == _DBG_CHUNK:
            def f_dbg2(eng):
                dbg = p.dbg
                eng.do(lambda g: g.dma_start(out=dbg[1, :, 0:CH],
                                             in_=kraw.t[:, 0, :]),
                       reads=[kraw], dma=True)
            p.sec("g", f_dbg2)

        def f_halo(eng):
            eng.do(lambda v: v.tensor_copy(xn.t[:, :, 0:1], xn.t[:, :, CH:CH + 1]),
                   rw=[xn])
        p.sec("v", f_halo)

        for pr in range(4):
            for which in range(2):
                srcb = qf if which == 0 else kraw
                dstb = qb if which == 0 else kf_c
                sa_, sb_ = (qsa, qsb) if which == 0 else (ksa, ksb)
                bias = LN8 if which == 0 else 0.0
                ct, st_ = (0, 1) if which == 0 else (2, 3)

                def f_sq2(eng, pr=pr, srcb=srcb):
                    eng.do(lambda s: s.activation(sqq.t[:], srcb.t[:, pr, :], AF.Square),
                           reads=[srcb], writes=[sqq])
                p.sec("s", f_sq2)

                def f_ls(eng):
                    eng.do(lambda t: t.matmul(ps_c.t[0:1, :], C["halfselb"].t[:, 0:1],
                                              sqq.t[:], start=True, stop=True),
                           reads=[sqq, C["halfselb"]], writes=[ps_c])
                    eng.do(lambda t: t.matmul(ps_d.t[0:1, :], C["halfselb"].t[:, 1:2],
                                              sqq.t[:], start=True, stop=True),
                           reads=[sqq, C["halfselb"]], writes=[ps_d])
                p.sec("t", f_ls)

                def f_rw(eng):
                    eng.do(lambda s: s.activation(rows.t[0:1, 0:CH], ps_c.t[0:1, :],
                                                  AF.Ln), reads=[ps_c], writes=[rows])
                    eng.do(lambda s: s.activation(rows.t[0:1, CH:2 * CH], ps_d.t[0:1, :],
                                                  AF.Ln), reads=[ps_d], rw=[rows])
                    eng.do(lambda s: s.activation(rows.t[0:1, :], rows.t[0:1, :],
                                                  AF.Exp, scale=-0.5), rw=[rows])
                p.sec("s", f_rw)

                def f_cap(eng, which=which):
                    if which == 0:
                        eng.do(lambda v: v.tensor_scalar(rows.t[0:1, :], rows.t[0:1, :],
                                                         8.0, 8e12, ALU.mult, ALU.min),
                               rw=[rows])
                    else:
                        eng.do(lambda v: v.tensor_scalar_min(rows.t[0:1, :],
                                                             rows.t[0:1, :], 1e12),
                               rw=[rows])
                p.sec("v", f_cap)

                def f_qm(eng, sa_=sa_, sb_=sb_):
                    eng.do(lambda t: t.matmul(ps_g.t[:], sa_.t[0:1, :],
                                              rows.t[0:1, 0:CH], start=True, stop=False),
                           reads=[rows, sa_], writes=[ps_g])
                    eng.do(lambda t: t.matmul(ps_g.t[:], sb_.t[0:1, :],
                                              rows.t[0:1, CH:2 * CH], start=False,
                                              stop=True),
                           reads=[rows, sb_], writes=[ps_g])
                p.sec("t", f_qm)

                def f_rot(eng, pr=pr, srcb=srcb, dstb=dstb, ct=ct, st_=st_, rtc=rtc):
                    eng.do(lambda v: v.tensor_mul(qn.t[:], srcb.t[:, pr, :], ps_g.t[:]),
                           reads=[srcb, ps_g], writes=[qn])
                    eng.do(lambda v: v.stream_shuffle(sh.t[:], qn.t[:], swapmask),
                           reads=[qn], writes=[sh])
                    eng.do(lambda v, ct=ct: v.tensor_mul(t1.t[:], qn.t[:],
                                                         rtc.t[:, ct, :]),
                           reads=[qn, rtc], writes=[t1])
                    eng.do(lambda v, st_=st_: v.tensor_mul(sh.t[:], sh.t[:],
                                                           rtc.t[:, st_, :]),
                           reads=[rtc], rw=[sh])
                    eng.do(lambda v, pr=pr: v.tensor_add(dstb.t[:, pr, :], t1.t[:],
                                                         sh.t[:]),
                           reads=[t1, sh], writes=[dstb])
                p.sec("v", f_rot)

        for tt in range(4):
            def f_tr(eng, tt=tt):
                for pr in range(4):
                    eng.do(lambda t, pr=pr, tt=tt: t.transpose(
                        ps_h.t[:, 128 * pr:128 * pr + 128],
                        vb.t[:, pr, 128 * tt:128 * tt + 128], C["ident"].t[:]),
                        reads=[vb, C["ident"]], writes=[ps_h])
            p.sec("t", f_tr)

            def f_tre(eng, tt=tt, vt_c=vt_c):
                eng.do(lambda s, tt=tt: s.activation(vt_c.t[:, tt, :, :],
                                                     ps_h.t[:, 0:512], AF.Copy),
                       reads=[ps_h], writes=[vt_c])
            p.sec("s", f_tre)

        for pr in range(4):
            for r in range(4):
                P = (4 - r) * 128 if c > 0 else 0
                W = (r + 1) * 128
                lo = 0 if P else 512
                hi = 512 + W
                for h in range(2):
                    hb = 64 * h
                    S = ps_A if h == 0 else ps_B
                    col = 8 * h

                    def f_s(eng, pr=pr, r=r, h=h, hb=hb, S=S, P=P, W=W,
                            kf_c=kf_c, kf_p=kf_p):
                        if P:
                            eng.do(lambda t: t.matmul(
                                S.t[:, 0:P], qb.t[hb:hb + 64, pr, 128 * r:128 * r + 128],
                                kf_p.t[hb:hb + 64, pr, 128 * r:CH],
                                start=True, stop=True, tile_position=(hb, 0)),
                                reads=[qb, kf_p], writes=[S])
                        eng.do(lambda t, W=W: t.matmul(
                            S.t[:, 512:512 + W],
                            qb.t[hb:hb + 64, pr, 128 * r:128 * r + 128],
                            kf_c.t[hb:hb + 64, pr, 0:W],
                            start=True, stop=True, tile_position=(hb, 0)),
                            reads=[qb, kf_c], writes=[S])
                    p.sec("t", f_s)

                    def f_m(eng, S=S, P=P, W=W, col=col, lo=lo, hi=hi):
                        if P and P < 512:
                            eng.do(lambda v: v.memset(S.t[:, P:512], NEG), rw=[S])
                        if P:
                            eng.do(lambda v: v.tensor_add(S.t[:, 0:128], S.t[:, 0:128],
                                                          C["mprev"].t[:]),
                                   reads=[C["mprev"]], rw=[S])
                        eng.do(lambda v, W=W: v.tensor_add(
                            S.t[:, 512 + W - 128:512 + W],
                            S.t[:, 512 + W - 128:512 + W], C["mdiag"].t[:]),
                            reads=[C["mdiag"]], rw=[S])
                        eng.do(lambda v: v.reduce_max(snm.t[:, col:col + 1],
                                                      S.t[:, lo:hi], axis=AX.X,
                                                      negate=True),
                               reads=[S], writes=[snm])
                    p.sec("v", f_m)

                    def f_e(eng, S=S, col=col, lo=lo, hi=hi):
                        eng.do(lambda s: s.activation(
                            Eb.t[:, lo:hi], S.t[:, lo:hi], AF.Exp,
                            bias=snm.t[:, col:col + 1],
                            accum_out=snm.t[:, col + 5:col + 6]),
                            reads=[S, snm], writes=[Eb], rw=[snm])
                    p.sec("s", f_e)

                    def f_n(eng, col=col, lo=lo, hi=hi):
                        eng.do(lambda v: v.reciprocal(snm.t[:, col + 6:col + 7],
                                                      snm.t[:, col + 5:col + 6]),
                               rw=[snm])
                        eng.do(lambda v: v.tensor_scalar_mul(
                            Eb.t[:, lo:hi], Eb.t[:, lo:hi],
                            snm.t[:, col + 6:col + 7]),
                            reads=[snm], rw=[Eb])
                    p.sec("v", f_n)

                    np_ = P // 128
                    nw = W // 128
                    ns = np_ + nw
                    esrc = [128 * s2 for s2 in range(np_)] +                            [512 + 128 * s2 for s2 in range(nw)]

                    def f_t2(eng, ns=ns, esrc=esrc):
                        for s2 in range(ns):
                            eng.do(lambda t, s2=s2: t.transpose(
                                ps_h.t[:, 128 * s2:128 * s2 + 128],
                                Eb.t[:, esrc[s2]:esrc[s2] + 128], C["ident"].t[:]),
                                reads=[Eb, C["ident"]], writes=[ps_h])
                    p.sec("t", f_t2)

                    def f_te(eng, ns=ns):
                        eng.do(lambda s, ns=ns: s.activation(Et.t[:, 0:128 * ns],
                                                             ps_h.t[:, 0:128 * ns],
                                                             AF.Copy),
                               reads=[ps_h], writes=[Et])
                    p.sec("s", f_te)

                    def f_pv(eng, pr=pr, r=r, h=h, hb=hb, ns=ns, np_=np_,
                             vt_c=vt_c, vt_p=vt_p):
                        for s2 in range(ns):
                            if s2 < np_:
                                vsl = vt_p
                                tt = r + s2
                            else:
                                vsl = vt_c
                                tt = s2 - np_
                            oslc = (ps_g.t[0:64, 0:128] if h == 0
                                    else ps_g.t[64:128, 0:128])
                            eng.do(lambda t, s2=s2, vsl=vsl, tt=tt, oslc=oslc: t.matmul(
                                oslc, vsl.t[:, tt, pr, 64 * h:64 * h + 64],
                                Et.t[:, 128 * s2:128 * s2 + 128],
                                start=(s2 == 0), stop=(s2 == ns - 1),
                                tile_position=(0, 64 * h)),
                                reads=[vsl, Et], writes=[ps_g])
                    p.sec("t", f_pv)

                if c == _DBG_CHUNK and False and pr == 0 and r == 2:
                    def f_dba(eng):
                        dbg = p.dbg
                        eng.do(lambda g: g.dma_start(out=dbg[4, :, 0:640], in_=Eb.t[:]),
                               reads=[Eb], dma=True)
                        eng.do(lambda g: g.dma_start(out=dbg[5, :, 0:640], in_=Et.t[:]),
                               reads=[Et], dma=True)
                        eng.do(lambda g: g.dma_start(out=dbg[3, :, 520:536],
                                                     in_=snm.t[:]),
                               reads=[snm], dma=True)
                    p.sec("g", f_dba)

                def f_oe(eng, pr=pr, r=r):
                    eng.do(lambda s: s.activation(Ob.t[:, pr, 128 * r:128 * r + 128],
                                                  ps_g.t[:, 0:128], AF.Copy),
                           reads=[ps_g], writes=[Ob])
                p.sec("s", f_oe)

        if c == _DBG_CHUNK:
            def f_dbg(eng, c0=c0, ycc=ycc, kf_c=kf_c):
                dbg = p.dbg
                eng.do(lambda g: g.dma_start(out=dbg[0, :, 0:CH], in_=qb.t[:, 0, :]),
                       reads=[qb], dma=True)
                eng.do(lambda g: g.dma_start(out=dbg[1, :, 0:CH], in_=kf_c.t[:, 0, :]),
                       reads=[kf_c], dma=True)
                eng.do(lambda g: g.dma_start(out=dbg[2, :, 0:CH], in_=vb.t[:, 0, :]),
                       reads=[vb], dma=True)
                eng.do(lambda g: g.dma_start(out=dbg[3, :, 0:CH + 1],
                                             in_=xn.t[:, 0, :]),
                       reads=[xn], dma=True)
                eng.do(lambda g: g.dma_start(out=dbg[6, :, 0:CH], in_=Ob.t[:, 0, :]),
                       reads=[Ob], dma=True)
                eng.do(lambda g: g.dma_start(out=dbg[7, :, 0:CH],
                                             in_=(yn if first_ln else ycc).t[:, 0, :]),
                       reads=[(yn if first_ln else ycc)], dma=True)
                eng.do(lambda g: g.dma_start(out=dbg[4, :, 0:CH], in_=qf.t[:, 0, :]),
                       reads=[qf], dma=True)
                eng.do(lambda g: g.dma_start(out=dbg[5, :, 0:CH], in_=kraw.t[:, 0, :]),
                       reads=[kraw], dma=True)
            p.sec("g", f_dbg)

        for m in range(NFT):
            pp = ps_c if m % 2 == 0 else ps_d

            def f_wo(eng, m=m, pp=pp):
                for k in range(4):
                    eng.do(lambda t, k=k, m=m: t.matmul(
                        pp.t[:], wo.t[:, k, 128 * m:128 * m + 128], Ob.t[:, k, :],
                        start=(k == 0), stop=(k == 3)),
                        reads=[wo, Ob], writes=[pp])
            p.sec("t", f_wo)

            def f_res(eng, m=m, pp=pp, base=base):
                eng.do(lambda v: v.scalar_tensor_tensor(
                    yo.t[:, m, :], pp.t[:], 1.0, base.t[:, m, :], ALU.bypass, ALU.add),
                    reads=[pp, base], writes=[yo])
            p.sec("v", f_res)

        def f_st(eng, c0=c0):
            eng.do(lambda y: y.dma_start(out=dst3[:, :, c0:c0 + CH], in_=yo.t[:]),
                   reads=[yo], dma=True)
        p.sec("y", f_st)

    es.close()




def st_ffn_mlp(p, nc, C, l, src3, dst3, merge_src3, fwl):
    es = ExitStack()
    sb = lambda nm, shp, dt: p.buf(es.enter_context(nc.sbuf_tensor(nm, shp, dt)), nm)
    ps = lambda nm, shp: p.buf(es.enter_context(nc.psum_tensor(nm, shp, F32)), nm)
    sfx = f"_m{l}_{p.cnt['t']}"

    w1 = sb("m_w1" + sfx, [128, NFT, 2 * HIDP], BF16)
    w2 = sb("m_w2" + sfx, [128, NW2K, D], BF16)
    b1 = sb("m_b1" + sfx, [128, NW1T], F32)
    b2 = sb("m_b2" + sfx, [128, NFT], F32)
    lg = sb("m_lg" + sfx, [128, NFT], F32)
    lb = sb("m_lb" + sfx, [128, NFT], F32)
    load_w(p, w1, fwl["W1"].rearrange("(a p) m -> p a m", p=128))
    load_w(p, w2, fwl["W2"].rearrange("(a p) m -> p a m", p=128))
    load_w(p, b1, fwl["B1"]); load_w(p, b2, fwl["B2"])
    load_w(p, lg, fwl["LG"]); load_w(p, lb, fwl["LB"])

    hb = sb("m_hb" + sfx, [128, NFT, CH + 1], F32)
    sq = sb("m_sq" + sfx, [128, NFT, CH], BF16)
    hnb = sq
    yo = sb("m_yo" + sfx, [128, NFT, CH], F32)
    prod = sb("m_prod" + sfx, [128, NW2K, CH], BF16)
    gel = sb("m_gel" + sfx, [128, CH], BF16)
    rows = sb("m_rows" + sfx, [1, 2 * CH], F32)

    P = [ps(f"m_p{i}" + sfx, [128, CH]) for i in range(8)]
    statx, statq, mb, rb = P[0], P[1], P[2], P[3]
    if merge_src3 is not None:
        mrgt = [sb(f"m_mrg{i}" + sfx, [128, CH], F32) for i in range(2)]

    def f_z(eng):
        eng.do(lambda v: v.memset(hb.t[:, :, 0:1], 0.0), writes=[hb])
    p.sec("v", f_z)

    for c in range(NCH):
        c0 = c * CH

        def f_load(eng, c0=c0):
            eng.do(lambda y: y.dma_start(out=hb.t[:, :, 1:CH + 1],
                                         in_=src3[:, :, c0:c0 + CH]),
                   writes=[hb], dma=True)
        p.sec("y", f_load)

        def shap(i):
            return hb.t[:, i, 1:CH + 1] if i < 4 else hb.t[:, i, 0:CH]

        emit_ln(p, C, shap, [hb], sq, rows, statx, statq, mb, rb, CH, sq_bf16=True,
                outap=None)
        p.skip = _GLA_OP < 2

        def f_ap(eng):
            for i in range(NFT):
                eng.do(lambda v, i=i: v.tensor_sub(yo.t[:, i, :], shap(i), mb.t[:]),
                       reads=[hb, mb], writes=[yo])
            for i in range(NFT):
                eng.do(lambda v, i=i: v.tensor_mul(yo.t[:, i, :], yo.t[:, i, :],
                                                   rb.t[:]), reads=[rb], rw=[yo])
            for i in range(NFT):
                eng.do(lambda v, i=i: v.tensor_scalar(
                    hnb.t[:, i, :], yo.t[:, i, :], lg.t[:, i:i + 1], lb.t[:, i:i + 1],
                    ALU.mult, ALU.add), reads=[yo, lg, lb], writes=[hnb])
        p.sec("v", f_ap)

        def f_halo(eng):
            eng.do(lambda v: v.tensor_copy(hb.t[:, :, 0:1], hb.t[:, :, CH:CH + 1]),
                   rw=[hb])
        p.sec("v", f_halo)

        for j in range(NW2K):
            pa = P[4 + (2 * j) % 4]
            pg = P[4 + (2 * j + 1) % 4]

            def f_w1(eng, j=j, pa=pa, pg=pg):
                for k in range(NFT):
                    eng.do(lambda t, k=k, j=j: t.matmul(
                        pa.t[:], w1.t[:, k, 256 * j:256 * j + 128], hnb.t[:, k, :],
                        start=(k == 0), stop=(k == NFT - 1)),
                        reads=[w1, hnb], writes=[pa])
                for k in range(NFT):
                    eng.do(lambda t, k=k, j=j: t.matmul(
                        pg.t[:], w1.t[:, k, 256 * j + 128:256 * j + 256], hnb.t[:, k, :],
                        start=(k == 0), stop=(k == NFT - 1)),
                        reads=[w1, hnb], writes=[pg])
            p.sec("t", f_w1)

            def f_gel(eng, j=j, pg=pg):
                eng.do(lambda s, j=j: s.activation(gel.t[:], pg.t[:], AF.Gelu,
                                                   bias=b1.t[:, 2 * j + 1:2 * j + 2]),
                       reads=[pg, b1], writes=[gel])
            p.sec("s", f_gel)

            def f_prod(eng, j=j, pa=pa):
                eng.do(lambda v, j=j: v.scalar_tensor_tensor(
                    prod.t[:, j, :], pa.t[:], b1.t[:, 2 * j:2 * j + 1], gel.t[:],
                    ALU.add, ALU.mult), reads=[pa, gel, b1], writes=[prod])
            p.sec("v", f_prod)

        for m in range(NFT):
            pp = P[m % 2]

            def f_w2(eng, m=m, pp=pp):
                for k in range(NW2K):
                    eng.do(lambda t, k=k, m=m: t.matmul(
                        pp.t[:], w2.t[:, k, 128 * m:128 * m + 128], prod.t[:, k, :],
                        start=(k == 0), stop=(k == NW2K - 1)),
                        reads=[w2, prod], writes=[pp])
            p.sec("t", f_w2)

            if merge_src3 is not None:
                def f_ml(eng, m=m, c0=c0):
                    eng.do(lambda y, m=m: y.dma_start(
                        out=mrgt[m % 2].t[:], in_=merge_src3[:, m, c0:c0 + CH]),
                        writes=[mrgt[m % 2]], dma=True)
                p.sec("y", f_ml)

            def f_res(eng, m=m, pp=pp):
                eng.do(lambda v, m=m: v.scalar_tensor_tensor(
                    yo.t[:, m, :], pp.t[:], b2.t[:, m:m + 1], hb.t[:, m, 1:CH + 1],
                    ALU.add, ALU.add), reads=[pp, b2, hb], writes=[yo])
                if merge_src3 is not None:
                    eng.do(lambda v, m=m: v.tensor_add(yo.t[:, m, :], yo.t[:, m, :],
                                                       mrgt[m % 2].t[:]),
                           reads=[mrgt[m % 2]], rw=[yo])
            p.sec("v", f_res)

        def f_st(eng, c0=c0):
            eng.do(lambda y: y.dma_start(out=dst3[:, :, c0:c0 + CH], in_=yo.t[:]),
                   reads=[yo], dma=True)
        p.sec("y", f_st)

    es.close()


def st_gla(p, nc, C, src3, dst3, gw):
    es = ExitStack()
    sb = lambda nm, shp, dt: p.buf(es.enter_context(nc.sbuf_tensor(nm, shp, dt)), nm)

    upt = sb("g_upt", [128, NFT, GD], BF16)
    gatet = sb("g_gatet", [128, NFT, GD], BF16)
    downt = sb("g_downt", [128, NGT, D], BF16)
    qkt = sb("g_qkt", [128, NGT, 128], BF16)
    gtt = sb("g_gtt", [128, NGT, 128], BF16)
    ig1t = sb("g_ig1t", [128, NGT, 16], BF16)
    ig2t = sb("g_ig2t", [16, GD], BF16)
    convd = sb("g_convd", [128, 4, NGT, 128], BF16)
    convb = sb("g_convb", [128, NGT], F32)
    gbv = sb("g_gb", [64, 1], F32)
    taubv = sb("g_taub", [64, 1], F32)
    igb2 = sb("g_igb2", [128, NGT], F32)
    rwv = sb("g_rw", [128, NGT], F32)
    nwv = sb("g_nw", [128, NGT], F32)
    load_w(p, upt, gw["UPT"].rearrange("(a p) m -> p a m", p=128))
    load_w(p, gatet, gw["GATET"].rearrange("(a p) m -> p a m", p=128))
    load_w(p, downt, gw["DOWNT"].rearrange("(a p) m -> p a m", p=128))
    load_w(p, qkt, gw["QKT"].rearrange("(a p) m -> p a m", p=128))
    load_w(p, gtt, gw["GTT"].rearrange("(a p) m -> p a m", p=128))
    load_w(p, ig1t, gw["IG1T"].rearrange("(a p) m -> p a m", p=128))
    load_w(p, ig2t, gw["IG2T"])
    load_w(p, convd, gw["CONVD"])
    load_w(p, convb, gw["CONVB"])
    load_w(p, gbv, gw["GB"])
    load_w(p, taubv, gw["TAUB"])
    load_w(p, igb2, gw["IGB2"])
    load_w(p, rwv, gw["RW"])
    load_w(p, nwv, gw["NW"])

    if _GLA_NOUNION:
        x2c = sb("g_x2c", [128, 8, GCH], F32)
        xn = sb("g_xnr", [128, 8, GCH], BF16)
        xgn = sb("g_xgnr", [128, 16, GCH], BF16)
        xo = sb("g_xor", [128, 8, GCH], F32)
        sq2 = sb("g_sq2r", [128, 16, GIC], BF16)
        o2 = sb("g_o2r", [128, 16, GIC], F32)
        usb = sb("g_usbr", [64, GCH], F32)
        t2g = sb("g_t2gr", [64, GCH], F32)
        ggs = sb("g_ggsr", [64, GCH], F32)
        taus = sb("g_tausr", [64, GCH], F32)
        its = sb("g_itsr", [64, GCH], F32)
        kg = sb("g_kgr", [64, GCH], F32)
    else:
        UNA = sb("g_una", [128, 6144], BF16)
        x2c = Pr.alias(UNA, UNA.t[:][:, 0:4096].bitcast(F32)
                       .rearrange("p (a n) -> p a n", a=8), "x2c")
        xn = Pr.alias(UNA, UNA.t[:][:, 4096:6144]
                      .rearrange("p (a n) -> p a n", a=8), "xn")
        xgn = Pr.alias(UNA, UNA.t[:][:, 0:4096]
                       .rearrange("p (a n) -> p a n", a=16), "xgn")
        UNB = sb("g_unb", [128, 4096], BF16)
        xo = Pr.alias(UNB, UNB.t[:].bitcast(F32).rearrange("p (a n) -> p a n", a=8),
                      "xo")
        sq2 = Pr.alias(UNB, UNB.t[:][:, 0:2048].rearrange("p (a n) -> p a n", a=16),
                       "sq2")
        UNC = sb("g_unc", [128, 4096], BF16)
        o2 = Pr.alias(UNC, UNC.t[:].bitcast(F32).rearrange("p (a n) -> p a n", a=16),
                      "o2")
        _sm = UNC.t[:][:, 0:3072].bitcast(F32)
        usb = Pr.alias(UNC, _sm[0:64, 0:256], "usb")
        t2g = Pr.alias(UNC, _sm[0:64, 256:512], "t2g")
        ggs = Pr.alias(UNC, _sm[0:64, 512:768], "ggs")
        taus = Pr.alias(UNC, _sm[0:64, 768:1024], "taus")
        its = Pr.alias(UNC, _sm[0:64, 1024:1280], "its")
        kg = Pr.alias(UNC, _sm[0:64, 1280:1536], "kg")

    grow = sb("g_grow", [1, 2 * GCH], F32)
    xup = sb("g_xup", [128, NGT, GCH + 3], BF16)
    fgb = sb("g_fgb", [128, NGT, GCH], BF16)
    shiftb = sb("g_shiftb", [128, NGT, GCH], BF16)
    vbg = sb("g_vbg", [128, NGT, GCH], BF16)
    sgt = sb("g_sgt", [128, GCH], BF16)
    qf32 = sb("g_qf32", [64, GCH], F32)
    knf = sb("g_knf", [64, GCH], BF16)
    sqk = Pr.alias(knf, knf.t[:], "sqk")
    rts = sb("g_rts", [64, GCH], F32)
    ig1b = sb("g_ig1b", [16, GCH], BF16)
    S_sb = sb("g_S", [64, GD], F32)
    S_bf = sb("g_Sbf", [64, GD], BF16)
    if _GLA_NOUNION:
        vt_sb = sb("g_vtr", [128, GD], BF16)
    else:
        vt_sb = Pr.alias(UNB, UNB.t[:][:, 2048:4096], "vt")
    bsc = sb("g_bsc", [64, GIC], F32)
    tm1 = sb("g_tm1", [64, GIC], F32)
    ebx = sb("g_ebx", [64, GIC], F32)
    ebt = sb("g_ebt", [64, 1], F32)
    qd_b = sb("g_qd", [64, GIC], BF16)
    kd_b = sb("g_kd", [64, GIC], BF16)
    kdec_b = sb("g_kdec", [64, GIC], BF16)
    qdec_b = sb("g_qdec", [64, GIC], BF16)
    am_sb = sb("g_am", [GIC, GIC], BF16)
    kdt_sb = sb("g_kdt", [GIC, 64], BF16)
    gr2 = Pr.alias(grow, grow.t[:][0:1, GCH:2 * GCH], "gr2")

    def f_init(eng):
        eng.do(lambda v: v.memset(S_sb.t[:], 0.0), writes=[S_sb])
        eng.do(lambda v: v.memset(S_bf.t[:], 0.0), writes=[S_bf])
        eng.do(lambda v: v.memset(xup.t[:, :, 0:3], 0.0), writes=[xup])
    p.sec("v", f_init)

    if _GLA_NOUNION:
        resv = o2.t[:].rearrange("p a n -> p (a n)").bitcast(F32)
    else:
        resv = UNC.t[:].bitcast(F32)
    resv = resv.rearrange("p (a n) -> p a n", a=NFT)
    resb = Pr.alias(o2, resv, "resv")

    _n = NGCH if _GLA_LIMIT is None else _GLA_LIMIT
    for c in range(_n):
        c0 = c * GCH
        po = ExitStack()
        pso = lambda nm: p.buf(po.enter_context(
            nc.psum_tensor(nm + f"_{c}", [128, GCH], F32)), nm)
        statx, statq = pso("go_sx"), pso("go_sq")
        mbp, rbp = pso("go_mb"), pso("go_rb")
        pA, pB = pso("go_pA"), pso("go_pB")
        pQ, pG = pso("go_pQ"), pso("go_pG")

        def f_load(eng, c0=c0):
            eng.do(lambda y: y.dma_start(out=x2c.t[:], in_=src3[:, :, c0:c0 + GCH]),
                   writes=[x2c], dma=True)
        p.sec("y", f_load)

        p.skip = _GLA_OP < 1
        emit_ln(p, C, lambda i: x2c.t[:, i, :], [x2c], xn, grow, statx, statq,
                mbp, rbp, GCH, sq_bf16=True, outap=None, xap_whole=x2c.t[:])

        p.skip = _GLA_OP < 2

        def f_ap(eng):
            for i in range(NFT):
                eng.do(lambda v, i=i: v.tensor_sub(xo.t[:, i, :], x2c.t[:, i, :],
                                                   mbp.t[:]),
                       reads=[x2c, mbp], writes=[xo])
            for i in range(NFT):
                eng.do(lambda v, i=i: v.tensor_mul(xn.t[:, i, :], xo.t[:, i, :],
                                                   rbp.t[:]),
                       reads=[xo, rbp], writes=[xn])
        p.sec("v", f_ap)

        p.skip = _GLA_OP < 3
        for m in range(NGT):
            pp = pA if m % 2 == 0 else pB

            def f_up(eng, m=m, pp=pp):
                for k in range(NFT):
                    eng.do(lambda t, k=k, m=m: t.matmul(
                        pp.t[:], upt.t[:, k, 128 * m:128 * m + 128], xn.t[:, k, :],
                        start=(k == 0), stop=(k == NFT - 1)),
                        reads=[upt, xn], writes=[pp])
            p.sec("t", f_up)

            def f_upe(eng, m=m, pp=pp):
                eng.do(lambda s, m=m: s.activation(xup.t[:, m, 3:3 + GCH], pp.t[:],
                                                   AF.Copy),
                       reads=[pp], writes=[xup])
            p.sec("s", f_upe)
        for m in range(NGT):
            pp = pA if m % 2 == 0 else pB

            def f_ga(eng, m=m, pp=pp):
                for k in range(NFT):
                    eng.do(lambda t, k=k, m=m: t.matmul(
                        pp.t[:], gatet.t[:, k, 128 * m:128 * m + 128], xn.t[:, k, :],
                        start=(k == 0), stop=(k == NFT - 1)),
                        reads=[gatet, xn], writes=[pp])
            p.sec("t", f_ga)

            def f_gae(eng, m=m, pp=pp):
                eng.do(lambda s, m=m: s.activation(fgb.t[:, m, :], pp.t[:], AF.Silu),
                       reads=[pp], writes=[fgb])
            p.sec("s", f_gae)

        p.skip = _GLA_OP < 4
        for m in range(NGT):
            pp = pQ if m % 2 == 0 else pG

            def f_cv(eng, m=m, pp=pp):
                for j in range(4):
                    eng.do(lambda t, j=j, m=m: t.matmul(
                        pp.t[:], convd.t[:, j, m, :], xup.t[:, m, j:j + GCH],
                        start=(j == 0), stop=(j == 3)),
                        reads=[convd, xup], writes=[pp])
            p.sec("t", f_cv)

            def f_cve(eng, m=m, pp=pp):
                eng.do(lambda s, m=m: s.activation(shiftb.t[:, m, :], pp.t[:],
                                                   AF.Silu, bias=convb.t[:, m:m + 1]),
                       reads=[pp, convb], writes=[shiftb])
            p.sec("s", f_cve)

        p.skip = _GLA_OP < 5
        def f_qk(eng):
            for k in range(NGT):
                eng.do(lambda t, k=k: t.matmul(pQ.t[:], qkt.t[:, k, :],
                                               shiftb.t[:, k, :],
                                               start=(k == 0), stop=(k == NGT - 1)),
                       reads=[qkt, shiftb], writes=[pQ])
            for k in range(NGT):
                eng.do(lambda t, k=k: t.matmul(pG.t[:], gtt.t[:, k, :],
                                               shiftb.t[:, k, :],
                                               start=(k == 0), stop=(k == NGT - 1)),
                       reads=[gtt, shiftb], writes=[pG])
        p.sec("t", f_qk)

        def f_qke(eng):
            eng.do(lambda s: s.activation(qf32.t[:], pQ.t[0:64, :], AF.Copy,
                                          scale=0.125), reads=[pQ], writes=[qf32])
            eng.do(lambda s: s.activation(kg.t, pQ.t[64:128, :], AF.Copy),
                   reads=[pQ], writes=[kg])
            eng.do(lambda s: s.activation(taus.t, pG.t[64:128, :], AF.Sigmoid,
                                          bias=taubv.t[:, 0:1]),
                   reads=[pG, taubv], writes=[taus])
        p.sec("s", f_qke)

        def f_qke2(eng):
            eng.do(lambda v: v.tensor_scalar_add(usb.t, pG.t[0:64, :],
                                                 gbv.t[:, 0:1]),
                   reads=[pG, gbv], writes=[usb])
        p.sec("v", f_qke2)

        p.skip = _GLA_OP < 6
        def f_ig1(eng):
            for k in range(NGT):
                eng.do(lambda t, k=k: t.matmul(statq.t[0:16, :], ig1t.t[:, k, :],
                                               xup.t[:, k, 3:3 + GCH],
                                               start=(k == 0), stop=(k == NGT - 1)),
                       reads=[ig1t, xup], writes=[statq])
        p.sec("t", f_ig1)

        def f_ig1e(eng):
            eng.do(lambda s: s.activation(ig1b.t[:], statq.t[0:16, :], AF.Copy),
                   reads=[statq], writes=[ig1b])
        p.sec("s", f_ig1e)

        for m in range(NGT):
            pp = pA if m % 2 == 0 else pB

            def f_ig2(eng, m=m, pp=pp):
                eng.do(lambda t, m=m: t.matmul(pp.t[:],
                                               ig2t.t[:, 128 * m:128 * m + 128],
                                               ig1b.t[:], start=True, stop=True),
                       reads=[ig2t, ig1b], writes=[pp])
            p.sec("t", f_ig2)

            def f_ig2e(eng, m=m, pp=pp):
                eng.do(lambda s, m=m: s.activation(sgt.t[:], pp.t[:], AF.Sigmoid,
                                                   bias=igb2.t[:, m:m + 1]),
                       reads=[pp, igb2], writes=[sgt])
            p.sec("s", f_ig2e)

            def f_vm(eng, m=m):
                eng.do(lambda v, m=m: v.tensor_mul(vbg.t[:, m, :], sgt.t[:],
                                                   xup.t[:, m, 3:3 + GCH]),
                       reads=[sgt, xup], writes=[vbg])
            p.sec("v", f_vm)

        p.skip = _GLA_OP < 7
        def f_sp2(eng):
            eng.do(lambda s: s.activation(t2g.t, usb.t, AF.Abs),
                   reads=[usb], writes=[t2g])
            eng.do(lambda s: s.activation(t2g.t, t2g.t, AF.Exp, scale=-1.0),
                   rw=[t2g])
            eng.do(lambda s: s.activation(t2g.t, t2g.t, AF.Ln, bias=1.0),
                   rw=[t2g])
        p.sec("s", f_sp2)

        def f_sp3(eng):
            eng.do(lambda v: v.tensor_scalar_max(ggs.t, usb.t, 0.0),
                   reads=[usb], writes=[ggs])
            eng.do(lambda v: v.tensor_add(ggs.t, ggs.t, t2g.t),
                   reads=[t2g], rw=[ggs])
            eng.do(lambda v: v.scalar_tensor_tensor(rts.t[:], ggs.t, -1.0,
                                                    taus.t, ALU.mult, ALU.mult),
                   reads=[ggs, taus], writes=[rts])
        p.sec("v", f_sp3)

        def f_it1(eng):
            eng.do(lambda s: s.activation(t2g.t, ggs.t, AF.Ln),
                   reads=[ggs], writes=[t2g])
        p.sec("s", f_it1)

        def f_it2(eng):
            eng.do(lambda v: v.tensor_mul(t2g.t, t2g.t, taus.t),
                   reads=[taus], rw=[t2g])
        p.sec("v", f_it2)

        def f_it3(eng):
            eng.do(lambda s: s.activation(its.t, t2g.t, AF.Exp),
                   reads=[t2g], writes=[its])
        p.sec("s", f_it3)

        def f_kn1(eng):
            eng.do(lambda s: s.activation(sqk.t[:], kg.t, AF.Square),
                   reads=[kg], writes=[sqk])
        p.sec("s", f_kn1)

        def f_kn2(eng):
            eng.do(lambda t: t.matmul(statx.t[0:1, :], C["onescolb"].t[0:64, 0:1],
                                      sqk.t[:], start=True, stop=True),
                   reads=[sqk, C["onescolb"]], writes=[statx])
        p.sec("t", f_kn2)

        def f_kn3(eng):
            eng.do(lambda s: s.activation(grow.t[0:1, 0:GCH], statx.t[0:1, :], AF.Ln),
                   reads=[statx], writes=[grow])
            eng.do(lambda s: s.activation(grow.t[0:1, 0:GCH], grow.t[0:1, 0:GCH],
                                          AF.Exp, scale=-0.5), rw=[grow])
        p.sec("s", f_kn3)

        def f_kn4(eng):
            eng.do(lambda v: v.tensor_scalar_min(grow.t[0:1, 0:GCH],
                                                 grow.t[0:1, 0:GCH], 1e12), rw=[grow])
        p.sec("v", f_kn4)

        def f_kn5(eng):
            eng.do(lambda t: t.matmul(mbp.t[0:64, :], C["onesrow"].t[0:1, 0:64],
                                      grow.t[0:1, 0:GCH], start=True, stop=True),
                   reads=[grow, C["onesrow"]], writes=[mbp])
        p.sec("t", f_kn5)

        def f_kn6(eng):
            eng.do(lambda v: v.tensor_mul(t2g.t, kg.t, mbp.t[0:64, :]),
                   reads=[kg, mbp], writes=[t2g])
            eng.do(lambda v: v.tensor_mul(knf.t[:], t2g.t, its.t),
                   reads=[t2g, its], writes=[knf])
        p.sec("v", f_kn6)

        p.skip = _GLA_OP < 8
        def f_shrw(eng):
            for m in range(NGT):
                eng.do(lambda v, m=m: v.tensor_scalar_mul(
                    shiftb.t[:, m, :], shiftb.t[:, m, :], rwv.t[:, m:m + 1]),
                    reads=[rwv], rw=[shiftb])
        p.sec("v", f_shrw)

        def f_halo(eng):
            eng.do(lambda v: v.tensor_copy(xup.t[:, :, 0:3],
                                           xup.t[:, :, GCH:GCH + 3]), rw=[xup])
        p.sec("v", f_halo)

        p.skip = False
        po.close()
        p.barrier()

        pss = ExitStack()
        _scan_on = _GLA_PH >= 2
        of_ps = p.buf(pss.enter_context(
            nc.psum_tensor(f"gs_of_{c}", [128, 8, GIC], F32)), "of")
        vt_ps = p.buf(pss.enter_context(
            nc.psum_tensor(f"gs_vt_{c}", [128, GD], BF16)), "vtp")
        at_ps = p.buf(pss.enter_context(
            nc.psum_tensor(f"gs_at_{c}", [128, GIC], F32)), "atp")
        kdt_ps = p.buf(pss.enter_context(
            nc.psum_tensor(f"gs_kdt_{c}", [GIC, 64], BF16)), "kdtp")
        U_ps = p.buf(pss.enter_context(
            nc.psum_tensor(f"gs_u_{c}", [64, GD // 2], F32)), "ups")

        for ic in range(GCH // GIC if _scan_on else 0):
            i0 = ic * GIC

            def f_b(eng, i0=i0):
                eng.do(lambda v: v.tensor_tensor_scan(
                    bsc.t[:], rts.t[:, i0:i0 + GIC], rts.t[:, i0:i0 + GIC], 0.0,
                    ALU.add, ALU.bypass), reads=[rts], writes=[bsc])
                eng.do(lambda v: v.tensor_scalar_sub(tm1.t[:], bsc.t[:],
                                                     bsc.t[:, GIC // 2:GIC // 2 + 1]),
                       reads=[bsc], writes=[tm1])
            p.sec("v", f_b)

            def f_eb1(eng):
                eng.do(lambda s: s.activation(ebx.t[:], tm1.t[:], AF.Exp),
                       reads=[tm1], writes=[ebx])
            p.sec("s", f_eb1)

            def f_qd(eng, i0=i0):
                eng.do(lambda v: v.tensor_mul(qd_b.t[:], qf32.t[:, i0:i0 + GIC],
                                              ebx.t[:]), reads=[qf32, ebx],
                       writes=[qd_b])
                eng.do(lambda v: v.tensor_scalar(tm1.t[:], bsc.t[:], -1.0,
                                                 bsc.t[:, GIC // 2:GIC // 2 + 1],
                                                 ALU.mult, ALU.add),
                       reads=[bsc], writes=[tm1])
            p.sec("v", f_qd)

            def f_eb2(eng):
                eng.do(lambda s: s.activation(ebx.t[:], tm1.t[:], AF.Exp),
                       reads=[tm1], writes=[ebx])
            p.sec("s", f_eb2)

            def f_kd(eng, i0=i0):
                eng.do(lambda v: v.tensor_mul(kd_b.t[:], knf.t[:, i0:i0 + GIC],
                                              ebx.t[:]), reads=[knf, ebx],
                       writes=[kd_b])
                eng.do(lambda v: v.tensor_scalar(tm1.t[:], bsc.t[:], -1.0,
                                                 bsc.t[:, GIC - 1:GIC], ALU.mult,
                                                 ALU.add),
                       reads=[bsc], writes=[tm1])
            p.sec("v", f_kd)

            def f_eb3(eng):
                eng.do(lambda s: s.activation(ebx.t[:], tm1.t[:], AF.Exp),
                       reads=[tm1], writes=[ebx])
                eng.do(lambda s: s.activation(ebt.t[:], bsc.t[0:64, GIC - 1:GIC],
                                              AF.Exp), reads=[bsc], writes=[ebt])
            p.sec("s", f_eb3)

            def f_kdec(eng, i0=i0):
                eng.do(lambda v: v.tensor_mul(kdec_b.t[:], knf.t[:, i0:i0 + GIC],
                                              ebx.t[:]), reads=[knf, ebx],
                       writes=[kdec_b])
            p.sec("v", f_kdec)

            def f_eb4(eng):
                eng.do(lambda s: s.activation(ebx.t[:], bsc.t[:], AF.Exp),
                       reads=[bsc], writes=[ebx])
            p.sec("s", f_eb4)

            def f_qdec(eng, i0=i0):
                eng.do(lambda v: v.tensor_mul(qdec_b.t[:], qf32.t[:, i0:i0 + GIC],
                                              ebx.t[:]), reads=[qf32, ebx],
                       writes=[qdec_b])
            p.sec("v", f_qdec)

            def f_at(eng):
                eng.do(lambda t: t.matmul(at_ps.t[:], kd_b.t[:], qd_b.t[:],
                                          start=True, stop=True),
                       reads=[kd_b, qd_b], writes=[at_ps])
                eng.do(lambda t: t.transpose(kdt_ps.t[:], kdec_b.t[:],
                                             C["ident"].t[0:64, 0:64]),
                       reads=[kdec_b, C["ident"]], writes=[kdt_ps])
            p.sec("t", f_at)

            def f_ate(eng):
                eng.do(lambda v: v.tensor_mul(am_sb.t[:], at_ps.t[:],
                                              C["gtri"].t[:]),
                       reads=[at_ps, C["gtri"]], writes=[am_sb])
                eng.do(lambda v: v.tensor_copy(kdt_sb.t[:], kdt_ps.t[:]),
                       reads=[kdt_ps], writes=[kdt_sb])
            p.sec("v", f_ate)

            def f_vt(eng, i0=i0):
                for m in range(NGT):
                    eng.do(lambda t, m=m: t.transpose(
                        vt_ps.t[:, 128 * m:128 * m + 128],
                        vbg.t[:, m, i0:i0 + GIC], C["ident"].t[:]),
                        reads=[vbg, C["ident"]], writes=[vt_ps])
            p.sec("t", f_vt)

            def f_vte(eng):
                eng.do(lambda s: s.activation(vt_sb.t[:], vt_ps.t[:], AF.Copy),
                       reads=[vt_ps], writes=[vt_sb])
            p.sec("s", f_vte)

            for hf in range(2):
                def f_o(eng, hf=hf):
                    for m2 in range(8):
                        m = 8 * hf + m2
                        eng.do(lambda t, m=m, m2=m2: t.matmul(
                            of_ps.t[:, m2, :],
                            S_bf.t[:, 128 * m:128 * m + 128], qdec_b.t[:],
                            start=True, stop=False),
                            reads=[S_bf, qdec_b], writes=[of_ps])
                        eng.do(lambda t, m=m, m2=m2: t.matmul(
                            of_ps.t[:, m2, :],
                            vt_sb.t[:, 128 * m:128 * m + 128], am_sb.t[:],
                            start=False, stop=True),
                            reads=[vt_sb, am_sb], writes=[of_ps])
                p.sec("t", f_o)

                def f_g1(eng, hf=hf, i0=i0):
                    eng.do(lambda v: v.tensor_add(
                        o2.t[:, 8 * hf:8 * hf + 8, :],
                        shiftb.t[:, 8 * hf:8 * hf + 8, i0:i0 + GIC], of_ps.t[:]),
                        reads=[shiftb, of_ps], writes=[o2])
                p.sec("v", f_g1)

            for hf in range(2):
                def f_u(eng, hf=hf):
                    for q2 in range(2):
                        eng.do(lambda t, hf=hf, q2=q2: t.matmul(
                            U_ps.t[:, 512 * q2:512 * q2 + 512], kdt_sb.t[:],
                            vt_sb.t[:, 1024 * hf + 512 * q2:
                                    1024 * hf + 512 * q2 + 512],
                            start=True, stop=True),
                            reads=[kdt_sb, vt_sb], writes=[U_ps])
                p.sec("t", f_u)

                def f_su(eng, hf=hf):
                    eng.do(lambda v, hf=hf: v.tensor_scalar_mul(
                        S_sb.t[:, 1024 * hf:1024 * hf + 1024],
                        S_sb.t[:, 1024 * hf:1024 * hf + 1024], ebt.t[:, 0:1]),
                        reads=[ebt], rw=[S_sb])
                    eng.do(lambda v, hf=hf: v.tensor_add(
                        S_sb.t[:, 1024 * hf:1024 * hf + 1024],
                        S_sb.t[:, 1024 * hf:1024 * hf + 1024], U_ps.t[:]),
                        reads=[U_ps], rw=[S_sb])
                p.sec("v", f_su)

            def f_sbf(eng):
                eng.do(lambda v: v.tensor_copy(S_bf.t[:], S_sb.t[:]),
                       reads=[S_sb], writes=[S_bf])
            p.sec("v", f_sbf)

            def f_g2(eng, i0=i0):
                eng.do(lambda v, i0=i0: v.tensor_mul(
                    o2.t[:], o2.t[:], fgb.t[:, :, i0:i0 + GIC]),
                    reads=[fgb], rw=[o2])
            p.sec("v", f_g2)

            def f_g3(eng):
                eng.do(lambda s: s.activation(sq2.t[:], o2.t[:], AF.Square),
                       reads=[o2], writes=[sq2])
            p.sec("s", f_g3)

            def f_g4(eng):
                for m in range(NGT):
                    eng.do(lambda t, m=m: t.matmul(
                        at_ps.t[0:1, :], C["onescolb"].t[:, 0:1],
                        sq2.t[:, m, :],
                        start=(m == 0), stop=(m == NGT - 1)),
                        reads=[sq2, C["onescolb"]], writes=[at_ps])
            p.sec("t", f_g4)

            def f_g5(eng):
                eng.do(lambda v: v.tensor_scalar(gr2.t[0:1, 0:GIC],
                                                 at_ps.t[0:1, :], 1.0 / GD, 1e-5,
                                                 ALU.mult, ALU.add),
                       reads=[at_ps], writes=[gr2])
            p.sec("v", f_g5)

            def f_g6(eng):
                eng.do(lambda s: s.activation(gr2.t[0:1, GIC:2 * GIC],
                                              gr2.t[0:1, 0:GIC], AF.Ln), rw=[gr2])
                eng.do(lambda s: s.activation(gr2.t[0:1, 0:GIC],
                                              gr2.t[0:1, GIC:2 * GIC], AF.Exp,
                                              scale=-0.5), rw=[gr2])
            p.sec("s", f_g6)

            def f_g7(eng):
                eng.do(lambda t: t.matmul(at_ps.t[:], C["onesrow"].t[0:1, :],
                                          gr2.t[0:1, 0:GIC], start=True, stop=True),
                       reads=[gr2, C["onesrow"]], writes=[at_ps])
            p.sec("t", f_g7)

            def f_g8(eng, i0=i0):
                for m in range(NGT):
                    eng.do(lambda v, m=m: v.scalar_tensor_tensor(
                        xgn.t[:, m, i0:i0 + GIC], o2.t[:, m, :],
                        nwv.t[:, m:m + 1], at_ps.t[:], ALU.mult, ALU.mult),
                        reads=[o2, nwv, at_ps], writes=[xgn])
            p.sec("v", f_g8)

        pss.close()
        p.barrier()

        pd = ExitStack()
        if _GLA_PH < 3:
            def f_st0(eng, c0=c0):
                eng.do(lambda y: y.dma_start(out=dst3[:, :, c0:c0 + GCH],
                                             in_=xo.t[:]),
                       reads=[xo], dma=True)
            p.sec("y", f_st0)
            pd.close()
            p.barrier()
            continue

        def f_rld(eng, c0=c0):
            eng.do(lambda y: y.dma_start(out=resv, in_=src3[:, :, c0:c0 + GCH]),
                   writes=[resb], dma=True)
        p.sec("y", f_rld)
        pD = [p.buf(pd.enter_context(
            nc.psum_tensor(f"gd_p{i}_{c}", [128, GCH], F32)), f"pD{i}")
            for i in range(2)]
        for m in range(NFT):
            pp = pD[m % 2]

            def f_dn(eng, m=m, pp=pp):
                for k in range(NGT):
                    eng.do(lambda t, k=k, m=m: t.matmul(
                        pp.t[:], downt.t[:, k, 128 * m:128 * m + 128],
                        xgn.t[:, k, :],
                        start=(k == 0), stop=(k == NGT - 1)),
                        reads=[downt, xgn], writes=[pp])
            p.sec("t", f_dn)

            def f_dne(eng, m=m, pp=pp):
                eng.do(lambda v, m=m: v.scalar_tensor_tensor(
                    xo.t[:, m, :], pp.t[:], 1.0, resv[:, m, :], ALU.bypass, ALU.add),
                    reads=[pp, resb], writes=[xo])
            p.sec("v", f_dne)

        def f_st(eng, c0=c0):
            eng.do(lambda y: y.dma_start(out=dst3[:, :, c0:c0 + GCH], in_=xo.t[:]),
                   reads=[xo], dma=True)
        p.sec("y", f_st)
        pd.close()
        p.barrier()

    es.close()



def build_program(upto):
    nc = bass.Bass("TRN2", target_bir_lowering=False,
                   dynamic_dma_scratch_size=16384)
    stk = ExitStack()

    def din(name, shape, dt=F32):
        return nc.dram_tensor(name, list(shape), dt, kind="ExternalInput").ap()

    xt = din("xt", [D, T_FULL])
    aWKT = din("aWKT", [D, D]); aW1T = din("aW1T", [D, 64]); aW2T = din("aW2T", [64, D])
    rtab = din("rtab", [4, 128, T_FULL])
    mprev = din("mprev", [128, 128]); mdiag = din("mdiag", [128, 128])
    gtri = din("gtri", [GIC, GIC]); ident = din("ident", [128, 128])
    halfsel = din("halfsel", [128, 2])
    fw = {}
    for l in (0, 1):
        fw[l] = dict(
            WQKV=din(f"fWQKV{l}", [D, 1536]),
            QSa=din(f"fQSa{l}", [1, 128]), QSb=din(f"fQSb{l}", [1, 128]),
            KSa=din(f"fKSa{l}", [1, 128]), KSb=din(f"fKSb{l}", [1, 128]),
            WO=din(f"fWO{l}", [512, D]),
            LG=din(f"fLG{l}", [128, NFT]), LB=din(f"fLB{l}", [128, NFT]),
            W1=din(f"fW1{l}", [D, 2 * HIDP]), B1=din(f"fB1{l}", [128, NW1T]),
            W2=din(f"fW2{l}", [HIDP, D]), B2=din(f"fB2{l}", [128, NFT]),
        )
    gw = dict(
        UPT=din("gUPT", [D, GD]), GATET=din("gGATET", [D, GD]),
        DOWNT=din("gDOWNT", [GD, D]), QKT=din("gQKT", [GD, 128]),
        GTT=din("gGTT", [GD, 128]), IG1T=din("gIG1T", [GD, 16]),
        IG2T=din("gIG2T", [16, GD]), CONVD=din("gCONVD", [128, 4, NGT, 128]),
        CONVB=din("gCONVB", [128, NGT]), GB=din("gGB", [64, 1]),
        TAUB=din("gTAUB", [64, 1]), IGB2=din("gIGB2", [128, NGT]),
        RW=din("gRW", [128, NGT]), NW=din("gNW", [128, NGT]),
    )
    res = nc.dram_tensor("res", [D, T_FULL], F32, kind="Internal").ap()
    ffs = nc.dram_tensor("ffs", [D, T_FULL], F32, kind="Internal").ap()
    out = nc.dram_tensor("out", [D, T_FULL], F32, kind="ExternalOutput").ap()
    dbg = (nc.dram_tensor("dbg", [8, 128, 656], F32, kind="ExternalOutput").ap()
           if _DBG_CHUNK >= 0 else None)
    r3 = lambda ap: ap.rearrange("(a p) n -> p a n", p=128)
    xt3, res3, ffs3, out3 = r3(xt), r3(res), r3(ffs), r3(out)

    with stk:
        block = stk.enter_context(nc.Block())
        p = Pr(nc, block, stk)
        p.dbg = dbg

        cst = ExitStack()
        sb = lambda nm, shp, dt: p.buf(cst.enter_context(nc.sbuf_tensor(nm, shp, dt)), nm)
        C = dict(
            ident=sb("c_ident", [128, 128], BF16),
            mprev=sb("c_mprev", [128, 128], F32),
            mdiag=sb("c_mdiag", [128, 128], F32),
            gtri=sb("c_gtri", [GIC, GIC], BF16),
            onesrow=sb("c_onesrow", [1, 128], F32),
            onescol=sb("c_onescol", [128, 2], F32),
            onescolb=sb("c_onescolb", [128, 2], BF16),
            halfselb=sb("c_halfselb", [128, 2], BF16),
        )
        load_w(p, C["ident"], ident)
        load_w(p, C["mprev"], mprev)
        load_w(p, C["mdiag"], mdiag)
        load_w(p, C["gtri"], gtri)
        load_w(p, C["halfselb"], halfsel)

        def f_c(eng):
            eng.do(lambda g: g.memset(C["onesrow"].t[:], 1.0), writes=[C["onesrow"]])
            eng.do(lambda g: g.memset(C["onescol"].t[:], 1.0), writes=[C["onescol"]])
            eng.do(lambda g: g.memset(C["onescolb"].t[:], 1.0), writes=[C["onescolb"]])
        p.sec("g", f_c)
        p.barrier()

        st_attn_a(p, nc, C, xt3, res3 if upto > 1 else res3, aWKT, aW1T, aW2T)
        p.barrier()
        if upto >= 2:
            st_ffn_attn(p, nc, C, 0, True, res3, ffs3, fw[0], rtab)
            p.barrier()
        if upto >= 3:
            st_ffn_mlp(p, nc, C, 0, ffs3, ffs3, None, fw[0])
            p.barrier()
        if upto >= 4:
            st_ffn_attn(p, nc, C, 1, False, ffs3,
                        (res3 if _F1A_OOP and upto == 4 else ffs3), fw[1], rtab)
            p.barrier()
        if upto >= 5:
            st_ffn_mlp(p, nc, C, 1, ffs3, (res3 if upto > 5 else out3), res3, fw[1])
            p.barrier()
        if upto >= 6:
            st_gla(p, nc, C, res3, (res3 if upto > 6 else out3), gw)
            p.barrier()
        if upto >= 7:
            st_ffn_attn(p, nc, C, 0, True, res3, ffs3, fw[0], rtab)
            p.barrier()
        if upto >= 8:
            st_ffn_mlp(p, nc, C, 0, ffs3, ffs3, None, fw[0])
            p.barrier()
        if upto >= 9:
            st_ffn_attn(p, nc, C, 1, False, ffs3, ffs3, fw[1], rtab)
            p.barrier()
        if upto >= 10:
            st_ffn_mlp(p, nc, C, 1, ffs3, out3, res3, fw[1])
            p.barrier()

        if upto in (1, 2, 3, 4, 7, 8, 9):
            src3 = res3 if (upto == 1 or (_F1A_OOP and upto == 4)) else ffs3
            cp = ExitStack()
            tb = p.buf(cp.enter_context(nc.sbuf_tensor("cpb", [128, NFT, CH], F32)))
            for c in range(NCH):
                c0 = c * CH

                def fl(eng, c0=c0):
                    eng.do(lambda y: y.dma_start(out=tb.t[:], in_=src3[:, :, c0:c0 + CH]),
                           writes=[tb], dma=True)
                    eng.do(lambda y: y.dma_start(out=out3[:, :, c0:c0 + CH], in_=tb.t[:]),
                           reads=[tb], dma=True)
                p.sec("y", fl)
            cp.close()

        snap = dict(p.cnt)

        def fin(eng):
            for sn, c in snap.items():
                if c and p.seen["y"][sn] < c:
                    eng.raw.wait_ge(p.sem[sn], c)
                    p.seen["y"][sn] = c
        p.sec("y", fin)
        cst.close()
    return nc


def _kernel_impl(inputs, upto=_UPTO, cores=None):
    w = _prep(inputs)
    xin = np.asarray(inputs["L0_attn_a_x"])
    B = xin.shape[0]
    if cores is None:
        cores = list(range(B))
    nc = build_program(upto)
    in_maps = []
    for b in cores:
        m = {"xt": np.ascontiguousarray(xin[b].T)}
        m.update(w)
        in_maps.append(m)
    resl = run_bass_kernel_spmd(nc, in_maps, core_ids=list(range(len(cores))))
    global _LAST
    _LAST = resl.results
    outs = [np.ascontiguousarray(r["out"].T) for r in resl.results]
    return np.stack(outs, 0)


_LAST = None


def kernel(**inputs):
    return _kernel_impl(inputs)


# version 03
# speedup vs baseline: 1.0990x; 1.0939x over previous
import numpy as np
import concourse.bass as bass
import concourse.mybir as mybir
from concourse.bass_utils import run_bass_kernel_spmd
from contextlib import ExitStack

F32 = mybir.dt.float32
BF16 = mybir.dt.bfloat16
AF = mybir.ActivationFunctionType
ALU = mybir.AluOpType
AX = mybir.AxisListType

T_FULL = 8192
D = 1024
NFT = 8
CH = 512
NCH = T_FULL // CH
HEADS = 8
DH = 64
WIN = 512
HIDP = 2816
NW1T = 44
NW2K = 22
GD = 2048
NGT = 16
GCH = 256
NGCH = T_FULL // GCH
GIC = 128
NEG = -30000.0

_UPTO = 10
_DBG_CHUNK = -1
_GLA_LIMIT = None
_GLA_PH = 3
_GLA_OP = 8
_LN_LV = 9
_GLA_NOUNION = False
_T_RELAX = True
_F1A_OOP = False


def _host_consts():
    n = T_FULL
    half = np.arange(0, DH, 2, dtype=np.float64)
    inv_freq = 1.0 / 10000.0 ** (half / DH)
    t = np.arange(n, dtype=np.float64)
    freqs32 = t[:, None] * inv_freq[None, :]
    base32 = (half + 0.4 * DH) / (1.4 * DH)
    power = (t - n // 2) / float(WIN // 2)
    scale32 = base32[None, :] ** power[:, None]
    freqs = np.concatenate([freqs32, freqs32], 1)
    scale = np.concatenate([scale32, scale32], 1)
    cos = np.cos(freqs); sin = np.sin(freqs)
    perm = np.empty(64, np.int64)
    perm[0::2] = np.arange(32); perm[1::2] = np.arange(32, 64)
    cos = cos[:, perm]; sin = sin[:, perm]; scale = scale[:, perm]
    sgn = np.empty(64); sgn[0::2] = -1.0; sgn[1::2] = 1.0
    cq = (cos * scale).T
    sq = (sin * scale).T * sgn[:, None]
    ck = (cos / scale).T
    sk = (sin / scale).T * sgn[:, None]
    dup = lambda a: np.concatenate([a, a], 0)
    rtab = np.ascontiguousarray(
        np.stack([dup(cq), dup(sq), dup(ck), dup(sk)], 0).astype(np.float32))
    i = np.arange(128)[:, None]; j = np.arange(128)[None, :]
    mprev = np.where(j >= i, 0.0, NEG).astype(np.float32)
    mdiag = np.where(j <= i, 0.0, NEG).astype(np.float32)
    jj = np.arange(GIC)[:, None]; ii = np.arange(GIC)[None, :]
    gtri = np.where(jj <= ii, 1.0, 0.0).astype(np.float32)
    ident = np.eye(128, dtype=np.float32)
    hs = np.zeros((128, 2), np.float32)
    hs[0:64, 0] = 1.0
    hs[64:128, 1] = 1.0
    return rtab, mprev, mdiag, gtri, ident, perm, hs


def _prep(inputs):
    g = {k: np.asarray(v) for k, v in inputs.items()}
    ac = np.ascontiguousarray
    w = {}
    w["aWKT"] = ac(g["L0_attn_a_W_K"].T)
    w["aW1T"] = ac(g["L0_attn_a_W1"].T)
    w["aW2T"] = ac(g["L0_attn_a_W2"].T)
    rtab, mprev, mdiag, gtri, ident, perm, hs = _host_consts()
    w["rtab"], w["mprev"], w["mdiag"], w["gtri"], w["ident"] = \
        rtab, mprev, mdiag, gtri, ident
    w["halfsel"] = hs
    for l in (0, 1):
        wq = g["L0_ffn_wqkv"][l]
        qc = wq[:, 0:512].reshape(D, HEADS, DH)[:, :, perm].reshape(D, 512)
        kc = wq[:, 512:1024].reshape(D, HEADS, DH)[:, :, perm].reshape(D, 512)
        vc = wq[:, 1024:1536]
        w[f"fWQKV{l}"] = ac(np.concatenate([qc, kc, vc], 1))
        qs = g["L0_ffn_q_scale"][l][perm]; ks = g["L0_ffn_k_scale"][l][perm]
        z64 = np.zeros(64, np.float32)
        w[f"fQSa{l}"] = ac(np.concatenate([qs, z64])[None, :].astype(np.float32))
        w[f"fQSb{l}"] = ac(np.concatenate([z64, qs])[None, :].astype(np.float32))
        w[f"fKSa{l}"] = ac(np.concatenate([ks, z64])[None, :].astype(np.float32))
        w[f"fKSb{l}"] = ac(np.concatenate([z64, ks])[None, :].astype(np.float32))
        w[f"fWO{l}"] = ac(g["L0_ffn_wo"][l])
        w[f"fLG{l}"] = ac(g["L0_ffn_ln_g"][l].reshape(NFT, 128).T)
        w[f"fLB{l}"] = ac(g["L0_ffn_ln_b"][l].reshape(NFT, 128).T)
        w1 = g["L0_ffn_w1"][l]; b1 = g["L0_ffn_b1"][l]
        w1p = np.zeros((D, 2 * HIDP), np.float32)
        b1p = np.zeros(2 * HIDP, np.float32)
        for q2 in range(NW1T // 2):
            a0, a1 = 128 * q2, min(128 * q2 + 128, 2730)
            wd = a1 - a0
            if wd <= 0:
                continue
            w1p[:, 256 * q2:256 * q2 + wd] = w1[:, a0:a1]
            w1p[:, 256 * q2 + 128:256 * q2 + 128 + wd] = w1[:, 2730 + a0:2730 + a1]
            b1p[256 * q2:256 * q2 + wd] = b1[a0:a1]
            b1p[256 * q2 + 128:256 * q2 + 128 + wd] = b1[2730 + a0:2730 + a1]
        w[f"fW1{l}"] = ac(w1p)
        w[f"fB1{l}"] = ac(b1p.reshape(NW1T, 128).T)
        w2p = np.zeros((HIDP, D), np.float32)
        w2p[0:2730] = g["L0_ffn_w2"][l]
        w[f"fW2{l}"] = ac(w2p)
        w[f"fB2{l}"] = ac(g["L0_ffn_b2"][l].reshape(NFT, 128).T)
    w["gUPT"] = ac(g["L0_attn_b_up_w"].T)
    w["gGATET"] = ac(g["L0_attn_b_gate_w"].T)
    w["gDOWNT"] = ac(g["L0_attn_b_down_w"].T)
    w["gQKT"] = ac(np.concatenate([g["L0_attn_b_q_w"].T, g["L0_attn_b_k_w"].T], 1))
    w["gGTT"] = ac(np.concatenate([g["L0_attn_b_g_w"].T, g["L0_attn_b_tau_w"].T], 1))
    w["gIG1T"] = ac(g["L0_attn_b_ig_w1"].T)
    w["gIG2T"] = ac(g["L0_attn_b_ig_w2"].T)
    cw = g["L0_attn_b_conv_w"]
    cd = np.zeros((4, NGT, 128, 128), np.float32)
    for jt in range(4):
        for it in range(NGT):
            np.fill_diagonal(cd[jt, it], cw[128 * it:128 * it + 128, jt])
    w["gCONVD"] = ac(cd.transpose(2, 0, 1, 3))
    w["gCONVB"] = ac(g["L0_attn_b_conv_b"].reshape(NGT, 128).T)
    w["gGB"] = ac(g["L0_attn_b_g_b"].reshape(64, 1))
    w["gTAUB"] = ac(g["L0_attn_b_tau_b"].reshape(64, 1))
    w["gIGB2"] = ac(g["L0_attn_b_ig_b2"].reshape(NGT, 128).T)
    w["gRW"] = ac(g["L0_attn_b_residual_weight"].reshape(NGT, 128).T)
    w["gNW"] = ac(g["L0_attn_b_norm_weight"].reshape(NGT, 128).T)
    return w



class Buf:
    __slots__ = ("name", "t", "prod", "cons")

    def __init__(self, name, t):
        self.name, self.t = name, t
        self.prod = {}
        self.cons = {}


ENGFN = {"t": "tensor", "v": "vector", "s": "scalar", "g": "gpsimd", "y": "sync"}
OWN = {"t": "t", "v": "v", "s": "s", "g": "g", "y": None}
DMAS = {"g": "gd", "y": "yd", "s": "sd"}


class Pr:
    strict = True

    def __init__(self, nc, block, stack):
        self.nc, self.block, self.stack = nc, block, stack
        self.sem = {}
        self.cnt = {}
        for s in ("t", "v", "s", "g", "gd", "yd", "sd"):
            self.sem[s] = stack.enter_context(nc.semaphore(f"S{s}"))
            self.cnt[s] = 0
        self.seen = {e: dict.fromkeys(self.sem, 0) for e in ENGFN}

    skip = False

    def sec(self, e, f):
        if self.skip:
            return
        getattr(self.block, ENGFN[e])(lambda raw: f(Eng(self, e, raw)))

    def buf(self, handle, name=None):
        return Buf(name or getattr(handle, "name", "b"), handle)

    @staticmethod
    def alias(primary, handle_or_ap, name="alias"):
        b = Buf(name, handle_or_ap)
        b.prod = primary.prod
        b.cons = primary.cons
        return b

    def barrier(self):
        snap = dict(self.cnt)
        for e in ENGFN:
            def f(eng, snap=snap, e=e):
                for sn, c in snap.items():
                    if c and sn != OWN[e] and self.seen[e][sn] < c:
                        eng.raw.wait_ge(self.sem[sn], c)
                        self.seen[e][sn] = c
            self.sec(e, f)


class Eng:
    __slots__ = ("pr", "e", "raw")

    def __init__(self, pr, e, raw):
        self.pr, self.e, self.raw = pr, e, raw

    def _wait(self, need):
        for sn, c in need.items():
            if sn == OWN[self.e] and (not self.pr.strict or (self.e == "t" and _T_RELAX)):
                continue
            if sn in ("gd", "yd", "sd"):
                c = self.pr.cnt[sn]
            if self.pr.seen[self.e][sn] < c:
                self.raw.wait_ge(self.pr.sem[sn], c)
                self.pr.seen[self.e][sn] = c

    def do(self, build, reads=(), writes=(), dma=False, rw=()):
        need = {}
        for b in tuple(reads) + tuple(rw):
            for s, c in b.prod.items():
                need[s] = max(need.get(s, 0), c)
        for b in tuple(writes) + tuple(rw):
            for s, c in b.cons.items():
                need[s] = max(need.get(s, 0), c)
            for s, c in b.prod.items():
                need[s] = max(need.get(s, 0), c)
        if dma:
            sn = DMAS[self.e]
            need[sn] = max(need.get(sn, 0), self.pr.cnt[sn])
        self._wait(need)
        ins = build(self.raw)
        if dma:
            sn, inc = DMAS[self.e], 16
        else:
            sn, inc = OWN[self.e], 1
        self.pr.cnt[sn] += inc
        ins.then_inc(self.pr.sem[sn], inc)
        if not dma and (not self.pr.strict or (self.e == "t" and _T_RELAX)):
            self.pr.seen[self.e][sn] = self.pr.cnt[sn]
        cend = self.pr.cnt[sn]
        for b in tuple(writes) + tuple(rw):
            b.prod.clear()
            b.prod[sn] = cend
        for b in tuple(reads) + tuple(rw):
            b.cons[sn] = cend
        return ins


def load_w(p, dst, src_ap):
    def f(eng):
        eng.do(lambda g: g.dma_start(out=dst.t[:], in_=src_ap), writes=[dst], dma=True)
    p.sec("g", f)



def emit_ln(p, C, xap, xbufs, sq, rows, statx, statq, mb, rb, N, sq_bf16=False,
            outap=None, outbufs=None, xmap=None, xmbufs=None, xap_whole=None):
    _lv = _LN_LV
    """Emit LN for one chunk of N tokens over NFT feature tiles.

    xap(i): f32 AP [128, N] of input tile i; xbufs: dep Bufs covering them.
    sq:     scratch Buf [128, NFT, N] (f32 or bf16, matching sq_bf16).
    rows:   scratch Buf f32 [1, 3N].
    statx/statq: psum Bufs (row 0, N cols used); mb/rb: psum Bufs [128, N].
    Writes xm (x - mean, f32) via xmap(i) and (x-mean)*rstd via outap(i).
    """
    def f_sq(eng):
        if xap_whole is not None:
            eng.do(lambda s: s.activation(sq.t[:], xap_whole, AF.Square),
                   reads=xbufs, writes=[sq])
        else:
            for i in range(NFT):
                eng.do(lambda s, i=i: s.activation(sq.t[:, i, :], xap(i), AF.Square),
                       reads=xbufs, writes=[sq])
    p.sec("s", f_sq)

    onesq = C["onescolb"] if sq_bf16 else C["onescol"]
    if _lv < 2:
        return

    def f_stat(eng):
        for i in range(NFT):
            eng.do(lambda t, i=i: t.matmul(statx.t[0:1, 0:N], C["onescol"].t[:, 0:1],
                                           xap(i), start=(i == 0), stop=(i == NFT - 1)),
                   reads=list(xbufs) + [C["onescol"]], writes=[statx])
        for i in range(NFT):
            eng.do(lambda t, i=i: t.matmul(statq.t[0:1, 0:N], onesq.t[:, 0:1],
                                           sq.t[:, i, :], start=(i == 0), stop=(i == NFT - 1)),
                   reads=[sq, onesq], writes=[statq])
    p.sec("t", f_stat)

    if _lv < 3:
        return

    def f_rows1(eng):
        eng.do(lambda v: v.tensor_scalar_mul(rows.t[0:1, 0:N], statx.t[0:1, 0:N],
                                             1.0 / D), reads=[statx], writes=[rows])
        eng.do(lambda v: v.tensor_scalar(rows.t[0:1, N:2 * N], statq.t[0:1, 0:N],
                                         1.0 / D, 1e-5, ALU.mult, ALU.add),
               reads=[statq], rw=[rows])
    p.sec("v", f_rows1)

    def f_rows2(eng):
        eng.do(lambda s: s.activation(statx.t[0:1, 0:N], rows.t[0:1, 0:N],
                                      AF.Square), reads=[rows], writes=[statx])
    p.sec("s", f_rows2)

    def f_rows3(eng):
        eng.do(lambda v: v.tensor_sub(rows.t[0:1, N:2 * N], rows.t[0:1, N:2 * N],
                                      statx.t[0:1, 0:N]), reads=[statx], rw=[rows])
    p.sec("v", f_rows3)

    def f_rows4(eng):
        eng.do(lambda s: s.activation(statx.t[0:1, 0:N], rows.t[0:1, N:2 * N],
                                      AF.Ln), reads=[rows], writes=[statx])
        eng.do(lambda s: s.activation(rows.t[0:1, N:2 * N], statx.t[0:1, 0:N],
                                      AF.Exp, scale=-0.5), reads=[statx], rw=[rows])
    p.sec("s", f_rows4)

    if _lv < 4:
        return

    def f_bcast(eng):
        eng.do(lambda t: t.matmul(mb.t[:], C["onesrow"].t[0:1, :], rows.t[0:1, 0:N],
                                  start=True, stop=True),
               reads=[rows, C["onesrow"]], writes=[mb])
        eng.do(lambda t: t.matmul(rb.t[:], C["onesrow"].t[0:1, :], rows.t[0:1, N:2 * N],
                                  start=True, stop=True),
               reads=[rows, C["onesrow"]], writes=[rb])
    p.sec("t", f_bcast)

    if outap is not None:
        def f_apply(eng):
            for i in range(NFT):
                eng.do(lambda v, i=i: v.tensor_sub(xmap(i), xap(i), mb.t[:]),
                       reads=list(xbufs) + [mb], writes=list(xmbufs))
            for i in range(NFT):
                eng.do(lambda v, i=i: v.tensor_mul(outap(i), xmap(i), rb.t[:]),
                       reads=list(xmbufs) + [rb], writes=list(outbufs))
        p.sec("v", f_apply)



def st_attn_a(p, nc, C, src3, dst3, aWKT, aW1T, aW2T):
    es = ExitStack()
    sb = lambda nm, shp, dt: p.buf(es.enter_context(nc.sbuf_tensor(nm, shp, dt)), nm)
    ps = lambda nm, shp: p.buf(es.enter_context(nc.psum_tensor(nm, shp, F32)), nm)

    wkt = sb("a_wkt", [128, NFT, D], BF16)
    w1t = sb("a_w1t", [128, NFT, 64], BF16)
    w2t = sb("a_w2t", [64, D], BF16)
    load_w(p, wkt, aWKT.rearrange("(a p) m -> p a m", p=128))
    load_w(p, w1t, aW1T.rearrange("(a p) m -> p a m", p=128))
    load_w(p, w2t, aW2T)

    x = [sb(f"a_x{i}", [128, NFT, CH], F32) for i in range(2)]
    sq = sb("a_sq", [128, NFT, CH], F32)
    xm = sb("a_xm", [128, NFT, CH], F32)
    xn = sb("a_xn", [128, NFT, CH], BF16)
    kb = sb("a_kb", [128, NFT, CH], BF16)
    gg = sb("a_gg", [64, CH], BF16)
    rows = sb("a_rows", [1, 2 * CH], F32)
    xo = [sb(f"a_xo{i}", [128, NFT, CH], F32) for i in range(2)]

    statx = ps("a_statx", [1, CH])
    statq = ps("a_statq", [1, CH])
    mb = ps("a_mb", [128, CH])
    rb = ps("a_rb", [128, CH])
    kp = [ps(f"a_kp{i}", [128, CH]) for i in range(2)]
    hp = ps("a_hp", [64, CH])
    rp = [ps("a_rp0", [128, CH])] * 2

    for c in range(NCH):
        c0 = c * CH
        xc, xoc = x[c % 2], xo[c % 2]

        def f_load(eng, c0=c0, xc=xc):
            eng.do(lambda y: y.dma_start(out=xc.t[:], in_=src3[:, :, c0:c0 + CH]),
                   writes=[xc], dma=True)
        p.sec("y", f_load)

        emit_ln(p, C, lambda i, xc=xc: xc.t[:, i, :], [xc], sq, rows, statx, statq,
                mb, rb, CH, sq_bf16=False, xap_whole=xc.t[:],
                outap=lambda i: xn.t[:, i, :], outbufs=[xn],
                xmap=lambda i: xm.t[:, i, :], xmbufs=[xm])

        for m in range(NFT):
            kpp = kp[m % 2]

            def f_mm(eng, m=m, kpp=kpp):
                for k in range(NFT):
                    eng.do(lambda t, k=k, m=m: t.matmul(
                        kpp.t[:], wkt.t[:, k, 128 * m:128 * m + 128], xn.t[:, k, :],
                        start=(k == 0), stop=(k == NFT - 1)),
                        reads=[wkt, xn], writes=[kpp])
            p.sec("t", f_mm)

            def f_ev(eng, m=m, kpp=kpp):
                eng.do(lambda v: v.tensor_copy(kb.t[:, m, :], kpp.t[:]),
                       reads=[kpp], writes=[kb])
            p.sec("v", f_ev)

        def f_h(eng):
            for k in range(NFT):
                eng.do(lambda t, k=k: t.matmul(hp.t[:], w1t.t[:, k, :], kb.t[:, k, :],
                                               start=(k == 0), stop=(k == NFT - 1)),
                       reads=[w1t, kb], writes=[hp])
        p.sec("t", f_h)

        def f_g(eng):
            eng.do(lambda s: s.activation(gg.t[:], hp.t[:], AF.Gelu),
                   reads=[hp], writes=[gg])
        p.sec("s", f_g)

        for m in range(NFT):
            rpp = rp[m % 2]

            def f_mm2(eng, m=m, rpp=rpp):
                eng.do(lambda t: t.matmul(rpp.t[:], w2t.t[:, 128 * m:128 * m + 128],
                                          gg.t[:], start=True, stop=True),
                       reads=[w2t, gg], writes=[rpp])
            p.sec("t", f_mm2)

            def f_res(eng, m=m, rpp=rpp, xc=xc, xoc=xoc):
                eng.do(lambda v: v.scalar_tensor_tensor(
                    xoc.t[:, m, :], rpp.t[:], 1.0, xc.t[:, m, :], ALU.bypass, ALU.add),
                    reads=[rpp, xc], writes=[xoc])
            p.sec("v", f_res)

        def f_st(eng, c0=c0, xoc=xoc):
            eng.do(lambda y: y.dma_start(out=dst3[:, :, c0:c0 + CH], in_=xoc.t[:]),
                   reads=[xoc], dma=True)
        p.sec("y", f_st)

    es.close()


def st_ffn_attn(p, nc, C, l, first_ln, src3, dst3, fwl, rtab):
    es = ExitStack()
    sb = lambda nm, shp, dt: p.buf(es.enter_context(nc.sbuf_tensor(nm, shp, dt)), nm)
    ps = lambda nm, shp, dt=F32: p.buf(es.enter_context(nc.psum_tensor(nm, shp, dt)), nm)
    sfx = f"_{l}_{1 if first_ln else 0}_{p.cnt['t']}"

    wq = sb("f_wq" + sfx, [128, NFT, 1536], BF16)
    wo = sb("f_wo" + sfx, [128, 4, D], BF16)
    qsa = sb("f_qsa" + sfx, [1, 128], F32); qsb = sb("f_qsb" + sfx, [1, 128], F32)
    ksa = sb("f_ksa" + sfx, [1, 128], F32); ksb = sb("f_ksb" + sfx, [1, 128], F32)
    load_w(p, wq, fwl["WQKV"].rearrange("(a p) m -> p a m", p=128))
    load_w(p, wo, fwl["WO"].rearrange("(a p) m -> p a m", p=128))
    load_w(p, qsa, fwl["QSa"]); load_w(p, qsb, fwl["QSb"])
    load_w(p, ksa, fwl["KSa"]); load_w(p, ksb, fwl["KSb"])

    yc = [sb(f"f_yc{i}" + sfx, [128, NFT, CH], F32) for i in range(2)]
    xn = sb("f_xn" + sfx, [128, NFT, CH + 1], BF16)
    if first_ln:
        sq = sb("f_sq" + sfx, [128, NFT, CH], BF16)
        yn = sb("f_yn" + sfx, [128, NFT, CH], F32)
        lrow = sb("f_lrow" + sfx, [1, 2 * CH], F32)
    qf = sb("f_qf" + sfx, [128, 4, CH], F32)
    kraw = sb("f_kraw" + sfx, [128, 4, CH], F32)
    vb = sb("f_vb" + sfx, [128, 4, CH], BF16)
    sqq = sb("f_sqq" + sfx, [128, CH], BF16)
    qb = sb("f_qb" + sfx, [128, 4, CH], BF16)
    kf = [sb(f"f_kf{i}" + sfx, [128, 4, CH], BF16) for i in range(2)]
    vt = [sb(f"f_vt{i}" + sfx, [128, 4, 4, 128], BF16) for i in range(2)]
    rt2 = [sb(f"f_rt{i}" + sfx, [128, 4, CH], F32) for i in range(2)]
    rows = sb("f_rows" + sfx, [1, 2 * CH], F32)
    snm = sb("f_snm" + sfx, [128, 16], F32)
    Eb = sb("f_eb" + sfx, [128, 1024], BF16)
    Et = sb("f_et" + sfx, [128, 640], BF16)
    Ob = sb("f_ob" + sfx, [128, 4, CH], BF16)
    yo = sb("f_yo" + sfx, [128, NFT, CH], F32)
    qn = sb("f_qn" + sfx, [128, CH], F32)
    sh = sb("f_sh" + sfx, [128, CH], F32)
    t1 = sb("f_t1" + sfx, [128, CH], F32)

    ps_A = ps("f_pA" + sfx, [128, 2 * CH])
    ps_B = ps("f_pB" + sfx, [128, 2 * CH])
    ps_c = ps("f_pc" + sfx, [128, CH])
    ps_d = ps("f_pd" + sfx, [128, CH])
    ps_g = ps("f_pg" + sfx, [128, CH])
    ps_h = ps("f_ph" + sfx, [128, 1024], BF16)

    swapmask = [i + 1 if i % 2 == 0 else i - 1 for i in range(32)]
    LN8 = float(np.log(8.0))

    def f_z(eng):
        eng.do(lambda v: v.memset(xn.t[:, :, 0:1], 0.0), writes=[xn])
    p.sec("v", f_z)

    for c in range(NCH):
        c0 = c * CH
        ycc = yc[c % 2]
        kf_c, kf_p = kf[c % 2], kf[(c + 1) % 2]
        vt_c, vt_p = vt[c % 2], vt[(c + 1) % 2]
        rtc = rt2[c % 2]

        def f_load(eng, c0=c0, ycc=ycc, rtc=rtc):
            eng.do(lambda y: y.dma_start(out=ycc.t[:], in_=src3[:, :, c0:c0 + CH]),
                   writes=[ycc], dma=True)
            eng.do(lambda y: y.dma_start(
                out=rtc.t[:], in_=rtab.rearrange("a p n -> p a n")[:, :, c0:c0 + CH]),
                writes=[rtc], dma=True)
        p.sec("y", f_load)

        if first_ln:
            emit_ln(p, C, lambda i, ycc=ycc: ycc.t[:, i, :], [ycc], sq, lrow,
                    ps_c, ps_d, ps_c, ps_d, CH, sq_bf16=True, xap_whole=ycc.t[:],
                    outap=lambda i: yn.t[:, i, :], outbufs=[yn],
                    xmap=lambda i, ycc=ycc: ycc.t[:, i, :], xmbufs=[ycc])

            def f_xn(eng):
                eng.do(lambda v: v.tensor_copy(xn.t[:, :, 1:CH + 1], yn.t[:]),
                       reads=[yn], writes=[xn])
            p.sec("v", f_xn)
            base = yn
        else:
            def f_xn(eng, ycc=ycc):
                eng.do(lambda v: v.tensor_copy(xn.t[:, :, 1:CH + 1], ycc.t[:]),
                       reads=[ycc], writes=[xn])
            p.sec("v", f_xn)
            base = ycc

        for m in range(12):
            pp = ps_A if m % 2 == 0 else ps_B

            def f_mm(eng, m=m, pp=pp):
                for k in range(NFT):
                    off = 1 if k < 4 else 0
                    eng.do(lambda t, k=k, m=m, off=off: t.matmul(
                        pp.t[:, 0:CH], wq.t[:, k, 128 * m:128 * m + 128],
                        xn.t[:, k, off:off + CH],
                        start=(k == 0), stop=(k == NFT - 1)),
                        reads=[wq, xn], writes=[pp])
            p.sec("t", f_mm)

            if m < 4:
                def f_ev(eng, m=m, pp=pp):
                    eng.do(lambda v, m=m: v.tensor_copy(qf.t[:, m, :], pp.t[:, 0:CH]),
                           reads=[pp], writes=[qf])
                p.sec("v", f_ev)
            elif m < 8:
                def f_ev(eng, m=m, pp=pp):
                    eng.do(lambda v, m=m: v.tensor_copy(kraw.t[:, m - 4, :],
                                                        pp.t[:, 0:CH]),
                           reads=[pp], writes=[kraw])
                p.sec("v", f_ev)
            else:
                def f_ev(eng, m=m, pp=pp):
                    eng.do(lambda v, m=m: v.tensor_copy(vb.t[:, m - 8, :],
                                                        pp.t[:, 0:CH]),
                           reads=[pp], writes=[vb])
                p.sec("v", f_ev)

        if c == _DBG_CHUNK:
            def f_dbg2(eng):
                dbg = p.dbg
                eng.do(lambda g: g.dma_start(out=dbg[1, :, 0:CH],
                                             in_=kraw.t[:, 0, :]),
                       reads=[kraw], dma=True)
            p.sec("g", f_dbg2)

        def f_halo(eng):
            eng.do(lambda v: v.tensor_copy(xn.t[:, :, 0:1], xn.t[:, :, CH:CH + 1]),
                   rw=[xn])
        p.sec("v", f_halo)

        for pr in range(4):
            for which in range(2):
                srcb = qf if which == 0 else kraw
                dstb = qb if which == 0 else kf_c
                sa_, sb_ = (qsa, qsb) if which == 0 else (ksa, ksb)
                bias = LN8 if which == 0 else 0.0
                ct, st_ = (0, 1) if which == 0 else (2, 3)

                def f_sq2(eng, pr=pr, srcb=srcb):
                    eng.do(lambda s: s.activation(sqq.t[:], srcb.t[:, pr, :], AF.Square),
                           reads=[srcb], writes=[sqq])
                p.sec("s", f_sq2)

                def f_ls(eng):
                    eng.do(lambda t: t.matmul(ps_c.t[0:1, :], C["halfselb"].t[:, 0:1],
                                              sqq.t[:], start=True, stop=True),
                           reads=[sqq, C["halfselb"]], writes=[ps_c])
                    eng.do(lambda t: t.matmul(ps_d.t[0:1, :], C["halfselb"].t[:, 1:2],
                                              sqq.t[:], start=True, stop=True),
                           reads=[sqq, C["halfselb"]], writes=[ps_d])
                p.sec("t", f_ls)

                def f_rw(eng):
                    eng.do(lambda s: s.activation(rows.t[0:1, 0:CH], ps_c.t[0:1, :],
                                                  AF.Ln), reads=[ps_c], writes=[rows])
                    eng.do(lambda s: s.activation(rows.t[0:1, CH:2 * CH], ps_d.t[0:1, :],
                                                  AF.Ln), reads=[ps_d], rw=[rows])
                    eng.do(lambda s: s.activation(rows.t[0:1, :], rows.t[0:1, :],
                                                  AF.Exp, scale=-0.5), rw=[rows])
                p.sec("s", f_rw)

                def f_cap(eng, which=which):
                    if which == 0:
                        eng.do(lambda v: v.tensor_scalar(rows.t[0:1, :], rows.t[0:1, :],
                                                         8.0, 8e12, ALU.mult, ALU.min),
                               rw=[rows])
                    else:
                        eng.do(lambda v: v.tensor_scalar_min(rows.t[0:1, :],
                                                             rows.t[0:1, :], 1e12),
                               rw=[rows])
                p.sec("v", f_cap)

                def f_qm(eng, sa_=sa_, sb_=sb_):
                    eng.do(lambda t: t.matmul(ps_g.t[:], sa_.t[0:1, :],
                                              rows.t[0:1, 0:CH], start=True, stop=False),
                           reads=[rows, sa_], writes=[ps_g])
                    eng.do(lambda t: t.matmul(ps_g.t[:], sb_.t[0:1, :],
                                              rows.t[0:1, CH:2 * CH], start=False,
                                              stop=True),
                           reads=[rows, sb_], writes=[ps_g])
                p.sec("t", f_qm)

                def f_rot(eng, pr=pr, srcb=srcb, dstb=dstb, ct=ct, st_=st_, rtc=rtc):
                    eng.do(lambda v: v.tensor_mul(qn.t[:], srcb.t[:, pr, :], ps_g.t[:]),
                           reads=[srcb, ps_g], writes=[qn])
                    eng.do(lambda v: v.stream_shuffle(sh.t[:], qn.t[:], swapmask),
                           reads=[qn], writes=[sh])
                    eng.do(lambda v, ct=ct: v.tensor_mul(t1.t[:], qn.t[:],
                                                         rtc.t[:, ct, :]),
                           reads=[qn, rtc], writes=[t1])
                    eng.do(lambda v, st_=st_: v.tensor_mul(sh.t[:], sh.t[:],
                                                           rtc.t[:, st_, :]),
                           reads=[rtc], rw=[sh])
                    eng.do(lambda v, pr=pr: v.tensor_add(dstb.t[:, pr, :], t1.t[:],
                                                         sh.t[:]),
                           reads=[t1, sh], writes=[dstb])
                p.sec("v", f_rot)

        for tt in range(4):
            def f_tr(eng, tt=tt):
                for pr in range(4):
                    eng.do(lambda t, pr=pr, tt=tt: t.transpose(
                        ps_h.t[:, 128 * pr:128 * pr + 128],
                        vb.t[:, pr, 128 * tt:128 * tt + 128], C["ident"].t[:]),
                        reads=[vb, C["ident"]], writes=[ps_h])
            p.sec("t", f_tr)

            def f_tre(eng, tt=tt, vt_c=vt_c):
                eng.do(lambda v, tt=tt: v.tensor_copy(vt_c.t[:, tt, :, :],
                                                      ps_h.t[:, 0:512]),
                       reads=[ps_h], writes=[vt_c])
            p.sec("v", f_tre)

        for pr in range(4):
            for r in range(4):
                P = (4 - r) * 128 if c > 0 else 0
                W = (r + 1) * 128
                lo = 0 if P else 512
                hi = 512 + W
                for h in range(2):
                    hb = 64 * h
                    S = ps_A if h == 0 else ps_B
                    col = 8 * h

                    def f_s(eng, pr=pr, r=r, h=h, hb=hb, S=S, P=P, W=W,
                            kf_c=kf_c, kf_p=kf_p):
                        if P:
                            eng.do(lambda t: t.matmul(
                                S.t[:, 0:P], qb.t[hb:hb + 64, pr, 128 * r:128 * r + 128],
                                kf_p.t[hb:hb + 64, pr, 128 * r:CH],
                                start=True, stop=True, tile_position=(hb, 0)),
                                reads=[qb, kf_p], writes=[S])
                        eng.do(lambda t, W=W: t.matmul(
                            S.t[:, 512:512 + W],
                            qb.t[hb:hb + 64, pr, 128 * r:128 * r + 128],
                            kf_c.t[hb:hb + 64, pr, 0:W],
                            start=True, stop=True, tile_position=(hb, 0)),
                            reads=[qb, kf_c], writes=[S])
                    p.sec("t", f_s)

                    def f_m(eng, S=S, P=P, W=W, col=col, lo=lo, hi=hi):
                        if P and P < 512:
                            eng.do(lambda v: v.memset(S.t[:, P:512], NEG), rw=[S])
                        if P:
                            eng.do(lambda v: v.tensor_add(S.t[:, 0:128], S.t[:, 0:128],
                                                          C["mprev"].t[:]),
                                   reads=[C["mprev"]], rw=[S])
                        eng.do(lambda v, W=W: v.tensor_add(
                            S.t[:, 512 + W - 128:512 + W],
                            S.t[:, 512 + W - 128:512 + W], C["mdiag"].t[:]),
                            reads=[C["mdiag"]], rw=[S])
                        eng.do(lambda v: v.reduce_max(snm.t[:, col:col + 1],
                                                      S.t[:, lo:hi], axis=AX.X,
                                                      negate=True),
                               reads=[S], writes=[snm])
                    p.sec("v", f_m)

                    def f_e(eng, S=S, col=col, lo=lo, hi=hi):
                        eng.do(lambda s: s.activation(
                            Eb.t[:, lo:hi], S.t[:, lo:hi], AF.Exp,
                            bias=snm.t[:, col:col + 1],
                            accum_out=snm.t[:, col + 5:col + 6]),
                            reads=[S, snm], writes=[Eb], rw=[snm])
                    p.sec("s", f_e)

                    def f_n(eng, col=col, lo=lo, hi=hi):
                        eng.do(lambda v: v.reciprocal(snm.t[:, col + 6:col + 7],
                                                      snm.t[:, col + 5:col + 6]),
                               rw=[snm])
                        eng.do(lambda v: v.tensor_scalar_mul(
                            Eb.t[:, lo:hi], Eb.t[:, lo:hi],
                            snm.t[:, col + 6:col + 7]),
                            reads=[snm], rw=[Eb])
                    p.sec("v", f_n)

                    np_ = P // 128
                    nw = W // 128
                    ns = np_ + nw
                    esrc = [128 * s2 for s2 in range(np_)] +                            [512 + 128 * s2 for s2 in range(nw)]

                    def f_t2(eng, ns=ns, esrc=esrc):
                        for s2 in range(ns):
                            eng.do(lambda t, s2=s2: t.transpose(
                                ps_h.t[:, 128 * s2:128 * s2 + 128],
                                Eb.t[:, esrc[s2]:esrc[s2] + 128], C["ident"].t[:]),
                                reads=[Eb, C["ident"]], writes=[ps_h])
                    p.sec("t", f_t2)

                    def f_te(eng, ns=ns):
                        eng.do(lambda v, ns=ns: v.tensor_copy(Et.t[:, 0:128 * ns],
                                                              ps_h.t[:, 0:128 * ns]),
                               reads=[ps_h], writes=[Et])
                    p.sec("v", f_te)

                    def f_pv(eng, pr=pr, r=r, h=h, hb=hb, ns=ns, np_=np_,
                             vt_c=vt_c, vt_p=vt_p):
                        for s2 in range(ns):
                            if s2 < np_:
                                vsl = vt_p
                                tt = r + s2
                            else:
                                vsl = vt_c
                                tt = s2 - np_
                            oslc = (ps_g.t[0:64, 0:128] if h == 0
                                    else ps_g.t[64:128, 0:128])
                            eng.do(lambda t, s2=s2, vsl=vsl, tt=tt, oslc=oslc: t.matmul(
                                oslc, vsl.t[:, tt, pr, 64 * h:64 * h + 64],
                                Et.t[:, 128 * s2:128 * s2 + 128],
                                start=(s2 == 0), stop=(s2 == ns - 1),
                                tile_position=(0, 64 * h)),
                                reads=[vsl, Et], writes=[ps_g])
                    p.sec("t", f_pv)

                if c == _DBG_CHUNK and False and pr == 0 and r == 2:
                    def f_dba(eng):
                        dbg = p.dbg
                        eng.do(lambda g: g.dma_start(out=dbg[4, :, 0:640], in_=Eb.t[:]),
                               reads=[Eb], dma=True)
                        eng.do(lambda g: g.dma_start(out=dbg[5, :, 0:640], in_=Et.t[:]),
                               reads=[Et], dma=True)
                        eng.do(lambda g: g.dma_start(out=dbg[3, :, 520:536],
                                                     in_=snm.t[:]),
                               reads=[snm], dma=True)
                    p.sec("g", f_dba)

                def f_oe(eng, pr=pr, r=r):
                    eng.do(lambda v: v.tensor_copy(Ob.t[:, pr, 128 * r:128 * r + 128],
                                                   ps_g.t[:, 0:128]),
                           reads=[ps_g], writes=[Ob])
                p.sec("v", f_oe)

        if c == _DBG_CHUNK:
            def f_dbg(eng, c0=c0, ycc=ycc, kf_c=kf_c):
                dbg = p.dbg
                eng.do(lambda g: g.dma_start(out=dbg[0, :, 0:CH], in_=qb.t[:, 0, :]),
                       reads=[qb], dma=True)
                eng.do(lambda g: g.dma_start(out=dbg[1, :, 0:CH], in_=kf_c.t[:, 0, :]),
                       reads=[kf_c], dma=True)
                eng.do(lambda g: g.dma_start(out=dbg[2, :, 0:CH], in_=vb.t[:, 0, :]),
                       reads=[vb], dma=True)
                eng.do(lambda g: g.dma_start(out=dbg[3, :, 0:CH + 1],
                                             in_=xn.t[:, 0, :]),
                       reads=[xn], dma=True)
                eng.do(lambda g: g.dma_start(out=dbg[6, :, 0:CH], in_=Ob.t[:, 0, :]),
                       reads=[Ob], dma=True)
                eng.do(lambda g: g.dma_start(out=dbg[7, :, 0:CH],
                                             in_=(yn if first_ln else ycc).t[:, 0, :]),
                       reads=[(yn if first_ln else ycc)], dma=True)
                eng.do(lambda g: g.dma_start(out=dbg[4, :, 0:CH], in_=qf.t[:, 0, :]),
                       reads=[qf], dma=True)
                eng.do(lambda g: g.dma_start(out=dbg[5, :, 0:CH], in_=kraw.t[:, 0, :]),
                       reads=[kraw], dma=True)
            p.sec("g", f_dbg)

        for m in range(NFT):
            pp = ps_c if m % 2 == 0 else ps_d

            def f_wo(eng, m=m, pp=pp):
                for k in range(4):
                    eng.do(lambda t, k=k, m=m: t.matmul(
                        pp.t[:], wo.t[:, k, 128 * m:128 * m + 128], Ob.t[:, k, :],
                        start=(k == 0), stop=(k == 3)),
                        reads=[wo, Ob], writes=[pp])
            p.sec("t", f_wo)

            def f_res(eng, m=m, pp=pp, base=base):
                eng.do(lambda v: v.scalar_tensor_tensor(
                    yo.t[:, m, :], pp.t[:], 1.0, base.t[:, m, :], ALU.bypass, ALU.add),
                    reads=[pp, base], writes=[yo])
            p.sec("v", f_res)

        def f_st(eng, c0=c0):
            eng.do(lambda y: y.dma_start(out=dst3[:, :, c0:c0 + CH], in_=yo.t[:]),
                   reads=[yo], dma=True)
        p.sec("y", f_st)

    es.close()




def st_ffn_mlp(p, nc, C, l, src3, dst3, merge_src3, fwl):
    es = ExitStack()
    sb = lambda nm, shp, dt: p.buf(es.enter_context(nc.sbuf_tensor(nm, shp, dt)), nm)
    ps = lambda nm, shp: p.buf(es.enter_context(nc.psum_tensor(nm, shp, F32)), nm)
    sfx = f"_m{l}_{p.cnt['t']}"

    w1 = sb("m_w1" + sfx, [128, NFT, 2 * HIDP], BF16)
    w2 = sb("m_w2" + sfx, [128, NW2K, D], BF16)
    b1 = sb("m_b1" + sfx, [128, NW1T], F32)
    b2 = sb("m_b2" + sfx, [128, NFT], F32)
    lg = sb("m_lg" + sfx, [128, NFT], F32)
    lb = sb("m_lb" + sfx, [128, NFT], F32)
    load_w(p, w1, fwl["W1"].rearrange("(a p) m -> p a m", p=128))
    load_w(p, w2, fwl["W2"].rearrange("(a p) m -> p a m", p=128))
    load_w(p, b1, fwl["B1"]); load_w(p, b2, fwl["B2"])
    load_w(p, lg, fwl["LG"]); load_w(p, lb, fwl["LB"])

    hb = sb("m_hb" + sfx, [128, NFT, CH + 1], F32)
    sq = sb("m_sq" + sfx, [128, NFT, CH], BF16)
    hnb = sq
    yo = sb("m_yo" + sfx, [128, NFT, CH], F32)
    prod = sb("m_prod" + sfx, [128, NW2K, CH], BF16)
    gel = sb("m_gel" + sfx, [128, CH], BF16)
    rows = sb("m_rows" + sfx, [1, 2 * CH], F32)

    P = [ps(f"m_p{i}" + sfx, [128, CH]) for i in range(8)]
    statx, statq, mb, rb = P[0], P[1], P[2], P[3]
    if merge_src3 is not None:
        mrgt = [sb(f"m_mrg{i}" + sfx, [128, CH], F32) for i in range(2)]

    def f_z(eng):
        eng.do(lambda v: v.memset(hb.t[:, :, 0:1], 0.0), writes=[hb])
    p.sec("v", f_z)

    for c in range(NCH):
        c0 = c * CH

        def f_load(eng, c0=c0):
            eng.do(lambda y: y.dma_start(out=hb.t[:, :, 1:CH + 1],
                                         in_=src3[:, :, c0:c0 + CH]),
                   writes=[hb], dma=True)
        p.sec("y", f_load)

        def shap(i):
            return hb.t[:, i, 1:CH + 1] if i < 4 else hb.t[:, i, 0:CH]

        emit_ln(p, C, shap, [hb], sq, rows, statx, statq, mb, rb, CH, sq_bf16=True,
                outap=None)
        p.skip = _GLA_OP < 2

        def f_ap(eng):
            for i in range(NFT):
                eng.do(lambda v, i=i: v.tensor_sub(yo.t[:, i, :], shap(i), mb.t[:]),
                       reads=[hb, mb], writes=[yo])
            for i in range(NFT):
                eng.do(lambda v, i=i: v.tensor_mul(yo.t[:, i, :], yo.t[:, i, :],
                                                   rb.t[:]), reads=[rb], rw=[yo])
            for i in range(NFT):
                eng.do(lambda v, i=i: v.tensor_scalar(
                    hnb.t[:, i, :], yo.t[:, i, :], lg.t[:, i:i + 1], lb.t[:, i:i + 1],
                    ALU.mult, ALU.add), reads=[yo, lg, lb], writes=[hnb])
        p.sec("v", f_ap)

        def f_halo(eng):
            eng.do(lambda v: v.tensor_copy(hb.t[:, :, 0:1], hb.t[:, :, CH:CH + 1]),
                   rw=[hb])
        p.sec("v", f_halo)

        for j in range(NW2K):
            pa = P[4 + (2 * j) % 4]
            pg = P[4 + (2 * j + 1) % 4]

            def f_w1(eng, j=j, pa=pa, pg=pg):
                for k in range(NFT):
                    eng.do(lambda t, k=k, j=j: t.matmul(
                        pa.t[:], w1.t[:, k, 256 * j:256 * j + 128], hnb.t[:, k, :],
                        start=(k == 0), stop=(k == NFT - 1)),
                        reads=[w1, hnb], writes=[pa])
                for k in range(NFT):
                    eng.do(lambda t, k=k, j=j: t.matmul(
                        pg.t[:], w1.t[:, k, 256 * j + 128:256 * j + 256], hnb.t[:, k, :],
                        start=(k == 0), stop=(k == NFT - 1)),
                        reads=[w1, hnb], writes=[pg])
            p.sec("t", f_w1)

            def f_gel(eng, j=j, pg=pg):
                eng.do(lambda s, j=j: s.activation(gel.t[:], pg.t[:], AF.Gelu,
                                                   bias=b1.t[:, 2 * j + 1:2 * j + 2]),
                       reads=[pg, b1], writes=[gel])
            p.sec("s", f_gel)

            def f_prod(eng, j=j, pa=pa):
                eng.do(lambda v, j=j: v.scalar_tensor_tensor(
                    prod.t[:, j, :], pa.t[:], b1.t[:, 2 * j:2 * j + 1], gel.t[:],
                    ALU.add, ALU.mult), reads=[pa, gel, b1], writes=[prod])
            p.sec("v", f_prod)

        for m in range(NFT):
            pp = P[m % 2]

            def f_w2(eng, m=m, pp=pp):
                for k in range(NW2K):
                    eng.do(lambda t, k=k, m=m: t.matmul(
                        pp.t[:], w2.t[:, k, 128 * m:128 * m + 128], prod.t[:, k, :],
                        start=(k == 0), stop=(k == NW2K - 1)),
                        reads=[w2, prod], writes=[pp])
            p.sec("t", f_w2)

            if merge_src3 is not None:
                def f_ml(eng, m=m, c0=c0):
                    eng.do(lambda y, m=m: y.dma_start(
                        out=mrgt[m % 2].t[:], in_=merge_src3[:, m, c0:c0 + CH]),
                        writes=[mrgt[m % 2]], dma=True)
                p.sec("y", f_ml)

            def f_res(eng, m=m, pp=pp):
                eng.do(lambda v, m=m: v.scalar_tensor_tensor(
                    yo.t[:, m, :], pp.t[:], b2.t[:, m:m + 1], hb.t[:, m, 1:CH + 1],
                    ALU.add, ALU.add), reads=[pp, b2, hb], writes=[yo])
                if merge_src3 is not None:
                    eng.do(lambda v, m=m: v.tensor_add(yo.t[:, m, :], yo.t[:, m, :],
                                                       mrgt[m % 2].t[:]),
                           reads=[mrgt[m % 2]], rw=[yo])
            p.sec("v", f_res)

        def f_st(eng, c0=c0):
            eng.do(lambda y: y.dma_start(out=dst3[:, :, c0:c0 + CH], in_=yo.t[:]),
                   reads=[yo], dma=True)
        p.sec("y", f_st)

    es.close()


def st_gla(p, nc, C, src3, dst3, gw):
    es = ExitStack()
    sb = lambda nm, shp, dt: p.buf(es.enter_context(nc.sbuf_tensor(nm, shp, dt)), nm)

    upt = sb("g_upt", [128, NFT, GD], BF16)
    gatet = sb("g_gatet", [128, NFT, GD], BF16)
    downt = sb("g_downt", [128, NGT, D], BF16)
    qkt = sb("g_qkt", [128, NGT, 128], BF16)
    gtt = sb("g_gtt", [128, NGT, 128], BF16)
    ig1t = sb("g_ig1t", [128, NGT, 16], BF16)
    ig2t = sb("g_ig2t", [16, GD], BF16)
    convd = sb("g_convd", [128, 4, NGT, 128], BF16)
    convb = sb("g_convb", [128, NGT], F32)
    gbv = sb("g_gb", [64, 1], F32)
    taubv = sb("g_taub", [64, 1], F32)
    igb2 = sb("g_igb2", [128, NGT], F32)
    rwv = sb("g_rw", [128, NGT], F32)
    nwv = sb("g_nw", [128, NGT], F32)
    load_w(p, upt, gw["UPT"].rearrange("(a p) m -> p a m", p=128))
    load_w(p, gatet, gw["GATET"].rearrange("(a p) m -> p a m", p=128))
    load_w(p, downt, gw["DOWNT"].rearrange("(a p) m -> p a m", p=128))
    load_w(p, qkt, gw["QKT"].rearrange("(a p) m -> p a m", p=128))
    load_w(p, gtt, gw["GTT"].rearrange("(a p) m -> p a m", p=128))
    load_w(p, ig1t, gw["IG1T"].rearrange("(a p) m -> p a m", p=128))
    load_w(p, ig2t, gw["IG2T"])
    load_w(p, convd, gw["CONVD"])
    load_w(p, convb, gw["CONVB"])
    load_w(p, gbv, gw["GB"])
    load_w(p, taubv, gw["TAUB"])
    load_w(p, igb2, gw["IGB2"])
    load_w(p, rwv, gw["RW"])
    load_w(p, nwv, gw["NW"])

    if _GLA_NOUNION:
        x2c = sb("g_x2c", [128, 8, GCH], F32)
        xn = sb("g_xnr", [128, 8, GCH], BF16)
        xgn = sb("g_xgnr", [128, 16, GCH], BF16)
        xo = sb("g_xor", [128, 8, GCH], F32)
        sq2 = sb("g_sq2r", [128, 16, GIC], BF16)
        o2 = sb("g_o2r", [128, 16, GIC], F32)
        usb = sb("g_usbr", [64, GCH], F32)
        t2g = sb("g_t2gr", [64, GCH], F32)
        ggs = sb("g_ggsr", [64, GCH], F32)
        taus = sb("g_tausr", [64, GCH], F32)
        its = sb("g_itsr", [64, GCH], F32)
        kg = sb("g_kgr", [64, GCH], F32)
    else:
        UNA = sb("g_una", [128, 6144], BF16)
        x2c = Pr.alias(UNA, UNA.t[:][:, 0:4096].bitcast(F32)
                       .rearrange("p (a n) -> p a n", a=8), "x2c")
        xn = Pr.alias(UNA, UNA.t[:][:, 4096:6144]
                      .rearrange("p (a n) -> p a n", a=8), "xn")
        xgn = Pr.alias(UNA, UNA.t[:][:, 0:4096]
                       .rearrange("p (a n) -> p a n", a=16), "xgn")
        UNB = sb("g_unb", [128, 4096], BF16)
        xo = Pr.alias(UNB, UNB.t[:].bitcast(F32).rearrange("p (a n) -> p a n", a=8),
                      "xo")
        sq2 = Pr.alias(UNB, UNB.t[:][:, 0:2048].rearrange("p (a n) -> p a n", a=16),
                       "sq2")
        UNC = sb("g_unc", [128, 4096], BF16)
        o2 = Pr.alias(UNC, UNC.t[:].bitcast(F32).rearrange("p (a n) -> p a n", a=16),
                      "o2")
        _sm = UNC.t[:][:, 0:3072].bitcast(F32)
        usb = Pr.alias(UNC, _sm[0:64, 0:256], "usb")
        t2g = Pr.alias(UNC, _sm[0:64, 256:512], "t2g")
        ggs = Pr.alias(UNC, _sm[0:64, 512:768], "ggs")
        taus = Pr.alias(UNC, _sm[0:64, 768:1024], "taus")
        its = Pr.alias(UNC, _sm[0:64, 1024:1280], "its")
        kg = Pr.alias(UNC, _sm[0:64, 1280:1536], "kg")

    grow = sb("g_grow", [1, 2 * GCH], F32)
    xup = sb("g_xup", [128, NGT, GCH + 3], BF16)
    fgb = sb("g_fgb", [128, NGT, GCH], BF16)
    shiftb = sb("g_shiftb", [128, NGT, GCH], BF16)
    vbg = sb("g_vbg", [128, NGT, GCH], BF16)
    sgt = sb("g_sgt", [128, GCH], BF16)
    qf32 = sb("g_qf32", [64, GCH], F32)
    knf = sb("g_knf", [64, GCH], BF16)
    sqk = Pr.alias(knf, knf.t[:], "sqk")
    rts = sb("g_rts", [64, GCH], F32)
    ig1b = sb("g_ig1b", [16, GCH], BF16)
    S_sb = sb("g_S", [64, GD], F32)
    S_bf = sb("g_Sbf", [64, GD], BF16)
    if _GLA_NOUNION:
        vt_sb = sb("g_vtr", [128, GD], BF16)
    else:
        vt_sb = Pr.alias(UNB, UNB.t[:][:, 2048:4096], "vt")
    bsc = sb("g_bsc", [64, GIC], F32)
    tm1 = sb("g_tm1", [64, GIC], F32)
    ebx = sb("g_ebx", [64, GIC], F32)
    ebt = sb("g_ebt", [64, 1], F32)
    qd_b = sb("g_qd", [64, GIC], BF16)
    kd_b = sb("g_kd", [64, GIC], BF16)
    kdec_b = sb("g_kdec", [64, GIC], BF16)
    qdec_b = sb("g_qdec", [64, GIC], BF16)
    am_sb = sb("g_am", [GIC, GIC], BF16)
    kdt_sb = sb("g_kdt", [GIC, 64], BF16)
    gr2 = Pr.alias(grow, grow.t[:][0:1, GCH:2 * GCH], "gr2")

    def f_init(eng):
        eng.do(lambda v: v.memset(S_sb.t[:], 0.0), writes=[S_sb])
        eng.do(lambda v: v.memset(S_bf.t[:], 0.0), writes=[S_bf])
        eng.do(lambda v: v.memset(xup.t[:, :, 0:3], 0.0), writes=[xup])
    p.sec("v", f_init)

    if _GLA_NOUNION:
        resv = o2.t[:].rearrange("p a n -> p (a n)").bitcast(F32)
    else:
        resv = UNC.t[:].bitcast(F32)
    resv = resv.rearrange("p (a n) -> p a n", a=NFT)
    resb = Pr.alias(o2, resv, "resv")

    _n = NGCH if _GLA_LIMIT is None else _GLA_LIMIT
    for c in range(_n):
        c0 = c * GCH
        po = ExitStack()
        pso = lambda nm: p.buf(po.enter_context(
            nc.psum_tensor(nm + f"_{c}", [128, GCH], F32)), nm)
        statx, statq = pso("go_sx"), pso("go_sq")
        mbp, rbp = pso("go_mb"), pso("go_rb")
        pA, pB = pso("go_pA"), pso("go_pB")
        pQ, pG = pso("go_pQ"), pso("go_pG")

        def f_load(eng, c0=c0):
            eng.do(lambda y: y.dma_start(out=x2c.t[:], in_=src3[:, :, c0:c0 + GCH]),
                   writes=[x2c], dma=True)
        p.sec("y", f_load)

        p.skip = _GLA_OP < 1
        emit_ln(p, C, lambda i: x2c.t[:, i, :], [x2c], xn, grow, statx, statq,
                mbp, rbp, GCH, sq_bf16=True, outap=None, xap_whole=x2c.t[:])

        p.skip = _GLA_OP < 2

        def f_ap(eng):
            for i in range(NFT):
                eng.do(lambda v, i=i: v.tensor_sub(xo.t[:, i, :], x2c.t[:, i, :],
                                                   mbp.t[:]),
                       reads=[x2c, mbp], writes=[xo])
            for i in range(NFT):
                eng.do(lambda v, i=i: v.tensor_mul(xn.t[:, i, :], xo.t[:, i, :],
                                                   rbp.t[:]),
                       reads=[xo, rbp], writes=[xn])
        p.sec("v", f_ap)

        p.skip = _GLA_OP < 3
        for m in range(NGT):
            pp = pA if m % 2 == 0 else pB

            def f_up(eng, m=m, pp=pp):
                for k in range(NFT):
                    eng.do(lambda t, k=k, m=m: t.matmul(
                        pp.t[:], upt.t[:, k, 128 * m:128 * m + 128], xn.t[:, k, :],
                        start=(k == 0), stop=(k == NFT - 1)),
                        reads=[upt, xn], writes=[pp])
            p.sec("t", f_up)

            def f_upe(eng, m=m, pp=pp):
                eng.do(lambda v, m=m: v.tensor_copy(xup.t[:, m, 3:3 + GCH], pp.t[:]),
                       reads=[pp], writes=[xup])
            p.sec("v", f_upe)
        for m in range(NGT):
            pp = pA if m % 2 == 0 else pB

            def f_ga(eng, m=m, pp=pp):
                for k in range(NFT):
                    eng.do(lambda t, k=k, m=m: t.matmul(
                        pp.t[:], gatet.t[:, k, 128 * m:128 * m + 128], xn.t[:, k, :],
                        start=(k == 0), stop=(k == NFT - 1)),
                        reads=[gatet, xn], writes=[pp])
            p.sec("t", f_ga)

            def f_gae(eng, m=m, pp=pp):
                eng.do(lambda s, m=m: s.activation(fgb.t[:, m, :], pp.t[:], AF.Silu),
                       reads=[pp], writes=[fgb])
            p.sec("s", f_gae)

        p.skip = _GLA_OP < 4
        for m in range(NGT):
            pp = pQ if m % 2 == 0 else pG

            def f_cv(eng, m=m, pp=pp):
                for j in range(4):
                    eng.do(lambda t, j=j, m=m: t.matmul(
                        pp.t[:], convd.t[:, j, m, :], xup.t[:, m, j:j + GCH],
                        start=(j == 0), stop=(j == 3)),
                        reads=[convd, xup], writes=[pp])
            p.sec("t", f_cv)

            def f_cve(eng, m=m, pp=pp):
                eng.do(lambda s, m=m: s.activation(shiftb.t[:, m, :], pp.t[:],
                                                   AF.Silu, bias=convb.t[:, m:m + 1]),
                       reads=[pp, convb], writes=[shiftb])
            p.sec("s", f_cve)

        p.skip = _GLA_OP < 5
        def f_qk(eng):
            for k in range(NGT):
                eng.do(lambda t, k=k: t.matmul(pQ.t[:], qkt.t[:, k, :],
                                               shiftb.t[:, k, :],
                                               start=(k == 0), stop=(k == NGT - 1)),
                       reads=[qkt, shiftb], writes=[pQ])
            for k in range(NGT):
                eng.do(lambda t, k=k: t.matmul(pG.t[:], gtt.t[:, k, :],
                                               shiftb.t[:, k, :],
                                               start=(k == 0), stop=(k == NGT - 1)),
                       reads=[gtt, shiftb], writes=[pG])
        p.sec("t", f_qk)

        def f_qke(eng):
            eng.do(lambda s: s.activation(taus.t, pG.t[64:128, :], AF.Sigmoid,
                                          bias=taubv.t[:, 0:1]),
                   reads=[pG, taubv], writes=[taus])
        p.sec("s", f_qke)

        def f_qke2(eng):
            eng.do(lambda v: v.tensor_scalar_mul(qf32.t[:], pQ.t[0:64, :], 0.125),
                   reads=[pQ], writes=[qf32])
            eng.do(lambda v: v.tensor_copy(kg.t, pQ.t[64:128, :]),
                   reads=[pQ], writes=[kg])
            eng.do(lambda v: v.tensor_scalar_add(usb.t, pG.t[0:64, :],
                                                 gbv.t[:, 0:1]),
                   reads=[pG, gbv], writes=[usb])
        p.sec("v", f_qke2)

        p.skip = _GLA_OP < 6
        def f_ig1(eng):
            for k in range(NGT):
                eng.do(lambda t, k=k: t.matmul(statq.t[0:16, :], ig1t.t[:, k, :],
                                               xup.t[:, k, 3:3 + GCH],
                                               start=(k == 0), stop=(k == NGT - 1)),
                       reads=[ig1t, xup], writes=[statq])
        p.sec("t", f_ig1)

        def f_ig1e(eng):
            eng.do(lambda v: v.tensor_copy(ig1b.t[:], statq.t[0:16, :]),
                   reads=[statq], writes=[ig1b])
        p.sec("v", f_ig1e)

        for m in range(NGT):
            pp = pA if m % 2 == 0 else pB

            def f_ig2(eng, m=m, pp=pp):
                eng.do(lambda t, m=m: t.matmul(pp.t[:],
                                               ig2t.t[:, 128 * m:128 * m + 128],
                                               ig1b.t[:], start=True, stop=True),
                       reads=[ig2t, ig1b], writes=[pp])
            p.sec("t", f_ig2)

            def f_ig2e(eng, m=m, pp=pp):
                eng.do(lambda s, m=m: s.activation(sgt.t[:], pp.t[:], AF.Sigmoid,
                                                   bias=igb2.t[:, m:m + 1]),
                       reads=[pp, igb2], writes=[sgt])
            p.sec("s", f_ig2e)

            def f_vm(eng, m=m):
                eng.do(lambda v, m=m: v.tensor_mul(vbg.t[:, m, :], sgt.t[:],
                                                   xup.t[:, m, 3:3 + GCH]),
                       reads=[sgt, xup], writes=[vbg])
            p.sec("v", f_vm)

        p.skip = _GLA_OP < 7
        def f_sp2(eng):
            eng.do(lambda s: s.activation(t2g.t, usb.t, AF.Abs),
                   reads=[usb], writes=[t2g])
            eng.do(lambda s: s.activation(t2g.t, t2g.t, AF.Exp, scale=-1.0),
                   rw=[t2g])
            eng.do(lambda s: s.activation(t2g.t, t2g.t, AF.Ln, bias=1.0),
                   rw=[t2g])
        p.sec("s", f_sp2)

        def f_sp3(eng):
            eng.do(lambda v: v.tensor_scalar_max(ggs.t, usb.t, 0.0),
                   reads=[usb], writes=[ggs])
            eng.do(lambda v: v.tensor_add(ggs.t, ggs.t, t2g.t),
                   reads=[t2g], rw=[ggs])
            eng.do(lambda v: v.scalar_tensor_tensor(rts.t[:], ggs.t, -1.0,
                                                    taus.t, ALU.mult, ALU.mult),
                   reads=[ggs, taus], writes=[rts])
        p.sec("v", f_sp3)

        def f_it1(eng):
            eng.do(lambda s: s.activation(t2g.t, ggs.t, AF.Ln),
                   reads=[ggs], writes=[t2g])
        p.sec("s", f_it1)

        def f_it2(eng):
            eng.do(lambda v: v.tensor_mul(t2g.t, t2g.t, taus.t),
                   reads=[taus], rw=[t2g])
        p.sec("v", f_it2)

        def f_it3(eng):
            eng.do(lambda s: s.activation(its.t, t2g.t, AF.Exp),
                   reads=[t2g], writes=[its])
        p.sec("s", f_it3)

        def f_kn1(eng):
            eng.do(lambda s: s.activation(sqk.t[:], kg.t, AF.Square),
                   reads=[kg], writes=[sqk])
        p.sec("s", f_kn1)

        def f_kn2(eng):
            eng.do(lambda t: t.matmul(statx.t[0:1, :], C["onescolb"].t[0:64, 0:1],
                                      sqk.t[:], start=True, stop=True),
                   reads=[sqk, C["onescolb"]], writes=[statx])
        p.sec("t", f_kn2)

        def f_kn3(eng):
            eng.do(lambda s: s.activation(grow.t[0:1, 0:GCH], statx.t[0:1, :], AF.Ln),
                   reads=[statx], writes=[grow])
            eng.do(lambda s: s.activation(grow.t[0:1, 0:GCH], grow.t[0:1, 0:GCH],
                                          AF.Exp, scale=-0.5), rw=[grow])
        p.sec("s", f_kn3)

        def f_kn4(eng):
            eng.do(lambda v: v.tensor_scalar_min(grow.t[0:1, 0:GCH],
                                                 grow.t[0:1, 0:GCH], 1e12), rw=[grow])
        p.sec("v", f_kn4)

        def f_kn5(eng):
            eng.do(lambda t: t.matmul(mbp.t[0:64, :], C["onesrow"].t[0:1, 0:64],
                                      grow.t[0:1, 0:GCH], start=True, stop=True),
                   reads=[grow, C["onesrow"]], writes=[mbp])
        p.sec("t", f_kn5)

        def f_kn6(eng):
            eng.do(lambda v: v.tensor_mul(t2g.t, kg.t, mbp.t[0:64, :]),
                   reads=[kg, mbp], writes=[t2g])
            eng.do(lambda v: v.tensor_mul(knf.t[:], t2g.t, its.t),
                   reads=[t2g, its], writes=[knf])
        p.sec("v", f_kn6)

        p.skip = _GLA_OP < 8
        def f_shrw(eng):
            for m in range(NGT):
                eng.do(lambda v, m=m: v.tensor_scalar_mul(
                    shiftb.t[:, m, :], shiftb.t[:, m, :], rwv.t[:, m:m + 1]),
                    reads=[rwv], rw=[shiftb])
        p.sec("v", f_shrw)

        def f_halo(eng):
            eng.do(lambda v: v.tensor_copy(xup.t[:, :, 0:3],
                                           xup.t[:, :, GCH:GCH + 3]), rw=[xup])
        p.sec("v", f_halo)

        p.skip = False
        po.close()
        p.barrier()

        pss = ExitStack()
        _scan_on = _GLA_PH >= 2
        of_ps = p.buf(pss.enter_context(
            nc.psum_tensor(f"gs_of_{c}", [128, 8, GIC], F32)), "of")
        vt_ps = p.buf(pss.enter_context(
            nc.psum_tensor(f"gs_vt_{c}", [128, GD], BF16)), "vtp")
        at_ps = p.buf(pss.enter_context(
            nc.psum_tensor(f"gs_at_{c}", [128, GIC], F32)), "atp")
        kdt_ps = p.buf(pss.enter_context(
            nc.psum_tensor(f"gs_kdt_{c}", [GIC, 64], BF16)), "kdtp")
        U_ps = p.buf(pss.enter_context(
            nc.psum_tensor(f"gs_u_{c}", [64, GD // 2], F32)), "ups")

        for ic in range(GCH // GIC if _scan_on else 0):
            i0 = ic * GIC

            def f_b(eng, i0=i0):
                eng.do(lambda v: v.tensor_tensor_scan(
                    bsc.t[:], rts.t[:, i0:i0 + GIC], rts.t[:, i0:i0 + GIC], 0.0,
                    ALU.add, ALU.bypass), reads=[rts], writes=[bsc])
                eng.do(lambda v: v.tensor_scalar_sub(tm1.t[:], bsc.t[:],
                                                     bsc.t[:, GIC // 2:GIC // 2 + 1]),
                       reads=[bsc], writes=[tm1])
            p.sec("v", f_b)

            def f_eb1(eng):
                eng.do(lambda s: s.activation(ebx.t[:], tm1.t[:], AF.Exp),
                       reads=[tm1], writes=[ebx])
            p.sec("s", f_eb1)

            def f_qd(eng, i0=i0):
                eng.do(lambda v: v.tensor_mul(qd_b.t[:], qf32.t[:, i0:i0 + GIC],
                                              ebx.t[:]), reads=[qf32, ebx],
                       writes=[qd_b])
                eng.do(lambda v: v.tensor_scalar(tm1.t[:], bsc.t[:], -1.0,
                                                 bsc.t[:, GIC // 2:GIC // 2 + 1],
                                                 ALU.mult, ALU.add),
                       reads=[bsc], writes=[tm1])
            p.sec("v", f_qd)

            def f_eb2(eng):
                eng.do(lambda s: s.activation(ebx.t[:], tm1.t[:], AF.Exp),
                       reads=[tm1], writes=[ebx])
            p.sec("s", f_eb2)

            def f_kd(eng, i0=i0):
                eng.do(lambda v: v.tensor_mul(kd_b.t[:], knf.t[:, i0:i0 + GIC],
                                              ebx.t[:]), reads=[knf, ebx],
                       writes=[kd_b])
                eng.do(lambda v: v.tensor_scalar(tm1.t[:], bsc.t[:], -1.0,
                                                 bsc.t[:, GIC - 1:GIC], ALU.mult,
                                                 ALU.add),
                       reads=[bsc], writes=[tm1])
            p.sec("v", f_kd)

            def f_eb3(eng):
                eng.do(lambda s: s.activation(ebx.t[:], tm1.t[:], AF.Exp),
                       reads=[tm1], writes=[ebx])
                eng.do(lambda s: s.activation(ebt.t[:], bsc.t[0:64, GIC - 1:GIC],
                                              AF.Exp), reads=[bsc], writes=[ebt])
            p.sec("s", f_eb3)

            def f_kdec(eng, i0=i0):
                eng.do(lambda v: v.tensor_mul(kdec_b.t[:], knf.t[:, i0:i0 + GIC],
                                              ebx.t[:]), reads=[knf, ebx],
                       writes=[kdec_b])
            p.sec("v", f_kdec)

            def f_eb4(eng):
                eng.do(lambda s: s.activation(ebx.t[:], bsc.t[:], AF.Exp),
                       reads=[bsc], writes=[ebx])
            p.sec("s", f_eb4)

            def f_qdec(eng, i0=i0):
                eng.do(lambda v: v.tensor_mul(qdec_b.t[:], qf32.t[:, i0:i0 + GIC],
                                              ebx.t[:]), reads=[qf32, ebx],
                       writes=[qdec_b])
            p.sec("v", f_qdec)

            def f_at(eng):
                eng.do(lambda t: t.matmul(at_ps.t[:], kd_b.t[:], qd_b.t[:],
                                          start=True, stop=True),
                       reads=[kd_b, qd_b], writes=[at_ps])
                eng.do(lambda t: t.transpose(kdt_ps.t[:], kdec_b.t[:],
                                             C["ident"].t[0:64, 0:64]),
                       reads=[kdec_b, C["ident"]], writes=[kdt_ps])
            p.sec("t", f_at)

            def f_ate(eng):
                eng.do(lambda v: v.tensor_mul(am_sb.t[:], at_ps.t[:],
                                              C["gtri"].t[:]),
                       reads=[at_ps, C["gtri"]], writes=[am_sb])
                eng.do(lambda v: v.tensor_copy(kdt_sb.t[:], kdt_ps.t[:]),
                       reads=[kdt_ps], writes=[kdt_sb])
            p.sec("v", f_ate)

            def f_vt(eng, i0=i0):
                for m in range(NGT):
                    eng.do(lambda t, m=m: t.transpose(
                        vt_ps.t[:, 128 * m:128 * m + 128],
                        vbg.t[:, m, i0:i0 + GIC], C["ident"].t[:]),
                        reads=[vbg, C["ident"]], writes=[vt_ps])
            p.sec("t", f_vt)

            def f_vte(eng):
                eng.do(lambda v: v.tensor_copy(vt_sb.t[:], vt_ps.t[:]),
                       reads=[vt_ps], writes=[vt_sb])
            p.sec("v", f_vte)

            for hf in range(2):
                def f_o(eng, hf=hf):
                    for m2 in range(8):
                        m = 8 * hf + m2
                        eng.do(lambda t, m=m, m2=m2: t.matmul(
                            of_ps.t[:, m2, :],
                            S_bf.t[:, 128 * m:128 * m + 128], qdec_b.t[:],
                            start=True, stop=False),
                            reads=[S_bf, qdec_b], writes=[of_ps])
                        eng.do(lambda t, m=m, m2=m2: t.matmul(
                            of_ps.t[:, m2, :],
                            vt_sb.t[:, 128 * m:128 * m + 128], am_sb.t[:],
                            start=False, stop=True),
                            reads=[vt_sb, am_sb], writes=[of_ps])
                p.sec("t", f_o)

                def f_g1(eng, hf=hf, i0=i0):
                    eng.do(lambda v: v.tensor_add(
                        o2.t[:, 8 * hf:8 * hf + 8, :],
                        shiftb.t[:, 8 * hf:8 * hf + 8, i0:i0 + GIC], of_ps.t[:]),
                        reads=[shiftb, of_ps], writes=[o2])
                p.sec("v", f_g1)

            for hf in range(2):
                def f_u(eng, hf=hf):
                    for q2 in range(2):
                        eng.do(lambda t, hf=hf, q2=q2: t.matmul(
                            U_ps.t[:, 512 * q2:512 * q2 + 512], kdt_sb.t[:],
                            vt_sb.t[:, 1024 * hf + 512 * q2:
                                    1024 * hf + 512 * q2 + 512],
                            start=True, stop=True),
                            reads=[kdt_sb, vt_sb], writes=[U_ps])
                p.sec("t", f_u)

                def f_su(eng, hf=hf):
                    eng.do(lambda v, hf=hf: v.tensor_scalar_mul(
                        S_sb.t[:, 1024 * hf:1024 * hf + 1024],
                        S_sb.t[:, 1024 * hf:1024 * hf + 1024], ebt.t[:, 0:1]),
                        reads=[ebt], rw=[S_sb])
                    eng.do(lambda v, hf=hf: v.tensor_add(
                        S_sb.t[:, 1024 * hf:1024 * hf + 1024],
                        S_sb.t[:, 1024 * hf:1024 * hf + 1024], U_ps.t[:]),
                        reads=[U_ps], rw=[S_sb])
                p.sec("v", f_su)

            def f_sbf(eng):
                eng.do(lambda v: v.tensor_copy(S_bf.t[:], S_sb.t[:]),
                       reads=[S_sb], writes=[S_bf])
            p.sec("v", f_sbf)

            def f_g2(eng, i0=i0):
                eng.do(lambda v, i0=i0: v.tensor_mul(
                    o2.t[:], o2.t[:], fgb.t[:, :, i0:i0 + GIC]),
                    reads=[fgb], rw=[o2])
            p.sec("v", f_g2)

            def f_g3(eng):
                eng.do(lambda s: s.activation(sq2.t[:], o2.t[:], AF.Square),
                       reads=[o2], writes=[sq2])
            p.sec("s", f_g3)

            def f_g4(eng):
                for m in range(NGT):
                    eng.do(lambda t, m=m: t.matmul(
                        at_ps.t[0:1, :], C["onescolb"].t[:, 0:1],
                        sq2.t[:, m, :],
                        start=(m == 0), stop=(m == NGT - 1)),
                        reads=[sq2, C["onescolb"]], writes=[at_ps])
            p.sec("t", f_g4)

            def f_g5(eng):
                eng.do(lambda v: v.tensor_scalar(gr2.t[0:1, 0:GIC],
                                                 at_ps.t[0:1, :], 1.0 / GD, 1e-5,
                                                 ALU.mult, ALU.add),
                       reads=[at_ps], writes=[gr2])
            p.sec("v", f_g5)

            def f_g6(eng):
                eng.do(lambda s: s.activation(gr2.t[0:1, GIC:2 * GIC],
                                              gr2.t[0:1, 0:GIC], AF.Ln), rw=[gr2])
                eng.do(lambda s: s.activation(gr2.t[0:1, 0:GIC],
                                              gr2.t[0:1, GIC:2 * GIC], AF.Exp,
                                              scale=-0.5), rw=[gr2])
            p.sec("s", f_g6)

            def f_g7(eng):
                eng.do(lambda t: t.matmul(at_ps.t[:], C["onesrow"].t[0:1, :],
                                          gr2.t[0:1, 0:GIC], start=True, stop=True),
                       reads=[gr2, C["onesrow"]], writes=[at_ps])
            p.sec("t", f_g7)

            def f_g8(eng, i0=i0):
                for m in range(NGT):
                    eng.do(lambda v, m=m: v.scalar_tensor_tensor(
                        xgn.t[:, m, i0:i0 + GIC], o2.t[:, m, :],
                        nwv.t[:, m:m + 1], at_ps.t[:], ALU.mult, ALU.mult),
                        reads=[o2, nwv, at_ps], writes=[xgn])
            p.sec("v", f_g8)

        pss.close()
        p.barrier()

        pd = ExitStack()
        if _GLA_PH < 3:
            def f_st0(eng, c0=c0):
                eng.do(lambda y: y.dma_start(out=dst3[:, :, c0:c0 + GCH],
                                             in_=xo.t[:]),
                       reads=[xo], dma=True)
            p.sec("y", f_st0)
            pd.close()
            p.barrier()
            continue

        def f_rld(eng, c0=c0):
            eng.do(lambda y: y.dma_start(out=resv, in_=src3[:, :, c0:c0 + GCH]),
                   writes=[resb], dma=True)
        p.sec("y", f_rld)
        pD = [p.buf(pd.enter_context(
            nc.psum_tensor(f"gd_p{i}_{c}", [128, GCH], F32)), f"pD{i}")
            for i in range(2)]
        for m in range(NFT):
            pp = pD[m % 2]

            def f_dn(eng, m=m, pp=pp):
                for k in range(NGT):
                    eng.do(lambda t, k=k, m=m: t.matmul(
                        pp.t[:], downt.t[:, k, 128 * m:128 * m + 128],
                        xgn.t[:, k, :],
                        start=(k == 0), stop=(k == NGT - 1)),
                        reads=[downt, xgn], writes=[pp])
            p.sec("t", f_dn)

            def f_dne(eng, m=m, pp=pp):
                eng.do(lambda v, m=m: v.scalar_tensor_tensor(
                    xo.t[:, m, :], pp.t[:], 1.0, resv[:, m, :], ALU.bypass, ALU.add),
                    reads=[pp, resb], writes=[xo])
            p.sec("v", f_dne)

        def f_st(eng, c0=c0):
            eng.do(lambda y: y.dma_start(out=dst3[:, :, c0:c0 + GCH], in_=xo.t[:]),
                   reads=[xo], dma=True)
        p.sec("y", f_st)
        pd.close()
        p.barrier()

    es.close()



def build_program(upto):
    nc = bass.Bass("TRN2", target_bir_lowering=False,
                   dynamic_dma_scratch_size=16384)
    stk = ExitStack()

    def din(name, shape, dt=F32):
        return nc.dram_tensor(name, list(shape), dt, kind="ExternalInput").ap()

    xt = din("xt", [D, T_FULL])
    aWKT = din("aWKT", [D, D]); aW1T = din("aW1T", [D, 64]); aW2T = din("aW2T", [64, D])
    rtab = din("rtab", [4, 128, T_FULL])
    mprev = din("mprev", [128, 128]); mdiag = din("mdiag", [128, 128])
    gtri = din("gtri", [GIC, GIC]); ident = din("ident", [128, 128])
    halfsel = din("halfsel", [128, 2])
    fw = {}
    for l in (0, 1):
        fw[l] = dict(
            WQKV=din(f"fWQKV{l}", [D, 1536]),
            QSa=din(f"fQSa{l}", [1, 128]), QSb=din(f"fQSb{l}", [1, 128]),
            KSa=din(f"fKSa{l}", [1, 128]), KSb=din(f"fKSb{l}", [1, 128]),
            WO=din(f"fWO{l}", [512, D]),
            LG=din(f"fLG{l}", [128, NFT]), LB=din(f"fLB{l}", [128, NFT]),
            W1=din(f"fW1{l}", [D, 2 * HIDP]), B1=din(f"fB1{l}", [128, NW1T]),
            W2=din(f"fW2{l}", [HIDP, D]), B2=din(f"fB2{l}", [128, NFT]),
        )
    gw = dict(
        UPT=din("gUPT", [D, GD]), GATET=din("gGATET", [D, GD]),
        DOWNT=din("gDOWNT", [GD, D]), QKT=din("gQKT", [GD, 128]),
        GTT=din("gGTT", [GD, 128]), IG1T=din("gIG1T", [GD, 16]),
        IG2T=din("gIG2T", [16, GD]), CONVD=din("gCONVD", [128, 4, NGT, 128]),
        CONVB=din("gCONVB", [128, NGT]), GB=din("gGB", [64, 1]),
        TAUB=din("gTAUB", [64, 1]), IGB2=din("gIGB2", [128, NGT]),
        RW=din("gRW", [128, NGT]), NW=din("gNW", [128, NGT]),
    )
    res = nc.dram_tensor("res", [D, T_FULL], F32, kind="Internal").ap()
    ffs = nc.dram_tensor("ffs", [D, T_FULL], F32, kind="Internal").ap()
    out = nc.dram_tensor("out", [D, T_FULL], F32, kind="ExternalOutput").ap()
    dbg = (nc.dram_tensor("dbg", [8, 128, 656], F32, kind="ExternalOutput").ap()
           if _DBG_CHUNK >= 0 else None)
    r3 = lambda ap: ap.rearrange("(a p) n -> p a n", p=128)
    xt3, res3, ffs3, out3 = r3(xt), r3(res), r3(ffs), r3(out)

    with stk:
        block = stk.enter_context(nc.Block())
        p = Pr(nc, block, stk)
        p.dbg = dbg

        cst = ExitStack()
        sb = lambda nm, shp, dt: p.buf(cst.enter_context(nc.sbuf_tensor(nm, shp, dt)), nm)
        C = dict(
            ident=sb("c_ident", [128, 128], BF16),
            mprev=sb("c_mprev", [128, 128], F32),
            mdiag=sb("c_mdiag", [128, 128], F32),
            gtri=sb("c_gtri", [GIC, GIC], BF16),
            onesrow=sb("c_onesrow", [1, 128], F32),
            onescol=sb("c_onescol", [128, 2], F32),
            onescolb=sb("c_onescolb", [128, 2], BF16),
            halfselb=sb("c_halfselb", [128, 2], BF16),
        )
        load_w(p, C["ident"], ident)
        load_w(p, C["mprev"], mprev)
        load_w(p, C["mdiag"], mdiag)
        load_w(p, C["gtri"], gtri)
        load_w(p, C["halfselb"], halfsel)

        def f_c(eng):
            eng.do(lambda g: g.memset(C["onesrow"].t[:], 1.0), writes=[C["onesrow"]])
            eng.do(lambda g: g.memset(C["onescol"].t[:], 1.0), writes=[C["onescol"]])
            eng.do(lambda g: g.memset(C["onescolb"].t[:], 1.0), writes=[C["onescolb"]])
        p.sec("g", f_c)
        p.barrier()

        st_attn_a(p, nc, C, xt3, res3 if upto > 1 else res3, aWKT, aW1T, aW2T)
        p.barrier()
        if upto >= 2:
            st_ffn_attn(p, nc, C, 0, True, res3, ffs3, fw[0], rtab)
            p.barrier()
        if upto >= 3:
            st_ffn_mlp(p, nc, C, 0, ffs3, ffs3, None, fw[0])
            p.barrier()
        if upto >= 4:
            st_ffn_attn(p, nc, C, 1, False, ffs3,
                        (res3 if _F1A_OOP and upto == 4 else ffs3), fw[1], rtab)
            p.barrier()
        if upto >= 5:
            st_ffn_mlp(p, nc, C, 1, ffs3, (res3 if upto > 5 else out3), res3, fw[1])
            p.barrier()
        if upto >= 6:
            st_gla(p, nc, C, res3, (res3 if upto > 6 else out3), gw)
            p.barrier()
        if upto >= 7:
            st_ffn_attn(p, nc, C, 0, True, res3, ffs3, fw[0], rtab)
            p.barrier()
        if upto >= 8:
            st_ffn_mlp(p, nc, C, 0, ffs3, ffs3, None, fw[0])
            p.barrier()
        if upto >= 9:
            st_ffn_attn(p, nc, C, 1, False, ffs3, ffs3, fw[1], rtab)
            p.barrier()
        if upto >= 10:
            st_ffn_mlp(p, nc, C, 1, ffs3, out3, res3, fw[1])
            p.barrier()

        if upto in (1, 2, 3, 4, 7, 8, 9):
            src3 = res3 if (upto == 1 or (_F1A_OOP and upto == 4)) else ffs3
            cp = ExitStack()
            tb = p.buf(cp.enter_context(nc.sbuf_tensor("cpb", [128, NFT, CH], F32)))
            for c in range(NCH):
                c0 = c * CH

                def fl(eng, c0=c0):
                    eng.do(lambda y: y.dma_start(out=tb.t[:], in_=src3[:, :, c0:c0 + CH]),
                           writes=[tb], dma=True)
                    eng.do(lambda y: y.dma_start(out=out3[:, :, c0:c0 + CH], in_=tb.t[:]),
                           reads=[tb], dma=True)
                p.sec("y", fl)
            cp.close()

        snap = dict(p.cnt)

        def fin(eng):
            for sn, c in snap.items():
                if c and p.seen["y"][sn] < c:
                    eng.raw.wait_ge(p.sem[sn], c)
                    p.seen["y"][sn] = c
        p.sec("y", fin)
        cst.close()
    return nc


def _kernel_impl(inputs, upto=_UPTO, cores=None):
    w = _prep(inputs)
    xin = np.asarray(inputs["L0_attn_a_x"])
    B = xin.shape[0]
    if cores is None:
        cores = list(range(B))
    nc = build_program(upto)
    in_maps = []
    for b in cores:
        m = {"xt": np.ascontiguousarray(xin[b].T)}
        m.update(w)
        in_maps.append(m)
    resl = run_bass_kernel_spmd(nc, in_maps, core_ids=list(range(len(cores))))
    global _LAST
    _LAST = resl.results
    outs = [np.ascontiguousarray(r["out"].T) for r in resl.results]
    return np.stack(outs, 0)


_LAST = None


def kernel(**inputs):
    return _kernel_impl(inputs)
